# version 01
# baseline (speedup 1.0000x reference)
import numpy as np
import ml_dtypes
from contextlib import ExitStack

import concourse.bass as bass
import concourse.mybir as mybir
from concourse.bass_utils import run_bass_kernel_spmd

F32 = mybir.dt.float32
BF16 = mybir.dt.bfloat16
AF = mybir.ActivationFunctionType
ALU = mybir.AluOpType

E = 1024
NCH = 8
T = 2048
S = 8192
NPAN = 4
GW = 516
EXP_BIAS = -20.0
EPS = 1e-5

ENGS = ("sync", "gpsimd", "vector", "scalar", "tensor")


class Ctr:
    def __init__(self):
        self.h = None
        self.n = 0

    def bump(self, k=1):
        self.n += k
        return self.n


class Builder:
    def __init__(self):
        self.q = {e: [] for e in ENGS}
        self.waited = {e: {} for e in ENGS}

    def emit(self, eng, fn):
        self.q[eng].append(fn)

    def wait(self, eng, ctr, val):
        if not val or val <= 0:
            return
        if self.waited[eng].get(id(ctr), 0) >= val:
            return
        self.waited[eng][id(ctr)] = val
        self.emit(eng, lambda e, c=ctr, v=val: e.wait_ge(c.h, v))

    def waits(self, eng, deps):
        for (c, v) in deps:
            self.wait(eng, c, v)


def col_blocks(width, maxn=512):
    out, c = [], 0
    while c < width:
        w = min(maxn, width - c)
        out.append((c, w))
        c += w
    return out


def build_and_run(debug_stage, in_maps):
    nc = bass.Bass("TRN2", target_bir_lowering=False)

    xT = nc.dram_tensor("xT", [E, T + 2], F32, kind="ExternalInput").ap()
    wrT = nc.dram_tensor("wrT", [E, E], F32, kind="ExternalInput").ap()
    wkT = nc.dram_tensor("wkT", [E, E], F32, kind="ExternalInput").ap()
    wvT = nc.dram_tensor("wvT", [E, E], F32, kind="ExternalInput").ap()
    wqaT = nc.dram_tensor("wqaT", [E, 256], F32, kind="ExternalInput").ap()
    wkaT = nc.dram_tensor("wkaT", [E, 256], F32, kind="ExternalInput").ap()
    wvaT = nc.dram_tensor("wvaT", [E, 256], F32, kind="ExternalInput").ap()
    woT = nc.dram_tensor("woT", [E, E], F32, kind="ExternalInput").ap()
    bo_l = nc.dram_tensor("bo_l", [128, NCH], F32, kind="ExternalInput").ap()
    mur_l = nc.dram_tensor("mur_l", [128, NCH], F32, kind="ExternalInput").ap()
    muk_l = nc.dram_tensor("muk_l", [128, NCH], F32, kind="ExternalInput").ap()
    hm_in = nc.dram_tensor("hm_in", [128, 1], F32, kind="ExternalInput").ap()
    y = nc.dram_tensor("y", [E, T], F32, kind="ExternalOutput").ap()

    x1sp = nc.dram_tensor("x1sp", [E, T + 1], F32, kind="Internal").ap()
    cc1_in = nc.dram_tensor("cc1_in", [E, T], BF16, kind="Internal")
    cc1_out = nc.dram_tensor("cc1_out", [4 * E, T], BF16, kind="Internal")
    cc2_in = nc.dram_tensor("cc2_in", [256, S], BF16, kind="Internal")
    cc2_out = nc.dram_tensor("cc2_out", [E, S], BF16, kind="Internal")
    groups = [[0, 1, 2, 3], [4, 5, 6, 7]]


    ki = np.arange(128)[:, None]
    qi = np.arange(512)[None, :]
    masks_np = [((ki + 128 * d) <= qi) for d in range(4)]
    for d in range(-2, 4):
        kk = ki + 128 * d
        masks_np.append((kk <= qi) & (kk >= qi - 256))
    mask_t = [nc.inline_tensor(np.ascontiguousarray(m).astype(ml_dtypes.bfloat16),
                               name=f"mask{i}") for i, m in enumerate(masks_np)]

    masks_s = nc.alloc_sbuf_tensor("masks_s", [128, 10 * 512], BF16).ap()
    e_t = [nc.alloc_sbuf_tensor(f"e{i}", [128, 1024], BF16).ap() for i in range(2)]
    wqa_s = nc.alloc_sbuf_tensor("wqa_s", [128, NCH * 256], BF16).ap()
    wka_s = nc.alloc_sbuf_tensor("wka_s", [128, NCH * 256], BF16).ap()
    wva_s = nc.alloc_sbuf_tensor("wva_s", [128, NCH * 256], BF16).ap()
    mur_s = nc.alloc_sbuf_tensor("mur_s", [128, NCH], F32).ap()
    muk_s = nc.alloc_sbuf_tensor("muk_s", [128, NCH], F32).ap()
    bo_s = nc.alloc_sbuf_tensor("bo_s", [128, NCH], F32).ap()
    hm_s = nc.alloc_sbuf_tensor("hm_s", [128, 1], F32).ap()
    ones_st = nc.alloc_sbuf_tensor("ones_st", [128, 1], BF16).ap()
    ones_c = nc.alloc_sbuf_tensor("ones_c", [128, 1], BF16).ap()
    ones_bf = nc.alloc_sbuf_tensor("ones_bf", [1, 128], F32).ap()
    ebias_s = nc.alloc_sbuf_tensor("ebias_s", [128, 1], F32).ap()
    mean_r = nc.alloc_sbuf_tensor("mean_r", [1, GW], F32).ap()
    msq_r = nc.alloc_sbuf_tensor("msq_r", [1, GW], F32).ap()
    rstd_r = nc.alloc_sbuf_tensor("rstd_r", [1, GW], F32).ap()
    a_r = nc.alloc_sbuf_tensor("a_r", [1, GW], F32).ap()
    recip_r = nc.alloc_sbuf_tensor("recip_r", [1, 512], F32).ap()
    rstd_b = nc.alloc_sbuf_tensor("rstd_b", [128, GW], F32).ap()
    a_b = nc.alloc_sbuf_tensor("a_b", [128, GW], F32).ap()
    recb = nc.alloc_sbuf_tensor("recb", [64, 512], F32).ap()
    kf32 = nc.alloc_sbuf_tensor("kf32", [128, GW], F32).ap()
    kf32b = nc.alloc_sbuf_tensor("kf32b", [128, GW], F32).ap()
    tmpf = nc.alloc_sbuf_tensor("tmpf", [128, GW], F32).ap()
    xbf_c = nc.alloc_sbuf_tensor("xbf_c", [128, 2 * GW], BF16).ap()
    xsq_c = nc.alloc_sbuf_tensor("xsq_c", [128, 2 * GW], BF16).ap()
    xn2pan = nc.alloc_sbuf_tensor("xn2pan", [128, 2 * 512], BF16).ap()

    AR = 156416
    arena = nc.alloc_sbuf_tensor("arena", [128, AR // 2], BF16).ap()

    def abf(off, cols):
        assert off % 2 == 0 and off + 2 * cols <= AR, (off, cols)
        return arena[:, off // 2: off // 2 + cols]

    def af32(off, cols):
        assert off % 4 == 0 and off + 4 * cols <= AR, (off, cols)
        return arena[:, off // 2: off // 2 + 2 * cols].bitcast(F32)

    O_WR, O_WK, O_WV, O_WO = 0, 16384, 32768, 49152
    O_XB, O_XN1, O_D, O_XR, O_XK = 65536, 82176, 90432, 98688, 106944
    O_R, O_X1 = 115200, 131712
    O_AST, O_X1B, O_X2, O_XN3 = 65536, 73792, 90304, 106816
    O_D2, O_XR2, O_XK2, O_R2 = 115072, 123328, 131584, 139840
    O_QG, O_QL, O_KG, O_KL, O_V, O_XGA = 0, 16384, 32768, 49152, 65536, 98304

    s0 = nc.alloc_psum_tensor("S0", [128, 1024], F32).ap()
    s1 = nc.alloc_psum_tensor("S1", [128, 1024], F32).ap()
    ou = nc.alloc_psum_tensor("OU", [128, 1536], F32).ap()
    bcp = nc.alloc_psum_tensor("BCp", [128, 512], F32).ap()

    def banks(t, cs, cw):
        base = {id(s0): 0, id(s1): 2, id(ou): 4, id(bcp): 7}[id(t)]
        return set(range(base + cs // 512, base + (cs + cw - 1) // 512 + 1))

    bank_cons = {i: None for i in range(8)}

    b = Builder()
    LD, GD, SD, CCS, PE, SC, VE = (Ctr() for _ in range(7))

    def psum_w_deps(t, cs, cw):
        return [bank_cons[bk] for bk in banks(t, cs, cw) if bank_cons[bk] is not None]

    def psum_consumed(t, cs, cw, ctr, val):
        for bk in banks(t, cs, cw):
            bank_cons[bk] = (ctr, val)

    lane_defs = [(s0, 0), (s1, 0), (ou, 0)]
    ln_scr_free = [0, 0]
    dbg_hook = [0]

    def dump_sbuf(src_ap, rows, cols, dt):
        dbg_t = nc.dram_tensor("dbg", [rows, cols], dt, kind="ExternalOutput").ap()
        b.wait("scalar", VE, VE.n)
        b.wait("scalar", PE, PE.n)
        b.wait("scalar", SC, SC.n)
        SD.bump(16)
        b.emit("scalar", lambda e, d=dbg_t, s=src_ap: e.dma_start(out=d, in_=s).then_inc(SD.h, 16))
        b.wait("gpsimd", SD, SD.n)
        b.wait("gpsimd", LD, LD.n)
        b.wait("gpsimd", GD, GD.n)

    def dump_dram(src_ap, rows, cols, dt):
        dbg_t = nc.dram_tensor("dbg", [rows, cols], dt, kind="ExternalOutput").ap()
        GD.bump(16)
        b.emit("gpsimd", lambda e, d=dbg_t, s=src_ap: e.dma_start(out=d, in_=s).then_inc(GD.h, 16))
        b.wait("gpsimd", SD, SD.n)
        b.wait("gpsimd", LD, LD.n)
        b.wait("gpsimd", GD, GD.n)

    for (dst, src) in ((mur_s, mur_l), (muk_s, muk_l), (bo_s, bo_l), (hm_s, hm_in)):
        LD.bump(16)
        b.emit("sync", lambda e, d=dst, s=src: e.dma_start(out=d, in_=s).then_inc(LD.h, 16))
    for i in range(10):
        LD.bump(16)
        b.emit("sync", lambda e, d=masks_s[:, 512 * i:512 * (i + 1)], s=mask_t[i].ap():
               e.dma_start(out=d, in_=s).then_inc(LD.h, 16))
    for (dst, src) in ((wqa_s, wqaT), (wka_s, wkaT), (wva_s, wvaT)):
        GD.bump(16)
        b.emit("gpsimd", lambda e, d=dst.rearrange("p (c j) -> p c j", c=NCH),
               s=src.rearrange("(c p) j -> p c j", p=128):
               e.dma_start(out=d, in_=s).then_inc(GD.h, 16))
    for (off, src) in ((O_WR, wrT), (O_WK, wkT), (O_WV, wvT)):
        GD.bump(16)
        b.emit("gpsimd", lambda e, d=abf(off, NCH * 1024).rearrange("p (c j) -> p c j", c=NCH),
               s=src.rearrange("(c p) j -> p c j", p=128):
               e.dma_start(out=d, in_=s).then_inc(GD.h, 16))
    for (ap_, val_) in ((ones_st, 1.0 / 1024.0), (ones_c, 1.0), (ones_bf, 1.0),
                        (ebias_s, EXP_BIAS)):
        VE.bump()
        b.emit("vector", lambda e, a=ap_, vv=val_: e.memset(a, vv).then_inc(VE.h, 1))
    init_ld, init_gd, init_ve = LD.n, GD.n, VE.n

    def ln_pass(src_of_ec, width, dst_of_ec, src_dep_of_ec,
                dst_dep_of_ec=None, post_ec=None):
        blocks = col_blocks(width)
        for ec in range(NCH):
            par = ec % 2
            src = src_of_ec(ec)
            deps = src_dep_of_ec(ec) if src_dep_of_ec else []
            xb = xbf_c[:, GW * par: GW * par + width]
            xq = xsq_c[:, GW * par: GW * par + width]
            b.waits("scalar", deps)
            b.wait("scalar", PE, ln_scr_free[par])
            sc_v = SC.bump()
            b.emit("scalar", lambda e, o=xb, s=src: e.activation(o, s, AF.Copy).then_inc(SC.h, 1))
            b.waits("vector", deps)
            b.wait("vector", PE, ln_scr_free[par])
            ve_v = VE.bump()
            b.emit("vector", lambda e, o=xq, s=src: e.tensor_tensor(o, s, s, ALU.mult).then_inc(VE.h, 1))
            b.wait("tensor", SC, sc_v)
            b.wait("tensor", VE, ve_v)
            if ec == 0:
                b.waits("tensor", psum_w_deps(s0, 0, width))
                b.waits("tensor", psum_w_deps(s1, 0, width))
            pe_v = PE.bump()
            for bi, (cs, cw) in enumerate(blocks):
                b.emit("tensor", lambda e, o=s0[0:1, cs:cs + cw], r=xb[:, cs:cs + cw],
                       st=(ec == 0), sp=(ec == NCH - 1): e.matmul(o, ones_st, r, start=st, stop=sp))
                last = (bi == len(blocks) - 1)
                b.emit("tensor", lambda e, o=s1[0:1, cs:cs + cw], r=xq[:, cs:cs + cw],
                       st=(ec == 0), sp=(ec == NCH - 1), lt=last:
                       e.matmul(o, ones_st, r, start=st, stop=sp).then_inc(PE.h, 1) if lt
                       else e.matmul(o, ones_st, r, start=st, stop=sp))
            ln_scr_free[par] = pe_v
        b.wait("vector", PE, PE.n)
        w = width
        if debug_stage == "stats" and dbg_hook[0] == 0:
            dbg_hook[0] = 1
            b.emit("vector", lambda e: e.tensor_copy(mean_r[:, 0:w], s0[0:1, 0:w]))
            vv = VE.bump()
            b.emit("vector", lambda e: e.tensor_copy(msq_r[:, 0:w], s1[0:1, 0:w]).then_inc(VE.h, 1))
            b.wait("scalar", VE, vv)
            dump_sbuf(mean_r[:, 0:w], 1, w, F32)
            return "DUMPED"
        b.emit("vector", lambda e: e.tensor_copy(mean_r[:, 0:w], s0[0:1, 0:w]))
        b.emit("vector", lambda e: e.tensor_copy(msq_r[:, 0:w], s1[0:1, 0:w]))
        b.emit("vector", lambda e: e.tensor_tensor(a_r[:, 0:w], mean_r[:, 0:w], mean_r[:, 0:w], ALU.mult))
        b.emit("vector", lambda e: e.tensor_tensor(rstd_r[:, 0:w], msq_r[:, 0:w], a_r[:, 0:w], ALU.subtract))
        ve_rows = VE.bump()
        b.emit("vector", lambda e: e.tensor_scalar_add(rstd_r[:, 0:w], rstd_r[:, 0:w], EPS).then_inc(VE.h, 1))
        b.wait("scalar", VE, ve_rows)
        sc_sq = SC.bump()
        b.emit("scalar", lambda e: e.activation(rstd_r[:, 0:w], rstd_r[:, 0:w], AF.Sqrt).then_inc(SC.h, 1))
        b.wait("vector", SC, sc_sq)
        b.emit("vector", lambda e: e.reciprocal(rstd_r[:, 0:w], rstd_r[:, 0:w]))
        ve_rows2 = VE.bump()
        b.emit("vector", lambda e: e.scalar_tensor_tensor(
            a_r[:, 0:w], mean_r[:, 0:w], -1.0, rstd_r[:, 0:w], ALU.mult, ALU.mult).then_inc(VE.h, 1))
        b.wait("tensor", VE, ve_rows2)
        b.waits("tensor", psum_w_deps(ou, 0, width))
        b.waits("tensor", psum_w_deps(s0, 0, width))
        pe_bc = PE.bump()
        for bi, (cs, cw) in enumerate(blocks):
            b.emit("tensor", lambda e, o=ou[:, cs:cs + cw], r=rstd_r[0:1, cs:cs + cw]:
                   e.matmul(o, ones_bf, r, start=True, stop=True))
            last = (bi == len(blocks) - 1)
            b.emit("tensor", lambda e, o=s0[:, cs:cs + cw], r=a_r[0:1, cs:cs + cw], lt=last:
                   e.matmul(o, ones_bf, r, start=True, stop=True).then_inc(PE.h, 1) if lt
                   else e.matmul(o, ones_bf, r, start=True, stop=True))
        b.wait("scalar", PE, pe_bc)
        b.emit("scalar", lambda e: e.activation(rstd_b[:, 0:w], ou[:, 0:w], AF.Copy))
        sc_bc = SC.bump()
        b.emit("scalar", lambda e: e.activation(a_b[:, 0:w], s0[:, 0:w], AF.Copy).then_inc(SC.h, 1))
        psum_consumed(ou, 0, width, SC, sc_bc)
        psum_consumed(s0, 0, width, SC, sc_bc)
        psum_consumed(s1, 0, width, VE, ve_rows)
        b.wait("vector", SC, sc_bc)
        out_vals = []
        for ec in range(NCH):
            src = src_of_ec(ec)
            dst = dst_of_ec(ec)
            if dst_dep_of_ec:
                b.waits("vector", dst_dep_of_ec(ec))
            b.emit("vector", lambda e, s=src: e.tensor_tensor(
                tmpf[:, 0:w], s[:, 0:w], rstd_b[:, 0:w], ALU.mult))
            vv = VE.bump()
            b.emit("vector", lambda e, d=dst: e.tensor_tensor(
                d, tmpf[:, 0:w], a_b[:, 0:w], ALU.add).then_inc(VE.h, 1))
            out_vals.append(vv)
            if post_ec:
                post_ec(ec, vv)
        return out_vals

    def ffn_block(xr_of_ec, xk_of_ec, width, xres_of_ec, res_off, xout_of_ec,
                  wr_off, wk_off, wv_off, r_off, kact_off, mix_done):
        blocks = col_blocks(width)
        kact_free = [0, 0]
        kact_done = [0] * NCH
        b.wait("tensor", VE, mix_done)
        b.wait("tensor", GD, init_gd)
        for dc in range(NCH):
            for proj in range(2):
                woff = wr_off if proj == 0 else wk_off
                ln_t, ln_c = lane_defs[(2 * dc + proj) % 3]
                b.waits("tensor", psum_w_deps(ln_t, ln_c, width))
                pe_v = PE.bump()
                for ec in range(NCH):
                    lhs = abf(woff + 2 * (1024 * ec + 128 * dc), 128)
                    rhs = (xr_of_ec if proj == 0 else xk_of_ec)(ec)
                    for bi, (cs, cw) in enumerate(blocks):
                        last = (ec == NCH - 1 and bi == len(blocks) - 1)
                        b.emit("tensor", lambda e, o=ln_t[:, ln_c + cs:ln_c + cs + cw], l=lhs,
                               r=rhs[:, cs:cs + cw], st=(ec == 0), sp=(ec == NCH - 1), lt=last:
                               e.matmul(o, l, r, start=st, stop=sp).then_inc(PE.h, 1) if lt
                               else e.matmul(o, l, r, start=st, stop=sp))
                b.wait("scalar", PE, pe_v)
                if proj == 0:
                    sc_v = SC.bump()
                    b.emit("scalar", lambda e, o=af32(r_off + 4 * GW * dc, width),
                           i=ln_t[:, ln_c:ln_c + width]:
                           e.activation(o, i, AF.Sigmoid).then_inc(SC.h, 1))
                    psum_consumed(ln_t, ln_c, width, SC, sc_v)
                else:
                    kp = [kf32, kf32b][dc % 2]
                    b.wait("scalar", VE, kact_free[dc % 2])
                    sc_v = SC.bump()
                    b.emit("scalar", lambda e, o=kp[:, 0:width], i=ln_t[:, ln_c:ln_c + width]:
                           e.activation(o, i, AF.Copy).then_inc(SC.h, 1))
                    psum_consumed(ln_t, ln_c, width, SC, sc_v)
                    b.wait("vector", SC, sc_v)
                    ve_v = VE.bump()
                    b.emit("vector", lambda e, o=abf(kact_off + 2 * GW * dc, width), i=kp[:, 0:width]:
                           e.scalar_tensor_tensor(o, i, 0.0, i, ALU.max, ALU.mult).then_inc(VE.h, 1))
                    kact_free[dc % 2] = ve_v
                    kact_done[dc] = ve_v
        out_vals = []
        b.wait("tensor", VE, max(kact_done))
        for dc in range(NCH):
            ln_t, ln_c = lane_defs[dc % 3]
            b.waits("tensor", psum_w_deps(ln_t, ln_c, width))
            pe_v = PE.bump()
            for ac in range(NCH):
                lhs = abf(wv_off + 2 * (1024 * ac + 128 * dc), 128)
                rhs = abf(kact_off + 2 * GW * ac, width)
                for bi, (cs, cw) in enumerate(blocks):
                    last = (ac == NCH - 1 and bi == len(blocks) - 1)
                    b.emit("tensor", lambda e, o=ln_t[:, ln_c + cs:ln_c + cs + cw], l=lhs,
                           r=rhs[:, cs:cs + cw], st=(ac == 0), sp=(ac == NCH - 1), lt=last:
                           e.matmul(o, l, r, start=st, stop=sp).then_inc(PE.h, 1) if lt
                           else e.matmul(o, l, r, start=st, stop=sp))
            b.wait("vector", PE, pe_v)
            b.emit("vector", lambda e, i=ln_t[:, ln_c:ln_c + width],
                   r=af32(r_off + 4 * GW * dc, width):
                   e.scalar_tensor_tensor(tmpf[:, 0:width], i, 0.5, r, ALU.mult, ALU.mult))
            ve_v = VE.bump()
            b.emit("vector", lambda e, o=xout_of_ec(dc), xs=xres_of_ec(dc):
                   e.tensor_tensor(o, xs[:, res_off:res_off + width], tmpf[:, 0:width],
                                   ALU.add).then_inc(VE.h, 1))
            psum_consumed(ln_t, ln_c, width, VE, ve_v)
            out_vals.append(ve_v)
        return out_vals

    def emit_p1():
        xb_free = [0]
        xn2_sd = {}
        b.wait("tensor", VE, init_ve)
        for p in range(NPAN):
            W1, WF = 514, 513

            def xpan(ec):
                return af32(O_XB + 4 * GW * ec, GW)

            b.wait("sync", VE, xb_free[0])
            for ec in range(NCH):
                LD.bump(16)
                b.emit("sync", lambda e, d=xpan(ec)[:, 0:W1],
                       s=xT[128 * ec:128 * (ec + 1), 512 * p:512 * p + W1]:
                       e.dma_start(out=d, in_=s).then_inc(LD.h, 16))
            ldep = [(LD, LD.n), (VE, init_ve)]

            if debug_stage == "xload" and p == 0:
                b.wait("scalar", LD, LD.n)
                dump_sbuf(af32(O_XB, 8 * GW), 128, 8 * GW, F32)
                return "DUMPED"

            if ln_pass(lambda ec: xpan(ec)[:, 0:W1], W1,
                       lambda ec: abf(O_XN1 + 2 * GW * ec, W1),
                       lambda ec: ldep) == "DUMPED":
                return "DUMPED"

            if debug_stage == "ln1" and p == 0:
                dump_sbuf(abf(O_XN1, 8 * GW), 128, 8 * GW, BF16)
                return "DUMPED"

            mix_vals = []
            for ec in range(NCH):
                xn1 = abf(O_XN1 + 2 * GW * ec, GW)
                dt_ = abf(O_D + 2 * GW * ec, WF)
                b.emit("vector", lambda e, o=dt_, x=xn1:
                       e.tensor_tensor(o, x[:, 1:1 + WF], x[:, 0:WF], ALU.subtract))
                b.emit("vector", lambda e, o=abf(O_XR + 2 * GW * ec, WF), d=dt_,
                       m=mur_s[:, ec:ec + 1], x=xn1:
                       e.scalar_tensor_tensor(o, d, m, x[:, 0:WF], ALU.mult, ALU.add))
                vv = VE.bump()
                b.emit("vector", lambda e, o=abf(O_XK + 2 * GW * ec, WF), d=dt_,
                       m=muk_s[:, ec:ec + 1], x=xn1:
                       e.scalar_tensor_tensor(o, d, m, x[:, 0:WF], ALU.mult, ALU.add).then_inc(VE.h, 1))
                mix_vals.append(vv)

            if debug_stage == "mix" and p == 0:
                b.wait("scalar", VE, VE.n)
                dump_sbuf(abf(O_XR, 8 * GW), 128, 8 * GW, BF16)
                return "DUMPED"

            x1_vals = ffn_block(lambda ec: abf(O_XR + 2 * GW * ec, WF),
                                lambda ec: abf(O_XK + 2 * GW * ec, WF),
                                WF, xpan, 1,
                                lambda dc: af32(O_X1 + 4 * GW * dc, WF),
                                O_WR, O_WK, O_WV, O_R, O_D, max(mix_vals))
            xb_free[0] = max(x1_vals)

            if debug_stage == "ffn1" and p == 0:
                b.wait("scalar", VE, VE.n)
                dump_sbuf(af32(O_X1, 8 * GW), 128, 8 * GW, F32)
                return "DUMPED"

            for ec in range(NCH):
                b.wait("scalar", VE, x1_vals[ec])
                SD.bump(16)
                b.emit("scalar", lambda e, d=x1sp[128 * ec:128 * (ec + 1), 512 * p:512 * p + WF],
                       s=af32(O_X1 + 4 * GW * ec, WF):
                       e.dma_start(out=d, in_=s).then_inc(SD.h, 16))

            def post_ec(ec, ve_val, p=p):
                par = ec % 2
                b.wait("scalar", VE, ve_val)
                SD.bump(16)
                b.emit("scalar", lambda e,
                       d=cc1_in.ap()[128 * ec:128 * (ec + 1), 512 * p:512 * p + 512],
                       s=xn2pan[:, 512 * par:512 * par + 512]:
                       e.dma_start(out=d, in_=s).then_inc(SD.h, 16))
                xn2_sd[(p, ec)] = SD.n

            ln_pass(lambda ec: af32(O_X1 + 4 * GW * ec, GW)[:, 1:513], 512,
                    lambda ec: xn2pan[:, 512 * (ec % 2):512 * (ec % 2) + 512],
                    lambda ec, x1_vals=x1_vals: [(VE, x1_vals[ec])],
                    dst_dep_of_ec=lambda ec, p=p: [(SD, xn2_sd.get((p, ec - 2),
                                                                   xn2_sd.get((p - 1, ec + 6), 0)))],
                    post_ec=post_ec)

    if emit_p1() == "DUMPED":
        return _finish(nc, b, (LD, GD, SD, CCS, PE, SC, VE), in_maps)

    if debug_stage == "x1":
        b.wait("gpsimd", SD, SD.n)
        dump_dram(x1sp, E, T + 1, F32)
        return _finish(nc, b, (LD, GD, SD, CCS, PE, SC, VE), in_maps)

    b.wait("gpsimd", SD, SD.n)
    cc1v = CCS.bump()
    b.emit("gpsimd", lambda e: e.collective_compute(
        "AllGather", ALU.bypass, replica_groups=groups,
        ins=[cc1_in.ap().opt()], outs=[cc1_out.ap().opt()]).then_inc(CCS.h, 1))

    if debug_stage == "xn2g":
        b.wait("gpsimd", CCS, cc1v)
        dump_dram(cc1_out.ap(), 4 * E, T, BF16)
        return _finish(nc, b, (LD, GD, SD, CCS, PE, SC, VE), in_maps)

    qk_done = 0
    v_done = 0
    xg_free = 0
    lane_i = 0
    for rr in range(4):
        b.wait("sync", CCS, cc1v)
        b.wait("sync", PE, xg_free)
        for ec in range(NCH):
            LD.bump(16)
            b.emit("sync", lambda e, d=abf(O_XGA + 2 * 2048 * ec, 2048),
                   s=cc1_out.ap()[1024 * rr + 128 * ec:1024 * rr + 128 * (ec + 1), :]:
                   e.dma_start(out=d, in_=s).then_inc(LD.h, 16))
        xg_ld = LD.n
        b.wait("tensor", LD, xg_ld)
        b.wait("tensor", GD, init_gd)
        for kt in range(16):
            ln_t, ln_c = lane_defs[lane_i % 3]
            lane_i += 1
            b.waits("tensor", psum_w_deps(ln_t, ln_c, 256))
            pe_v = PE.bump()
            for ec in range(NCH):
                last = (ec == NCH - 1)
                b.emit("tensor", lambda e, o=ln_t[:, ln_c:ln_c + 256],
                       l=abf(O_XGA + 2 * (2048 * ec + 128 * kt), 128),
                       r=wva_s[:, 256 * ec:256 * (ec + 1)], st=(ec == 0), sp=(ec == NCH - 1), lt=last:
                       e.matmul(o, l, r, start=st, stop=sp).then_inc(PE.h, 1) if lt
                       else e.matmul(o, l, r, start=st, stop=sp))
            b.wait("vector", PE, pe_v)
            ve_v = VE.bump()
            b.emit("vector", lambda e, d=abf(O_V + 2 * 256 * (16 * rr + kt), 256),
                   s=ln_t[:, ln_c:ln_c + 256]: e.tensor_copy(d, s).then_inc(VE.h, 1))
            psum_consumed(ln_t, ln_c, 256, VE, ve_v)
            v_done = ve_v
        for (dst_off, w_s, cls, scale) in ((O_QG, wqa_s, 0, 0.125), (O_QL, wqa_s, 1, 0.125),
                                           (O_KG, wka_s, 0, 1.0), (O_KL, wka_s, 1, 1.0)):
            for cb in range(4):
                ln_t, ln_c = lane_defs[lane_i % 3]
                lane_i += 1
                b.waits("tensor", psum_w_deps(ln_t, ln_c, 512))
                pe_v = PE.bump()
                for ec in range(NCH):
                    last = (ec == NCH - 1)
                    b.emit("tensor", lambda e, o=ln_t[:, ln_c:ln_c + 512],
                           l=w_s[:, 256 * ec + 128 * cls:256 * ec + 128 * cls + 128],
                           r=abf(O_XGA + 2 * (2048 * ec + 512 * cb), 512),
                           st=(ec == 0), sp=(ec == NCH - 1), lt=last:
                           e.matmul(o, l, r, start=st, stop=sp).then_inc(PE.h, 1) if lt
                           else e.matmul(o, l, r, start=st, stop=sp))
                b.wait("scalar", PE, pe_v)
                sc_v = SC.bump()
                b.emit("scalar", lambda e, d=abf(dst_off + 2 * (2048 * rr + 512 * cb), 512),
                       s=ln_t[:, ln_c:ln_c + 512], sc=scale:
                       e.activation(d, s, AF.Copy, scale=sc).then_inc(SC.h, 1))
                psum_consumed(ln_t, ln_c, 512, SC, sc_v)
                qk_done = sc_v
        xg_free = PE.n

    emask_done = [0, 0]
    epar_reader = [0, 0]
    spar_reader = [0, 0]
    kbi = 0
    attn_norm_done = 0

    def emit_vmm(rec, is_first, is_last, cls, slots):
        kb, par, sc_exp, ve_m = rec
        et = e_t[par]
        b.wait("tensor", SC, sc_exp)
        if ve_m is not None:
            b.wait("tensor", VE, ve_m)
        if is_first:
            for pr in range(2):
                b.waits("tensor", psum_w_deps(ou, 512 * slots[pr], 512))
        pe_v = PE.bump()
        for pr in range(2):
            sl = ou[:, 512 * slots[pr]:512 * slots[pr] + 512]
            b.emit("tensor", lambda e, o=sl, l=abf(O_V + 2 * (256 * kb + 128 * cls + 64 * pr), 64),
                   r=et[:, 512 * pr:512 * pr + 512], st=is_first, sp=is_last:
                   e.matmul(o[0:64, :], l, r, start=st, stop=sp))
            b.emit("tensor", lambda e, o=sl, r=et[:, 512 * pr:512 * pr + 512],
                   st=is_first, sp=is_last, lt=(pr == 1):
                   e.matmul(o[64:65, :], ones_c, r, start=st, stop=sp,
                            tile_position=(0, 64)).then_inc(PE.h, 1) if lt
                   else e.matmul(o[64:65, :], ones_c, r, start=st, stop=sp, tile_position=(0, 64)))
        epar_reader[par] = pe_v

    b.wait("tensor", SC, qk_done)
    b.wait("tensor", VE, v_done)
    b.wait("vector", LD, init_ld)
    for cls in range(2):
        q_off = O_QG if cls == 0 else O_QL
        k_off = O_KG if cls == 0 else O_KL
        at_off = O_XGA + 2 * 8192 * cls
        for qb in range(16):
            kbs = (list(range(0, 4 * qb + 4)) if cls == 0
                   else list(range(max(0, 4 * qb - 2), 4 * qb + 4)))
            t_i = 16 * cls + qb
            slots = [(2 * t_i) % 3, (2 * t_i + 1) % 3]
            info = []
            for i, kb in enumerate(kbs):
                par = kbi % 2
                kbi += 1
                st_t = s0 if par == 0 else s1
                et = e_t[par]
                if cls == 0:
                    d = kb - 4 * qb
                    mid = d if d >= 0 else None
                else:
                    mid = 4 + (kb - 4 * qb) + 2
                b.wait("tensor", SC, spar_reader[par])
                b.waits("tensor", psum_w_deps(st_t, 0, 1024))
                b.emit("tensor", lambda e, o=st_t[:, 0:512], l=abf(k_off + 2 * 128 * kb, 128),
                       r=abf(q_off + 2 * 512 * qb, 512):
                       e.matmul(o, l[0:64, :], r[0:64, :], start=True, stop=True))
                pe_sc = PE.bump()
                b.emit("tensor", lambda e, o=st_t[:, 512:1024], l=abf(k_off + 2 * 128 * kb, 128),
                       r=abf(q_off + 2 * 512 * qb, 512):
                       e.matmul(o, l[64:128, :], r[64:128, :], start=True, stop=True).then_inc(PE.h, 1))
                b.wait("scalar", PE, pe_sc)
                b.wait("scalar", PE, epar_reader[par])
                b.wait("scalar", VE, emask_done[par])
                sc_exp = SC.bump()
                b.emit("scalar", lambda e, o=et, s=st_t:
                       e.activation(o, s, AF.Exp, bias=ebias_s).then_inc(SC.h, 1))
                spar_reader[par] = sc_exp
                ve_m = None
                if mid is not None:
                    b.wait("vector", SC, sc_exp)
                    b.wait("vector", PE, epar_reader[par])
                    b.emit("vector", lambda e, o=et[:, 0:512],
                           m=masks_s[:, 512 * mid:512 * mid + 512]:
                           e.tensor_tensor(o, o, m, ALU.mult))
                    ve_m = VE.bump()
                    b.emit("vector", lambda e, o=et[:, 512:1024],
                           m=masks_s[:, 512 * mid:512 * mid + 512]:
                           e.tensor_tensor(o, o, m, ALU.mult).then_inc(VE.h, 1))
                    emask_done[par] = ve_m
                info.append((kb, par, sc_exp, ve_m))
                if i > 0:
                    emit_vmm(info[i - 1], i - 1 == 0, False, cls, slots)
            emit_vmm(info[-1], len(kbs) == 1, True, cls, slots)
            if len(kbs) > 1:
                pass
            for pr in range(2):
                sl = ou[:, 512 * slots[pr]:512 * slots[pr] + 512]
                b.wait("vector", PE, PE.n)
                ve_r = VE.bump()
                b.emit("vector", lambda e, s=sl:
                       e.reciprocal(recip_r[0:1, :], s[64:65, :]).then_inc(VE.h, 1))
                b.wait("tensor", VE, ve_r)
                b.waits("tensor", psum_w_deps(bcp, 0, 512))
                pe_bc = PE.bump()
                b.emit("tensor", lambda e:
                       e.matmul(bcp[0:64, :], ones_bf[0:1, 0:64], recip_r[0:1, :],
                                start=True, stop=True).then_inc(PE.h, 1))
                b.wait("scalar", PE, pe_bc)
                sc_bc = SC.bump()
                b.emit("scalar", lambda e:
                       e.activation(recb, bcp[0:64, :], AF.Copy).then_inc(SC.h, 1))
                psum_consumed(bcp, 0, 512, SC, sc_bc)
                b.wait("vector", SC, sc_bc)
                ve_n = VE.bump()
                b.emit("vector", lambda e, o=abf(at_off + 2 * 512 * qb, 512), s=sl, pr_=pr:
                       e.tensor_tensor(o[64 * pr_:64 * pr_ + 64, :], s[0:64, :], recb,
                                       ALU.mult).then_inc(VE.h, 1))
                psum_consumed(ou, 512 * slots[pr], 512, VE, ve_n)
                attn_norm_done = ve_n

    psum_consumed(s0, 0, 1024, SC, spar_reader[0])
    psum_consumed(s1, 0, 1024, SC, spar_reader[1])

    b.wait("scalar", VE, attn_norm_done)
    for cls in range(2):
        SD.bump(16)
        b.emit("scalar", lambda e, d=cc2_in.ap()[128 * cls:128 * (cls + 1), :],
               s=abf(O_XGA + 2 * 8192 * cls, 8192): e.dma_start(out=d, in_=s).then_inc(SD.h, 16))
    b.wait("gpsimd", SD, SD.n)
    cc2v = CCS.bump()
    b.emit("gpsimd", lambda e: e.collective_compute(
        "AllGather", ALU.bypass, replica_groups=groups,
        ins=[cc2_in.ap().opt()], outs=[cc2_out.ap().opt()]).then_inc(CCS.h, 1))

    if debug_stage == "attn":
        b.wait("gpsimd", CCS, cc2v)
        dump_dram(cc2_out.ap(), E, S, BF16)
        return _finish(nc, b, (LD, GD, SD, CCS, PE, SC, VE), in_maps)

    attn_end_pe = PE.n
    b.wait("gpsimd", PE, attn_end_pe)
    b.wait("gpsimd", VE, attn_norm_done)
    for (off, src) in ((O_WR, wrT), (O_WK, wkT), (O_WV, wvT), (O_WO, woT)):
        GD.bump(16)
        b.emit("gpsimd", lambda e, d=abf(off, NCH * 1024).rearrange("p (c j) -> p c j", c=NCH),
               s=src.rearrange("(c p) j -> p c j", p=128):
               e.dma_start(out=d, in_=s).then_inc(GD.h, 16))
    w2_ld = GD.n

    sw_state = {}

    def stage_switch(e, p):
        if "idx" not in sw_state:
            pid = e.partition_id()
            rreg = e.alloc_register("rksw")
            e.reg_mod(rreg, pid, 4)
            sw_state["idx"] = e.snap(rreg, min_val=0, max_val=3)
        idx = sw_state["idx"]
        cc2o = cc2_out.ap()
        for i in e.Switch(idx, 4):
            if p == 0:
                for ec in range(NCH):
                    e.dma_start(out=abf(O_AST + 2 * GW * ec, GW)[:, 1:513],
                                in_=cc2o[128 * ec:128 * (ec + 1), 2048 * i:2048 * i + 512]
                                ).then_inc(LD.h, 16)
                for ec in range(NCH):
                    hc = max(0, 2048 * i - 1)
                    e.dma_start(out=abf(O_AST + 2 * GW * ec, GW)[:, 0:1],
                                in_=cc2o[128 * ec:128 * (ec + 1), hc:hc + 1],
                                allow_slow_non_contiguous=True).then_inc(LD.h, 16)
            else:
                c0 = 512 * p - 1
                for ec in range(NCH):
                    e.dma_start(out=abf(O_AST + 2 * GW * ec, GW)[:, 0:513],
                                in_=cc2o[128 * ec:128 * (ec + 1), 2048 * i + c0:2048 * i + c0 + 513]
                                ).then_inc(LD.h, 16)

    wo_done_panel = 0
    x2stt_done_panel = 0
    for p in range(NPAN):
        W2 = 513
        b.wait("sync", CCS, cc2v)
        b.wait("sync", PE, max(attn_end_pe, wo_done_panel))
        for _ in range(16 if p == 0 else 8):
            LD.bump(16)
        b.emit("sync", lambda e, p=p: stage_switch(e, p))
        ast_ld = LD.n
        b.wait("sync", VE, x2stt_done_panel)
        for ec in range(NCH):
            LD.bump(16)
            b.emit("sync", lambda e, d=af32(O_X1B + 4 * GW * ec, GW)[:, 0:W2],
                   s=x1sp[128 * ec:128 * (ec + 1), 512 * p:512 * p + W2]:
                   e.dma_start(out=d, in_=s).then_inc(LD.h, 16))
        x1b_ld = LD.n

        x2_vals = []
        b.wait("tensor", GD, w2_ld)
        b.wait("tensor", LD, max(ast_ld, x1b_ld))
        for dc in range(NCH):
            ln_t, ln_c = lane_defs[dc % 3]
            b.waits("tensor", psum_w_deps(ln_t, ln_c, W2))
            pe_v = PE.bump()
            wo_blocks = col_blocks(W2)
            for ac in range(NCH):
                lhs = abf(O_WO + 2 * (1024 * ac + 128 * dc), 128)
                rhs = abf(O_AST + 2 * GW * ac, GW)
                for bi, (cs, cw) in enumerate(wo_blocks):
                    last = (ac == NCH - 1 and bi == len(wo_blocks) - 1)
                    b.emit("tensor", lambda e, o=ln_t[:, ln_c + cs:ln_c + cs + cw], l=lhs,
                           r=rhs[:, cs:cs + cw], st=(ac == 0), sp=(ac == NCH - 1), lt=last:
                           e.matmul(o, l, r, start=st, stop=sp).then_inc(PE.h, 1) if lt
                           else e.matmul(o, l, r, start=st, stop=sp))
            b.wait("vector", PE, pe_v)
            ve_v = VE.bump()
            b.emit("vector", lambda e, o=af32(O_X2 + 4 * GW * dc, W2), i=ln_t[:, ln_c:ln_c + W2],
                   bb=bo_s[:, dc:dc + 1], x1=af32(O_X1B + 4 * GW * dc, W2):
                   e.scalar_tensor_tensor(o, i, bb, x1, ALU.add, ALU.add).then_inc(VE.h, 1))
            psum_consumed(ln_t, ln_c, W2, VE, ve_v)
            if p == 0:
                ve_v = VE.bump()
                b.emit("vector", lambda e, o=af32(O_X2 + 4 * GW * dc, 1):
                       e.tensor_scalar_mul(o, o, hm_s).then_inc(VE.h, 1))
            x2_vals.append(VE.n)
        wo_done_panel = PE.n
        x2stt_done_panel = max(x2_vals)

        ln_pass(lambda ec: af32(O_X2 + 4 * GW * ec, GW)[:, 0:W2], W2,
                lambda ec: abf(O_XN3 + 2 * GW * ec, W2),
                lambda ec, xv=list(x2_vals): [(VE, xv[ec])])

        mix_vals = []
        for ec in range(NCH):
            xn3 = abf(O_XN3 + 2 * GW * ec, GW)
            dt_ = abf(O_D2 + 2 * GW * ec, 512)
            b.emit("vector", lambda e, o=dt_, x=xn3:
                   e.tensor_tensor(o, x[:, 1:513], x[:, 0:512], ALU.subtract))
            b.emit("vector", lambda e, o=abf(O_XR2 + 2 * GW * ec, 512), d=dt_,
                   m=mur_s[:, ec:ec + 1], x=xn3:
                   e.scalar_tensor_tensor(o, d, m, x[:, 0:512], ALU.mult, ALU.add))
            vv = VE.bump()
            b.emit("vector", lambda e, o=abf(O_XK2 + 2 * GW * ec, 512), d=dt_,
                   m=muk_s[:, ec:ec + 1], x=xn3:
                   e.scalar_tensor_tensor(o, d, m, x[:, 0:512], ALU.mult, ALU.add).then_inc(VE.h, 1))
            mix_vals.append(vv)

        x3_vals = ffn_block(lambda ec: abf(O_XR2 + 2 * GW * ec, 512),
                            lambda ec: abf(O_XK2 + 2 * GW * ec, 512),
                            512,
                            lambda ec: af32(O_X2 + 4 * GW * ec, GW), 1,
                            lambda dc: af32(O_X2 + 4 * GW * dc, GW)[:, 1:513],
                            O_WR, O_WK, O_WV, O_R2, O_D2, max(mix_vals))
        x2stt_done_panel = max(x3_vals)

        for ec in range(NCH):
            b.wait("scalar", VE, x3_vals[ec])
            SD.bump(16)
            b.emit("scalar", lambda e, d=y[128 * ec:128 * (ec + 1), 512 * p:512 * p + 512],
                   s=af32(O_X2 + 4 * GW * ec, GW)[:, 1:513]:
                   e.dma_start(out=d, in_=s).then_inc(SD.h, 16))

    b.wait("gpsimd", SD, SD.n)
    b.wait("gpsimd", LD, LD.n)
    b.wait("gpsimd", GD, GD.n)
    return _finish(nc, b, (LD, GD, SD, CCS, PE, SC, VE), in_maps)


def _finish(nc, b, ctrs, in_maps):
    LD, GD, SD, CCS, PE, SC, VE = ctrs
    with ExitStack() as ctx:
        for c, name in zip(ctrs, ("LD", "GD", "SD", "CCS", "PE", "SC", "VE")):
            c.h = ctx.enter_context(nc.semaphore(name))
        with nc.Block() as block:
            if b.q["sync"]:
                @block.sync
                def _(e):
                    for f in b.q["sync"]:
                        f(e)
            if b.q["gpsimd"]:
                @block.gpsimd
                def _(e):
                    for f in b.q["gpsimd"]:
                        f(e)
            if b.q["vector"]:
                @block.vector
                def _(e):
                    for f in b.q["vector"]:
                        f(e)
            if b.q["scalar"]:
                @block.scalar
                def _(e):
                    for f in b.q["scalar"]:
                        f(e)
            if b.q["tensor"]:
                @block.tensor
                def _(e):
                    for f in b.q["tensor"]:
                        f(e)
        if in_maps is None:
            n = sum(len(v) for v in b.q.values())
            print(f"[dry-run] instructions: {n} (sync={len(b.q['sync'])} "
                  f"gp={len(b.q['gpsimd'])} ve={len(b.q['vector'])} "
                  f"sc={len(b.q['scalar'])} pe={len(b.q['tensor'])}) "
                  f"n_bass={nc.n_instructions()}")
            return None
        res = run_bass_kernel_spmd(nc, in_maps, core_ids=list(range(8)))
    return res.results


def host_inputs(inputs):
    x = np.ascontiguousarray(np.asarray(inputs["L0_attn_hidden_states"]), dtype=np.float32)
    Wq = np.asarray(inputs["L0_attn_Wq"])
    Wk = np.asarray(inputs["L0_attn_Wk"])
    Wv = np.asarray(inputs["L0_attn_Wv"])
    Wo = np.asarray(inputs["L0_attn_Wo"])
    bo = np.asarray(inputs["L0_attn_bo"])
    Wr = np.asarray(inputs["L0_ffn_W_r"])
    Wkf = np.asarray(inputs["L0_ffn_W_k"])
    Wvf = np.asarray(inputs["L0_ffn_W_v"])
    mur = np.asarray(inputs["L0_ffn_mu_r"])
    muk = np.asarray(inputs["L0_ffn_mu_k"])

    perm = np.concatenate(
        [np.arange(64 * h, 64 * h + 64) for j in range(4)
         for h in (2 * j, 2 * j + 1, 8 + 2 * j, 8 + 2 * j + 1)])
    lay = lambda v: np.ascontiguousarray(v.reshape(NCH, 128).T, dtype=np.float32)
    base = {
        "wrT": np.ascontiguousarray(Wr.T, dtype=np.float32),
        "wkT": np.ascontiguousarray(Wkf.T, dtype=np.float32),
        "wvT": np.ascontiguousarray(Wvf.T, dtype=np.float32),
        "woT": np.ascontiguousarray(Wo.T[perm, :], dtype=np.float32),
        "bo_l": lay(bo), "mur_l": lay(mur), "muk_l": lay(muk),
    }
    in_maps = []
    for c in range(8):
        bidx, r = c // 4, c % 4
        t0 = 2048 * r
        xs = np.zeros((T + 2, E), dtype=np.float32)
        lo = max(0, t0 - 2)
        xs[2 - (t0 - lo):] = x[bidx, lo: t0 + T]
        heads = [2 * r, 2 * r + 1, 8 + 2 * r, 8 + 2 * r + 1]
        hsl = np.concatenate([np.arange(64 * h, 64 * h + 64) for h in heads])
        m = dict(base)
        m["xT"] = np.ascontiguousarray(xs.T)
        m["wqaT"] = np.ascontiguousarray(Wq[hsl, :].T, dtype=np.float32)
        m["wkaT"] = np.ascontiguousarray(Wk[hsl, :].T, dtype=np.float32)
        m["wvaT"] = np.ascontiguousarray(Wv[hsl, :].T, dtype=np.float32)
        m["hm_in"] = np.full((128, 1), 0.0 if r == 0 else 1.0, dtype=np.float32)
        in_maps.append(m)
    return in_maps


def kernel(**inputs):
    in_maps = host_inputs(inputs)
    results = build_and_run("full", in_maps)
    out = np.empty((2, S, E), dtype=np.float32)
    for c in range(8):
        bidx, r = c // 4, c % 4
        out[bidx, 2048 * r: 2048 * r + T] = results[c]["y"].T
    return out


# version 04
# speedup vs baseline: 1.1032x; 1.1032x over previous
import numpy as np
import ml_dtypes
from contextlib import ExitStack

import concourse.bass as bass
import concourse.mybir as mybir
from concourse.bass_utils import run_bass_kernel_spmd

F32 = mybir.dt.float32
BF16 = mybir.dt.bfloat16
AF = mybir.ActivationFunctionType
ALU = mybir.AluOpType

E = 1024
NCH = 8
T = 2048
S = 8192
NPAN = 4
GW = 516
EXP_BIAS = -20.0
EPS = 1e-5

ENGS = ("sync", "gpsimd", "vector", "scalar", "tensor")


class Ctr:
    def __init__(self):
        self.h = None
        self.n = 0

    def bump(self, k=1):
        self.n += k
        return self.n


class Builder:
    def __init__(self):
        self.q = {e: [] for e in ENGS}
        self.waited = {e: {} for e in ENGS}

    def emit(self, eng, fn):
        self.q[eng].append(fn)

    def wait(self, eng, ctr, val):
        if not val or val <= 0:
            return
        if self.waited[eng].get(id(ctr), 0) >= val:
            return
        self.waited[eng][id(ctr)] = val
        self.emit(eng, lambda e, c=ctr, v=val: e.wait_ge(c.h, v))

    def waits(self, eng, deps):
        for (c, v) in deps:
            self.wait(eng, c, v)


def col_blocks(width, maxn=512):
    out, c = [], 0
    while c < width:
        w = min(maxn, width - c)
        out.append((c, w))
        c += w
    return out


def build_and_run(debug_stage, in_maps):
    nc = bass.Bass("TRN2", target_bir_lowering=False)

    xT = nc.dram_tensor("xT", [E, T + 2], F32, kind="ExternalInput").ap()
    wrT = nc.dram_tensor("wrT", [E, E], F32, kind="ExternalInput").ap()
    wkT = nc.dram_tensor("wkT", [E, E], F32, kind="ExternalInput").ap()
    wvT = nc.dram_tensor("wvT", [E, E], F32, kind="ExternalInput").ap()
    wqaT = nc.dram_tensor("wqaT", [E, 256], F32, kind="ExternalInput").ap()
    wkaT = nc.dram_tensor("wkaT", [E, 256], F32, kind="ExternalInput").ap()
    wvaT = nc.dram_tensor("wvaT", [E, 256], F32, kind="ExternalInput").ap()
    woT = nc.dram_tensor("woT", [E, E], F32, kind="ExternalInput").ap()
    bo_l = nc.dram_tensor("bo_l", [128, NCH], F32, kind="ExternalInput").ap()
    mur_l = nc.dram_tensor("mur_l", [128, NCH], F32, kind="ExternalInput").ap()
    muk_l = nc.dram_tensor("muk_l", [128, NCH], F32, kind="ExternalInput").ap()
    hm_in = nc.dram_tensor("hm_in", [128, 1], F32, kind="ExternalInput").ap()
    y = nc.dram_tensor("y", [E, T], F32, kind="ExternalOutput").ap()

    x1sp = nc.dram_tensor("x1sp", [E, T + 1], F32, kind="Internal").ap()
    cc1_in = nc.dram_tensor("cc1_in", [E, T], BF16, kind="Internal")
    cc1_out = nc.dram_tensor("cc1_out", [4 * E, T], BF16, kind="Internal")
    cc2_in = nc.dram_tensor("cc2_in", [256, S], BF16, kind="Internal")
    cc2_out = nc.dram_tensor("cc2_out", [E, S], BF16, kind="Internal")
    groups = [[0, 1, 2, 3], [4, 5, 6, 7]]


    ki = np.arange(128)[:, None]
    qi = np.arange(512)[None, :]
    masks_np = [((ki + 128 * d) <= qi) for d in range(4)]
    for d in range(-2, 4):
        kk = ki + 128 * d
        masks_np.append((kk <= qi) & (kk >= qi - 256))
    mask_t = [nc.inline_tensor(np.ascontiguousarray(m).astype(ml_dtypes.bfloat16),
                               name=f"mask{i}") for i, m in enumerate(masks_np)]

    masks_s = nc.alloc_sbuf_tensor("masks_s", [128, 10 * 512], BF16).ap()
    e_t = [nc.alloc_sbuf_tensor(f"e{i}", [128, 1024], BF16).ap() for i in range(2)]
    wqa_s = nc.alloc_sbuf_tensor("wqa_s", [128, NCH * 256], BF16).ap()
    wka_s = nc.alloc_sbuf_tensor("wka_s", [128, NCH * 256], BF16).ap()
    wva_s = nc.alloc_sbuf_tensor("wva_s", [128, NCH * 256], BF16).ap()
    mur_s = nc.alloc_sbuf_tensor("mur_s", [128, NCH], F32).ap()
    muk_s = nc.alloc_sbuf_tensor("muk_s", [128, NCH], F32).ap()
    bo_s = nc.alloc_sbuf_tensor("bo_s", [128, NCH], F32).ap()
    hm_s = nc.alloc_sbuf_tensor("hm_s", [128, 1], F32).ap()
    ones_st = nc.alloc_sbuf_tensor("ones_st", [128, 1], BF16).ap()
    ones_c = nc.alloc_sbuf_tensor("ones_c", [128, 1], BF16).ap()
    ones_bf = nc.alloc_sbuf_tensor("ones_bf", [1, 128], F32).ap()
    ebias_s = nc.alloc_sbuf_tensor("ebias_s", [128, 1], F32).ap()
    mean_r = nc.alloc_sbuf_tensor("mean_r", [1, GW], F32).ap()
    msq_r = nc.alloc_sbuf_tensor("msq_r", [1, GW], F32).ap()
    rstd_r = nc.alloc_sbuf_tensor("rstd_r", [1, GW], F32).ap()
    a_r = nc.alloc_sbuf_tensor("a_r", [1, GW], F32).ap()
    recip_r = nc.alloc_sbuf_tensor("recip_r", [1, 512], F32).ap()
    rstd_b = nc.alloc_sbuf_tensor("rstd_b", [128, GW], F32).ap()
    a_b = nc.alloc_sbuf_tensor("a_b", [128, GW], F32).ap()
    recb = nc.alloc_sbuf_tensor("recb", [64, 512], F32).ap()
    kf32 = nc.alloc_sbuf_tensor("kf32", [128, GW], F32).ap()
    kf32b = nc.alloc_sbuf_tensor("kf32b", [128, GW], F32).ap()
    tmpf = nc.alloc_sbuf_tensor("tmpf", [128, GW], F32).ap()
    xbf_c = nc.alloc_sbuf_tensor("xbf_c", [128, 2 * GW], BF16).ap()
    xsq_c = nc.alloc_sbuf_tensor("xsq_c", [128, 2 * GW], BF16).ap()
    xn2pan = nc.alloc_sbuf_tensor("xn2pan", [128, 2 * 512], BF16).ap()

    AR = 156416
    arena = nc.alloc_sbuf_tensor("arena", [128, AR // 2], BF16).ap()

    def abf(off, cols):
        assert off % 2 == 0 and off + 2 * cols <= AR, (off, cols)
        return arena[:, off // 2: off // 2 + cols]

    def af32(off, cols):
        assert off % 4 == 0 and off + 4 * cols <= AR, (off, cols)
        return arena[:, off // 2: off // 2 + 2 * cols].bitcast(F32)

    O_WR, O_WK, O_WV, O_WO = 0, 16384, 32768, 49152
    O_XB, O_XN1, O_D, O_XR, O_XK = 65536, 82176, 90432, 98688, 106944
    O_R, O_X1 = 115200, 131712
    O_AST, O_X1B, O_X2, O_XN3 = 65536, 73792, 90304, 106816
    O_D2, O_XR2, O_XK2, O_R2 = 115072, 123328, 131584, 139840
    O_QG, O_QL, O_KG, O_KL, O_V, O_XGA = 0, 16384, 32768, 49152, 65536, 99328

    s0 = nc.alloc_psum_tensor("S0", [128, 1024], F32).ap()
    s1 = nc.alloc_psum_tensor("S1", [128, 1024], F32).ap()
    ou = nc.alloc_psum_tensor("OU", [128, 1536], F32).ap()
    bcp = nc.alloc_psum_tensor("BCp", [128, 512], F32).ap()

    def banks(t, cs, cw):
        base = {id(s0): 0, id(s1): 2, id(ou): 4, id(bcp): 7}[id(t)]
        return set(range(base + cs // 512, base + (cs + cw - 1) // 512 + 1))

    bank_cons = {i: None for i in range(8)}

    b = Builder()
    LD, GD, SD, CCS, PE, SC, VE = (Ctr() for _ in range(7))

    def psum_w_deps(t, cs, cw):
        return [bank_cons[bk] for bk in banks(t, cs, cw) if bank_cons[bk] is not None]

    def psum_consumed(t, cs, cw, ctr, val):
        for bk in banks(t, cs, cw):
            bank_cons[bk] = (ctr, val)

    lane_defs = [(s0, 0), (s1, 0), (ou, 0)]
    ln_scr_free = [0, 0]
    dbg_hook = [0]

    def dump_sbuf(src_ap, rows, cols, dt):
        dbg_t = nc.dram_tensor("dbg", [rows, cols], dt, kind="ExternalOutput").ap()
        b.wait("scalar", VE, VE.n)
        b.wait("scalar", PE, PE.n)
        b.wait("scalar", SC, SC.n)
        SD.bump(16)
        b.emit("scalar", lambda e, d=dbg_t, s=src_ap: e.dma_start(out=d, in_=s).then_inc(SD.h, 16))
        b.wait("gpsimd", SD, SD.n)
        b.wait("gpsimd", LD, LD.n)
        b.wait("gpsimd", GD, GD.n)

    def dump_dram(src_ap, rows, cols, dt):
        dbg_t = nc.dram_tensor("dbg", [rows, cols], dt, kind="ExternalOutput").ap()
        GD.bump(16)
        b.emit("gpsimd", lambda e, d=dbg_t, s=src_ap: e.dma_start(out=d, in_=s).then_inc(GD.h, 16))
        b.wait("gpsimd", SD, SD.n)
        b.wait("gpsimd", LD, LD.n)
        b.wait("gpsimd", GD, GD.n)

    for (dst, src) in ((mur_s, mur_l), (muk_s, muk_l), (bo_s, bo_l), (hm_s, hm_in)):
        LD.bump(16)
        b.emit("sync", lambda e, d=dst, s=src: e.dma_start(out=d, in_=s).then_inc(LD.h, 16))
    for i in range(10):
        LD.bump(16)
        b.emit("sync", lambda e, d=masks_s[:, 512 * i:512 * (i + 1)], s=mask_t[i].ap():
               e.dma_start(out=d, in_=s).then_inc(LD.h, 16))
    for (dst, src) in ((wqa_s, wqaT), (wka_s, wkaT), (wva_s, wvaT)):
        GD.bump(16)
        b.emit("gpsimd", lambda e, d=dst.rearrange("p (c j) -> p c j", c=NCH),
               s=src.rearrange("(c p) j -> p c j", p=128):
               e.dma_start(out=d, in_=s).then_inc(GD.h, 16))
    for (off, src) in ((O_WR, wrT), (O_WK, wkT), (O_WV, wvT)):
        GD.bump(16)
        b.emit("gpsimd", lambda e, d=abf(off, NCH * 1024).rearrange("p (c j) -> p c j", c=NCH),
               s=src.rearrange("(c p) j -> p c j", p=128):
               e.dma_start(out=d, in_=s).then_inc(GD.h, 16))
    for (ap_, val_) in ((ones_st, 1.0 / 1024.0), (ones_c, 1.0), (ones_bf, 1.0),
                        (ebias_s, EXP_BIAS)):
        VE.bump()
        b.emit("vector", lambda e, a=ap_, vv=val_: e.memset(a, vv).then_inc(VE.h, 1))
    init_ld, init_gd, init_ve = LD.n, GD.n, VE.n

    def ln_pass(src_of_ec, width, dst_of_ec, src_dep_of_ec,
                dst_dep_of_ec=None, post_ec=None):
        blocks = col_blocks(width)
        for ec in range(NCH):
            par = ec % 2
            src = src_of_ec(ec)
            deps = src_dep_of_ec(ec) if src_dep_of_ec else []
            xb = xbf_c[:, GW * par: GW * par + width]
            xq = xsq_c[:, GW * par: GW * par + width]
            b.waits("scalar", deps)
            b.wait("scalar", PE, ln_scr_free[par])
            sc_v = SC.bump()
            b.emit("scalar", lambda e, o=xb, s=src: e.activation(o, s, AF.Copy).then_inc(SC.h, 1))
            b.waits("vector", deps)
            b.wait("vector", PE, ln_scr_free[par])
            ve_v = VE.bump()
            b.emit("vector", lambda e, o=xq, s=src: e.tensor_tensor(o, s, s, ALU.mult).then_inc(VE.h, 1))
            b.wait("tensor", SC, sc_v)
            b.wait("tensor", VE, ve_v)
            if ec == 0:
                b.waits("tensor", psum_w_deps(s0, 0, width))
                b.waits("tensor", psum_w_deps(s1, 0, width))
            pe_v = PE.bump()
            for bi, (cs, cw) in enumerate(blocks):
                b.emit("tensor", lambda e, o=s0[0:1, cs:cs + cw], r=xb[:, cs:cs + cw],
                       st=(ec == 0), sp=(ec == NCH - 1): e.matmul(o, ones_st, r, start=st, stop=sp))
                last = (bi == len(blocks) - 1)
                b.emit("tensor", lambda e, o=s1[0:1, cs:cs + cw], r=xq[:, cs:cs + cw],
                       st=(ec == 0), sp=(ec == NCH - 1), lt=last:
                       e.matmul(o, ones_st, r, start=st, stop=sp).then_inc(PE.h, 1) if lt
                       else e.matmul(o, ones_st, r, start=st, stop=sp))
            ln_scr_free[par] = pe_v
        b.wait("vector", PE, PE.n)
        w = width
        if debug_stage == "stats" and dbg_hook[0] == 0:
            dbg_hook[0] = 1
            b.emit("vector", lambda e: e.tensor_copy(mean_r[:, 0:w], s0[0:1, 0:w]))
            vv = VE.bump()
            b.emit("vector", lambda e: e.tensor_copy(msq_r[:, 0:w], s1[0:1, 0:w]).then_inc(VE.h, 1))
            b.wait("scalar", VE, vv)
            dump_sbuf(mean_r[:, 0:w], 1, w, F32)
            return "DUMPED"
        b.emit("vector", lambda e: e.tensor_copy(mean_r[:, 0:w], s0[0:1, 0:w]))
        b.emit("vector", lambda e: e.tensor_copy(msq_r[:, 0:w], s1[0:1, 0:w]))
        b.emit("vector", lambda e: e.tensor_tensor(a_r[:, 0:w], mean_r[:, 0:w], mean_r[:, 0:w], ALU.mult))
        b.emit("vector", lambda e: e.tensor_tensor(rstd_r[:, 0:w], msq_r[:, 0:w], a_r[:, 0:w], ALU.subtract))
        ve_rows = VE.bump()
        b.emit("vector", lambda e: e.tensor_scalar_add(rstd_r[:, 0:w], rstd_r[:, 0:w], EPS).then_inc(VE.h, 1))
        b.wait("scalar", VE, ve_rows)
        sc_sq = SC.bump()
        b.emit("scalar", lambda e: e.activation(rstd_r[:, 0:w], rstd_r[:, 0:w], AF.Sqrt).then_inc(SC.h, 1))
        b.wait("vector", SC, sc_sq)
        b.emit("vector", lambda e: e.reciprocal(rstd_r[:, 0:w], rstd_r[:, 0:w]))
        ve_rows2 = VE.bump()
        b.emit("vector", lambda e: e.scalar_tensor_tensor(
            a_r[:, 0:w], mean_r[:, 0:w], -1.0, rstd_r[:, 0:w], ALU.mult, ALU.mult).then_inc(VE.h, 1))
        b.wait("tensor", VE, ve_rows2)
        b.waits("tensor", psum_w_deps(ou, 0, width))
        b.waits("tensor", psum_w_deps(s0, 0, width))
        pe_bc = PE.bump()
        for bi, (cs, cw) in enumerate(blocks):
            b.emit("tensor", lambda e, o=ou[:, cs:cs + cw], r=rstd_r[0:1, cs:cs + cw]:
                   e.matmul(o, ones_bf, r, start=True, stop=True))
            last = (bi == len(blocks) - 1)
            b.emit("tensor", lambda e, o=s0[:, cs:cs + cw], r=a_r[0:1, cs:cs + cw], lt=last:
                   e.matmul(o, ones_bf, r, start=True, stop=True).then_inc(PE.h, 1) if lt
                   else e.matmul(o, ones_bf, r, start=True, stop=True))
        b.wait("scalar", PE, pe_bc)
        b.emit("scalar", lambda e: e.activation(rstd_b[:, 0:w], ou[:, 0:w], AF.Copy))
        sc_bc = SC.bump()
        b.emit("scalar", lambda e: e.activation(a_b[:, 0:w], s0[:, 0:w], AF.Copy).then_inc(SC.h, 1))
        psum_consumed(ou, 0, width, SC, sc_bc)
        psum_consumed(s0, 0, width, SC, sc_bc)
        psum_consumed(s1, 0, width, VE, ve_rows)
        b.wait("vector", SC, sc_bc)
        out_vals = []
        for ec in range(NCH):
            src = src_of_ec(ec)
            dst = dst_of_ec(ec)
            if dst_dep_of_ec:
                b.waits("vector", dst_dep_of_ec(ec))
            b.emit("vector", lambda e, s=src: e.tensor_tensor(
                tmpf[:, 0:w], s[:, 0:w], rstd_b[:, 0:w], ALU.mult))
            vv = VE.bump()
            b.emit("vector", lambda e, d=dst: e.tensor_tensor(
                d, tmpf[:, 0:w], a_b[:, 0:w], ALU.add).then_inc(VE.h, 1))
            out_vals.append(vv)
            if post_ec:
                post_ec(ec, vv)
        return out_vals

    def ffn_block(xr_of_ec, xk_of_ec, width, xres_of_ec, res_off, xout_of_ec,
                  wr_off, wk_off, wv_off, r_off, kact_off, mix_done):
        blocks = col_blocks(width)
        kact_free = [0, 0]
        kact_done = [0] * NCH
        b.wait("tensor", VE, mix_done)
        b.wait("tensor", GD, init_gd)
        for dc in range(NCH):
            for proj in range(2):
                woff = wr_off if proj == 0 else wk_off
                ln_t, ln_c = lane_defs[(2 * dc + proj) % 3]
                b.waits("tensor", psum_w_deps(ln_t, ln_c, width))
                pe_v = PE.bump()
                for ec in range(NCH):
                    lhs = abf(woff + 2 * (1024 * ec + 128 * dc), 128)
                    rhs = (xr_of_ec if proj == 0 else xk_of_ec)(ec)
                    for bi, (cs, cw) in enumerate(blocks):
                        last = (ec == NCH - 1 and bi == len(blocks) - 1)
                        b.emit("tensor", lambda e, o=ln_t[:, ln_c + cs:ln_c + cs + cw], l=lhs,
                               r=rhs[:, cs:cs + cw], st=(ec == 0), sp=(ec == NCH - 1), lt=last:
                               e.matmul(o, l, r, start=st, stop=sp).then_inc(PE.h, 1) if lt
                               else e.matmul(o, l, r, start=st, stop=sp))
                b.wait("scalar", PE, pe_v)
                if proj == 0:
                    sc_v = SC.bump()
                    b.emit("scalar", lambda e, o=af32(r_off + 4 * GW * dc, width),
                           i=ln_t[:, ln_c:ln_c + width]:
                           e.activation(o, i, AF.Sigmoid).then_inc(SC.h, 1))
                    psum_consumed(ln_t, ln_c, width, SC, sc_v)
                else:
                    kp = [kf32, kf32b][dc % 2]
                    b.wait("scalar", VE, kact_free[dc % 2])
                    sc_v = SC.bump()
                    b.emit("scalar", lambda e, o=kp[:, 0:width], i=ln_t[:, ln_c:ln_c + width]:
                           e.activation(o, i, AF.Copy).then_inc(SC.h, 1))
                    psum_consumed(ln_t, ln_c, width, SC, sc_v)
                    b.wait("vector", SC, sc_v)
                    ve_v = VE.bump()
                    b.emit("vector", lambda e, o=abf(kact_off + 2 * GW * dc, width), i=kp[:, 0:width]:
                           e.scalar_tensor_tensor(o, i, 0.0, i, ALU.max, ALU.mult).then_inc(VE.h, 1))
                    kact_free[dc % 2] = ve_v
                    kact_done[dc] = ve_v
        out_vals = []
        b.wait("tensor", VE, max(kact_done))
        for dc in range(NCH):
            ln_t, ln_c = lane_defs[dc % 3]
            b.waits("tensor", psum_w_deps(ln_t, ln_c, width))
            pe_v = PE.bump()
            for ac in range(NCH):
                lhs = abf(wv_off + 2 * (1024 * ac + 128 * dc), 128)
                rhs = abf(kact_off + 2 * GW * ac, width)
                for bi, (cs, cw) in enumerate(blocks):
                    last = (ac == NCH - 1 and bi == len(blocks) - 1)
                    b.emit("tensor", lambda e, o=ln_t[:, ln_c + cs:ln_c + cs + cw], l=lhs,
                           r=rhs[:, cs:cs + cw], st=(ac == 0), sp=(ac == NCH - 1), lt=last:
                           e.matmul(o, l, r, start=st, stop=sp).then_inc(PE.h, 1) if lt
                           else e.matmul(o, l, r, start=st, stop=sp))
            b.wait("vector", PE, pe_v)
            b.emit("vector", lambda e, i=ln_t[:, ln_c:ln_c + width],
                   r=af32(r_off + 4 * GW * dc, width):
                   e.scalar_tensor_tensor(tmpf[:, 0:width], i, 0.5, r, ALU.mult, ALU.mult))
            ve_v = VE.bump()
            b.emit("vector", lambda e, o=xout_of_ec(dc), xs=xres_of_ec(dc):
                   e.tensor_tensor(o, xs[:, res_off:res_off + width], tmpf[:, 0:width],
                                   ALU.add).then_inc(VE.h, 1))
            psum_consumed(ln_t, ln_c, width, VE, ve_v)
            out_vals.append(ve_v)
        return out_vals

    def emit_p1():
        xb_free = [0]
        xn2_sd = {}
        b.wait("tensor", VE, init_ve)
        for p in range(NPAN):
            W1, WF = 514, 513

            def xpan(ec):
                return af32(O_XB + 4 * GW * ec, GW)

            b.wait("sync", VE, xb_free[0])
            for ec in range(NCH):
                LD.bump(16)
                b.emit("sync", lambda e, d=xpan(ec)[:, 0:W1],
                       s=xT[128 * ec:128 * (ec + 1), 512 * p:512 * p + W1]:
                       e.dma_start(out=d, in_=s).then_inc(LD.h, 16))
            ldep = [(LD, LD.n), (VE, init_ve)]

            if debug_stage == "xload" and p == 0:
                b.wait("scalar", LD, LD.n)
                dump_sbuf(af32(O_XB, 8 * GW), 128, 8 * GW, F32)
                return "DUMPED"

            if ln_pass(lambda ec: xpan(ec)[:, 0:W1], W1,
                       lambda ec: abf(O_XN1 + 2 * GW * ec, W1),
                       lambda ec: ldep) == "DUMPED":
                return "DUMPED"

            if debug_stage == "ln1" and p == 0:
                dump_sbuf(abf(O_XN1, 8 * GW), 128, 8 * GW, BF16)
                return "DUMPED"

            mix_vals = []
            for ec in range(NCH):
                xn1 = abf(O_XN1 + 2 * GW * ec, GW)
                dt_ = abf(O_D + 2 * GW * ec, WF)
                b.emit("vector", lambda e, o=dt_, x=xn1:
                       e.tensor_tensor(o, x[:, 1:1 + WF], x[:, 0:WF], ALU.subtract))
                b.emit("vector", lambda e, o=abf(O_XR + 2 * GW * ec, WF), d=dt_,
                       m=mur_s[:, ec:ec + 1], x=xn1:
                       e.scalar_tensor_tensor(o, d, m, x[:, 0:WF], ALU.mult, ALU.add))
                vv = VE.bump()
                b.emit("vector", lambda e, o=abf(O_XK + 2 * GW * ec, WF), d=dt_,
                       m=muk_s[:, ec:ec + 1], x=xn1:
                       e.scalar_tensor_tensor(o, d, m, x[:, 0:WF], ALU.mult, ALU.add).then_inc(VE.h, 1))
                mix_vals.append(vv)

            if debug_stage == "mix" and p == 0:
                b.wait("scalar", VE, VE.n)
                dump_sbuf(abf(O_XR, 8 * GW), 128, 8 * GW, BF16)
                return "DUMPED"

            x1_vals = ffn_block(lambda ec: abf(O_XR + 2 * GW * ec, WF),
                                lambda ec: abf(O_XK + 2 * GW * ec, WF),
                                WF, xpan, 1,
                                lambda dc: af32(O_X1 + 4 * GW * dc, WF),
                                O_WR, O_WK, O_WV, O_R, O_D, max(mix_vals))
            xb_free[0] = max(x1_vals)

            if debug_stage == "ffn1" and p == 0:
                b.wait("scalar", VE, VE.n)
                dump_sbuf(af32(O_X1, 8 * GW), 128, 8 * GW, F32)
                return "DUMPED"

            for ec in range(NCH):
                b.wait("scalar", VE, x1_vals[ec])
                SD.bump(16)
                b.emit("scalar", lambda e, d=x1sp[128 * ec:128 * (ec + 1), 512 * p:512 * p + WF],
                       s=af32(O_X1 + 4 * GW * ec, WF):
                       e.dma_start(out=d, in_=s).then_inc(SD.h, 16))

            def post_ec(ec, ve_val, p=p):
                par = ec % 2
                b.wait("scalar", VE, ve_val)
                SD.bump(16)
                b.emit("scalar", lambda e,
                       d=cc1_in.ap()[128 * ec:128 * (ec + 1), 512 * p:512 * p + 512],
                       s=xn2pan[:, 512 * par:512 * par + 512]:
                       e.dma_start(out=d, in_=s).then_inc(SD.h, 16))
                xn2_sd[(p, ec)] = SD.n

            ln_pass(lambda ec: af32(O_X1 + 4 * GW * ec, GW)[:, 1:513], 512,
                    lambda ec: xn2pan[:, 512 * (ec % 2):512 * (ec % 2) + 512],
                    lambda ec, x1_vals=x1_vals: [(VE, x1_vals[ec])],
                    dst_dep_of_ec=lambda ec, p=p: [(SD, xn2_sd.get((p, ec - 2),
                                                                   xn2_sd.get((p - 1, ec + 6), 0)))],
                    post_ec=post_ec)

    if emit_p1() == "DUMPED":
        return _finish(nc, b, (LD, GD, SD, CCS, PE, SC, VE), in_maps)

    if debug_stage == "x1":
        b.wait("gpsimd", SD, SD.n)
        dump_dram(x1sp, E, T + 1, F32)
        return _finish(nc, b, (LD, GD, SD, CCS, PE, SC, VE), in_maps)

    b.wait("gpsimd", SD, SD.n)
    cc1v = CCS.bump()
    b.emit("gpsimd", lambda e: e.collective_compute(
        "AllGather", ALU.bypass, replica_groups=groups,
        ins=[cc1_in.ap().opt()], outs=[cc1_out.ap().opt()]).then_inc(CCS.h, 1))

    if debug_stage == "xn2g":
        b.wait("gpsimd", CCS, cc1v)
        dump_dram(cc1_out.ap(), 4 * E, T, BF16)
        return _finish(nc, b, (LD, GD, SD, CCS, PE, SC, VE), in_maps)

    qk_done = 0
    v_done = 0
    xg_free = 0
    lane_i = 0
    for rr in range(4):
        b.wait("sync", CCS, cc1v)
        b.wait("sync", PE, xg_free)
        for ec in range(NCH):
            LD.bump(16)
            b.emit("sync", lambda e, d=abf(O_XGA + 2 * 2048 * ec, 2048),
                   s=cc1_out.ap()[1024 * rr + 128 * ec:1024 * rr + 128 * (ec + 1), :]:
                   e.dma_start(out=d, in_=s).then_inc(LD.h, 16))
        xg_ld = LD.n
        if rr == 0:
            b.wait("vector", LD, xg_ld)
            VE.bump()
            b.emit("vector", lambda e: e.memset(
                abf(O_V, 64 * 264).rearrange("p (k c) -> p k c", c=66)[:, :, 64:65],
                1.0).then_inc(VE.h, 1))
            if debug_stage == "vones":
                dump_sbuf(abf(O_V, 64 * 264), 128, 64 * 264, BF16)
                return _finish(nc, b, (LD, GD, SD, CCS, PE, SC, VE), in_maps)
        b.wait("tensor", LD, xg_ld)
        b.wait("tensor", GD, init_gd)
        for kt in range(16):
            ln_t, ln_c = lane_defs[lane_i % 3]
            lane_i += 1
            b.waits("tensor", psum_w_deps(ln_t, ln_c, 256))
            pe_v = PE.bump()
            for ec in range(NCH):
                last = (ec == NCH - 1)
                b.emit("tensor", lambda e, o=ln_t[:, ln_c:ln_c + 256],
                       l=abf(O_XGA + 2 * (2048 * ec + 128 * kt), 128),
                       r=wva_s[:, 256 * ec:256 * (ec + 1)], st=(ec == 0), sp=(ec == NCH - 1), lt=last:
                       e.matmul(o, l, r, start=st, stop=sp).then_inc(PE.h, 1) if lt
                       else e.matmul(o, l, r, start=st, stop=sp))
            b.wait("vector", PE, pe_v)
            ve_v = VE.bump()
            b.emit("vector", lambda e,
                   d=abf(O_V + 2 * 264 * (16 * rr + kt), 264).rearrange(
                       "p (h c) -> p h c", c=66)[:, :, 0:64],
                   s=ln_t[:, ln_c:ln_c + 256].rearrange("p (h c) -> p h c", c=64):
                   e.tensor_copy(d, s).then_inc(VE.h, 1))
            psum_consumed(ln_t, ln_c, 256, VE, ve_v)
            v_done = ve_v
        for (dst_off, w_s, cls, scale) in ((O_QG, wqa_s, 0, 0.125), (O_QL, wqa_s, 1, 0.125),
                                           (O_KG, wka_s, 0, 1.0), (O_KL, wka_s, 1, 1.0)):
            for cb in range(4):
                ln_t, ln_c = lane_defs[lane_i % 3]
                lane_i += 1
                b.waits("tensor", psum_w_deps(ln_t, ln_c, 512))
                pe_v = PE.bump()
                for ec in range(NCH):
                    last = (ec == NCH - 1)
                    b.emit("tensor", lambda e, o=ln_t[:, ln_c:ln_c + 512],
                           l=w_s[:, 256 * ec + 128 * cls:256 * ec + 128 * cls + 128],
                           r=abf(O_XGA + 2 * (2048 * ec + 512 * cb), 512),
                           st=(ec == 0), sp=(ec == NCH - 1), lt=last:
                           e.matmul(o, l, r, start=st, stop=sp).then_inc(PE.h, 1) if lt
                           else e.matmul(o, l, r, start=st, stop=sp))
                b.wait("scalar", PE, pe_v)
                sc_v = SC.bump()
                b.emit("scalar", lambda e, d=abf(dst_off + 2 * (2048 * rr + 512 * cb), 512),
                       s=ln_t[:, ln_c:ln_c + 512], sc=scale:
                       e.activation(d, s, AF.Copy, scale=sc).then_inc(SC.h, 1))
                psum_consumed(ln_t, ln_c, 512, SC, sc_v)
                qk_done = sc_v
        xg_free = PE.n

    if debug_stage == "vtile":
        b.wait("scalar", VE, VE.n)
        b.wait("scalar", SC, SC.n)
        b.wait("scalar", PE, PE.n)
        dump_sbuf(abf(O_V, 64 * 264), 128, 64 * 264, BF16)
        return _finish(nc, b, (LD, GD, SD, CCS, PE, SC, VE), in_maps)

    emask_done = [0, 0]
    epar_reader = [0, 0]
    spar_reader = [0, 0]
    kbi = 0
    attn_norm_done = 0

    def emit_vmm(rec, is_first, is_last, cls, slots):
        kb, par, sc_exp, ve_m = rec
        et = e_t[par]
        b.wait("tensor", SC, sc_exp)
        if ve_m is not None:
            b.wait("tensor", VE, ve_m)
        if is_first:
            for pr in range(2):
                b.waits("tensor", psum_w_deps(ou, 512 * slots[pr], 512))
        pe_v = PE.bump()
        for pr in range(2):
            sl = ou[:, 512 * slots[pr]:512 * slots[pr] + 512]
            b.emit("tensor", lambda e, o=sl,
                   l=abf(O_V + 2 * (264 * kb + 66 * (2 * cls + pr)), 65),
                   r=et[:, 512 * pr:512 * pr + 512], st=is_first, sp=is_last, lt=(pr == 1):
                   e.matmul(o[0:65, :], l, r, start=st, stop=sp).then_inc(PE.h, 1) if lt
                   else e.matmul(o[0:65, :], l, r, start=st, stop=sp))
        epar_reader[par] = pe_v

    b.wait("tensor", SC, qk_done)
    b.wait("tensor", VE, v_done)
    b.wait("vector", LD, init_ld)
    for cls in range(2):
        q_off = O_QG if cls == 0 else O_QL
        k_off = O_KG if cls == 0 else O_KL
        at_off = O_XGA + 2 * 8192 * cls
        for qb in range(16):
            kbs = (list(range(0, 4 * qb + 4)) if cls == 0
                   else list(range(max(0, 4 * qb - 2), 4 * qb + 4)))
            t_i = 16 * cls + qb
            slots = [(2 * t_i) % 3, (2 * t_i + 1) % 3]
            info = []
            for i, kb in enumerate(kbs):
                par = kbi % 2
                kbi += 1
                st_t = s0 if par == 0 else s1
                et = e_t[par]
                if cls == 0:
                    d = kb - 4 * qb
                    mid = d if d >= 0 else None
                else:
                    mid = 4 + (kb - 4 * qb) + 2
                b.wait("tensor", SC, spar_reader[par])
                b.waits("tensor", psum_w_deps(st_t, 0, 1024))
                b.emit("tensor", lambda e, o=st_t[:, 0:512], l=abf(k_off + 2 * 128 * kb, 128),
                       r=abf(q_off + 2 * 512 * qb, 512):
                       e.matmul(o, l[0:64, :], r[0:64, :], start=True, stop=True))
                pe_sc = PE.bump()
                b.emit("tensor", lambda e, o=st_t[:, 512:1024], l=abf(k_off + 2 * 128 * kb, 128),
                       r=abf(q_off + 2 * 512 * qb, 512):
                       e.matmul(o, l[64:128, :], r[64:128, :], start=True, stop=True).then_inc(PE.h, 1))
                b.wait("scalar", PE, pe_sc)
                b.wait("scalar", PE, epar_reader[par])
                b.wait("scalar", VE, emask_done[par])
                sc_exp = SC.bump()
                b.emit("scalar", lambda e, o=et, s=st_t:
                       e.activation(o, s, AF.Exp, bias=ebias_s).then_inc(SC.h, 1))
                spar_reader[par] = sc_exp
                ve_m = None
                if mid is not None:
                    b.wait("vector", SC, sc_exp)
                    b.wait("vector", PE, epar_reader[par])
                    b.emit("vector", lambda e, o=et[:, 0:512],
                           m=masks_s[:, 512 * mid:512 * mid + 512]:
                           e.tensor_tensor(o, o, m, ALU.mult))
                    ve_m = VE.bump()
                    b.emit("vector", lambda e, o=et[:, 512:1024],
                           m=masks_s[:, 512 * mid:512 * mid + 512]:
                           e.tensor_tensor(o, o, m, ALU.mult).then_inc(VE.h, 1))
                    emask_done[par] = ve_m
                info.append((kb, par, sc_exp, ve_m))
                if i > 0:
                    emit_vmm(info[i - 1], i - 1 == 0, False, cls, slots)
            emit_vmm(info[-1], len(kbs) == 1, True, cls, slots)
            if len(kbs) > 1:
                pass
            for pr in range(2):
                sl = ou[:, 512 * slots[pr]:512 * slots[pr] + 512]
                b.wait("vector", PE, PE.n)
                ve_r = VE.bump()
                b.emit("vector", lambda e, s=sl:
                       e.reciprocal(recip_r[0:1, :], s[64:65, :]).then_inc(VE.h, 1))
                b.wait("tensor", VE, ve_r)
                b.waits("tensor", psum_w_deps(bcp, 0, 512))
                pe_bc = PE.bump()
                b.emit("tensor", lambda e:
                       e.matmul(bcp[0:64, :], ones_bf[0:1, 0:64], recip_r[0:1, :],
                                start=True, stop=True).then_inc(PE.h, 1))
                b.wait("scalar", PE, pe_bc)
                sc_bc = SC.bump()
                b.emit("scalar", lambda e:
                       e.activation(recb, bcp[0:64, :], AF.Copy).then_inc(SC.h, 1))
                psum_consumed(bcp, 0, 512, SC, sc_bc)
                b.wait("vector", SC, sc_bc)
                ve_n = VE.bump()
                b.emit("vector", lambda e, o=abf(at_off + 2 * 512 * qb, 512), s=sl, pr_=pr:
                       e.tensor_tensor(o[64 * pr_:64 * pr_ + 64, :], s[0:64, :], recb,
                                       ALU.mult).then_inc(VE.h, 1))
                psum_consumed(ou, 512 * slots[pr], 512, VE, ve_n)
                attn_norm_done = ve_n

    psum_consumed(s0, 0, 1024, SC, spar_reader[0])
    psum_consumed(s1, 0, 1024, SC, spar_reader[1])

    b.wait("scalar", VE, attn_norm_done)
    for cls in range(2):
        SD.bump(16)
        b.emit("scalar", lambda e, d=cc2_in.ap()[128 * cls:128 * (cls + 1), :],
               s=abf(O_XGA + 2 * 8192 * cls, 8192): e.dma_start(out=d, in_=s).then_inc(SD.h, 16))
    b.wait("gpsimd", SD, SD.n)
    cc2v = CCS.bump()
    b.emit("gpsimd", lambda e: e.collective_compute(
        "AllGather", ALU.bypass, replica_groups=groups,
        ins=[cc2_in.ap().opt()], outs=[cc2_out.ap().opt()]).then_inc(CCS.h, 1))

    if debug_stage == "attn":
        b.wait("gpsimd", CCS, cc2v)
        dump_dram(cc2_out.ap(), E, S, BF16)
        return _finish(nc, b, (LD, GD, SD, CCS, PE, SC, VE), in_maps)

    attn_end_pe = PE.n
    b.wait("gpsimd", PE, attn_end_pe)
    b.wait("gpsimd", VE, attn_norm_done)
    for (off, src) in ((O_WR, wrT), (O_WK, wkT), (O_WV, wvT), (O_WO, woT)):
        GD.bump(16)
        b.emit("gpsimd", lambda e, d=abf(off, NCH * 1024).rearrange("p (c j) -> p c j", c=NCH),
               s=src.rearrange("(c p) j -> p c j", p=128):
               e.dma_start(out=d, in_=s).then_inc(GD.h, 16))
    w2_ld = GD.n

    sw_state = {}

    def stage_switch(e, p):
        if "idx" not in sw_state:
            pid = e.partition_id()
            rreg = e.alloc_register("rksw")
            e.reg_mod(rreg, pid, 4)
            sw_state["idx"] = e.snap(rreg, min_val=0, max_val=3)
        idx = sw_state["idx"]
        cc2o = cc2_out.ap()
        for i in e.Switch(idx, 4):
            if p == 0:
                for ec in range(NCH):
                    e.dma_start(out=abf(O_AST + 2 * GW * ec, GW)[:, 1:513],
                                in_=cc2o[128 * ec:128 * (ec + 1), 2048 * i:2048 * i + 512]
                                ).then_inc(LD.h, 16)
                for ec in range(NCH):
                    hc = max(0, 2048 * i - 1)
                    e.dma_start(out=abf(O_AST + 2 * GW * ec, GW)[:, 0:1],
                                in_=cc2o[128 * ec:128 * (ec + 1), hc:hc + 1],
                                allow_slow_non_contiguous=True).then_inc(LD.h, 16)
            else:
                c0 = 512 * p - 1
                for ec in range(NCH):
                    e.dma_start(out=abf(O_AST + 2 * GW * ec, GW)[:, 0:513],
                                in_=cc2o[128 * ec:128 * (ec + 1), 2048 * i + c0:2048 * i + c0 + 513]
                                ).then_inc(LD.h, 16)

    wo_done_panel = 0
    x2stt_done_panel = 0
    for p in range(NPAN):
        W2 = 513
        b.wait("sync", CCS, cc2v)
        b.wait("sync", PE, max(attn_end_pe, wo_done_panel))
        for _ in range(16 if p == 0 else 8):
            LD.bump(16)
        b.emit("sync", lambda e, p=p: stage_switch(e, p))
        ast_ld = LD.n
        b.wait("sync", VE, x2stt_done_panel)
        for ec in range(NCH):
            LD.bump(16)
            b.emit("sync", lambda e, d=af32(O_X1B + 4 * GW * ec, GW)[:, 0:W2],
                   s=x1sp[128 * ec:128 * (ec + 1), 512 * p:512 * p + W2]:
                   e.dma_start(out=d, in_=s).then_inc(LD.h, 16))
        x1b_ld = LD.n

        x2_vals = []
        b.wait("tensor", GD, w2_ld)
        b.wait("tensor", LD, max(ast_ld, x1b_ld))
        for dc in range(NCH):
            ln_t, ln_c = lane_defs[dc % 3]
            b.waits("tensor", psum_w_deps(ln_t, ln_c, W2))
            pe_v = PE.bump()
            wo_blocks = col_blocks(W2)
            for ac in range(NCH):
                lhs = abf(O_WO + 2 * (1024 * ac + 128 * dc), 128)
                rhs = abf(O_AST + 2 * GW * ac, GW)
                for bi, (cs, cw) in enumerate(wo_blocks):
                    last = (ac == NCH - 1 and bi == len(wo_blocks) - 1)
                    b.emit("tensor", lambda e, o=ln_t[:, ln_c + cs:ln_c + cs + cw], l=lhs,
                           r=rhs[:, cs:cs + cw], st=(ac == 0), sp=(ac == NCH - 1), lt=last:
                           e.matmul(o, l, r, start=st, stop=sp).then_inc(PE.h, 1) if lt
                           else e.matmul(o, l, r, start=st, stop=sp))
            b.wait("vector", PE, pe_v)
            ve_v = VE.bump()
            b.emit("vector", lambda e, o=af32(O_X2 + 4 * GW * dc, W2), i=ln_t[:, ln_c:ln_c + W2],
                   bb=bo_s[:, dc:dc + 1], x1=af32(O_X1B + 4 * GW * dc, W2):
                   e.scalar_tensor_tensor(o, i, bb, x1, ALU.add, ALU.add).then_inc(VE.h, 1))
            psum_consumed(ln_t, ln_c, W2, VE, ve_v)
            if p == 0:
                ve_v = VE.bump()
                b.emit("vector", lambda e, o=af32(O_X2 + 4 * GW * dc, 1):
                       e.tensor_scalar_mul(o, o, hm_s).then_inc(VE.h, 1))
            x2_vals.append(VE.n)
        wo_done_panel = PE.n
        x2stt_done_panel = max(x2_vals)

        ln_pass(lambda ec: af32(O_X2 + 4 * GW * ec, GW)[:, 0:W2], W2,
                lambda ec: abf(O_XN3 + 2 * GW * ec, W2),
                lambda ec, xv=list(x2_vals): [(VE, xv[ec])])

        mix_vals = []
        for ec in range(NCH):
            xn3 = abf(O_XN3 + 2 * GW * ec, GW)
            dt_ = abf(O_D2 + 2 * GW * ec, 512)
            b.emit("vector", lambda e, o=dt_, x=xn3:
                   e.tensor_tensor(o, x[:, 1:513], x[:, 0:512], ALU.subtract))
            b.emit("vector", lambda e, o=abf(O_XR2 + 2 * GW * ec, 512), d=dt_,
                   m=mur_s[:, ec:ec + 1], x=xn3:
                   e.scalar_tensor_tensor(o, d, m, x[:, 0:512], ALU.mult, ALU.add))
            vv = VE.bump()
            b.emit("vector", lambda e, o=abf(O_XK2 + 2 * GW * ec, 512), d=dt_,
                   m=muk_s[:, ec:ec + 1], x=xn3:
                   e.scalar_tensor_tensor(o, d, m, x[:, 0:512], ALU.mult, ALU.add).then_inc(VE.h, 1))
            mix_vals.append(vv)

        x3_vals = ffn_block(lambda ec: abf(O_XR2 + 2 * GW * ec, 512),
                            lambda ec: abf(O_XK2 + 2 * GW * ec, 512),
                            512,
                            lambda ec: af32(O_X2 + 4 * GW * ec, GW), 1,
                            lambda dc: af32(O_X2 + 4 * GW * dc, GW)[:, 1:513],
                            O_WR, O_WK, O_WV, O_R2, O_D2, max(mix_vals))
        x2stt_done_panel = max(x3_vals)

        for ec in range(NCH):
            b.wait("scalar", VE, x3_vals[ec])
            SD.bump(16)
            b.emit("scalar", lambda e, d=y[128 * ec:128 * (ec + 1), 512 * p:512 * p + 512],
                   s=af32(O_X2 + 4 * GW * ec, GW)[:, 1:513]:
                   e.dma_start(out=d, in_=s).then_inc(SD.h, 16))

    b.wait("gpsimd", SD, SD.n)
    b.wait("gpsimd", LD, LD.n)
    b.wait("gpsimd", GD, GD.n)
    return _finish(nc, b, (LD, GD, SD, CCS, PE, SC, VE), in_maps)


def _finish(nc, b, ctrs, in_maps):
    LD, GD, SD, CCS, PE, SC, VE = ctrs
    with ExitStack() as ctx:
        for c, name in zip(ctrs, ("LD", "GD", "SD", "CCS", "PE", "SC", "VE")):
            c.h = ctx.enter_context(nc.semaphore(name))
        with nc.Block() as block:
            if b.q["sync"]:
                @block.sync
                def _(e):
                    for f in b.q["sync"]:
                        f(e)
            if b.q["gpsimd"]:
                @block.gpsimd
                def _(e):
                    for f in b.q["gpsimd"]:
                        f(e)
            if b.q["vector"]:
                @block.vector
                def _(e):
                    for f in b.q["vector"]:
                        f(e)
            if b.q["scalar"]:
                @block.scalar
                def _(e):
                    for f in b.q["scalar"]:
                        f(e)
            if b.q["tensor"]:
                @block.tensor
                def _(e):
                    for f in b.q["tensor"]:
                        f(e)
        if in_maps is None:
            n = sum(len(v) for v in b.q.values())
            print(f"[dry-run] instructions: {n} (sync={len(b.q['sync'])} "
                  f"gp={len(b.q['gpsimd'])} ve={len(b.q['vector'])} "
                  f"sc={len(b.q['scalar'])} pe={len(b.q['tensor'])}) "
                  f"n_bass={nc.n_instructions()}")
            return None
        res = run_bass_kernel_spmd(nc, in_maps, core_ids=list(range(8)))
    return res.results


def host_inputs(inputs):
    x = np.ascontiguousarray(np.asarray(inputs["L0_attn_hidden_states"]), dtype=np.float32)
    Wq = np.asarray(inputs["L0_attn_Wq"])
    Wk = np.asarray(inputs["L0_attn_Wk"])
    Wv = np.asarray(inputs["L0_attn_Wv"])
    Wo = np.asarray(inputs["L0_attn_Wo"])
    bo = np.asarray(inputs["L0_attn_bo"])
    Wr = np.asarray(inputs["L0_ffn_W_r"])
    Wkf = np.asarray(inputs["L0_ffn_W_k"])
    Wvf = np.asarray(inputs["L0_ffn_W_v"])
    mur = np.asarray(inputs["L0_ffn_mu_r"])
    muk = np.asarray(inputs["L0_ffn_mu_k"])

    perm = np.concatenate(
        [np.arange(64 * h, 64 * h + 64) for j in range(4)
         for h in (2 * j, 2 * j + 1, 8 + 2 * j, 8 + 2 * j + 1)])
    lay = lambda v: np.ascontiguousarray(v.reshape(NCH, 128).T, dtype=np.float32)
    base = {
        "wrT": np.ascontiguousarray(Wr.T, dtype=np.float32),
        "wkT": np.ascontiguousarray(Wkf.T, dtype=np.float32),
        "wvT": np.ascontiguousarray(Wvf.T, dtype=np.float32),
        "woT": np.ascontiguousarray(Wo.T[perm, :], dtype=np.float32),
        "bo_l": lay(bo), "mur_l": lay(mur), "muk_l": lay(muk),
    }
    in_maps = []
    for c in range(8):
        bidx, r = c // 4, c % 4
        t0 = 2048 * r
        xs = np.zeros((T + 2, E), dtype=np.float32)
        lo = max(0, t0 - 2)
        xs[2 - (t0 - lo):] = x[bidx, lo: t0 + T]
        heads = [2 * r, 2 * r + 1, 8 + 2 * r, 8 + 2 * r + 1]
        hsl = np.concatenate([np.arange(64 * h, 64 * h + 64) for h in heads])
        m = dict(base)
        m["xT"] = np.ascontiguousarray(xs.T)
        m["wqaT"] = np.ascontiguousarray(Wq[hsl, :].T, dtype=np.float32)
        m["wkaT"] = np.ascontiguousarray(Wk[hsl, :].T, dtype=np.float32)
        m["wvaT"] = np.ascontiguousarray(Wv[hsl, :].T, dtype=np.float32)
        m["hm_in"] = np.full((128, 1), 0.0 if r == 0 else 1.0, dtype=np.float32)
        in_maps.append(m)
    return in_maps


def kernel(**inputs):
    in_maps = host_inputs(inputs)
    results = build_and_run("full", in_maps)
    out = np.empty((2, S, E), dtype=np.float32)
    for c in range(8):
        bidx, r = c // 4, c % 4
        out[bidx, 2048 * r: 2048 * r + T] = results[c]["y"].T
    return out


# version 05
# speedup vs baseline: 1.1748x; 1.0650x over previous
import numpy as np
import ml_dtypes
from contextlib import ExitStack

import concourse.bass as bass
import concourse.mybir as mybir
from concourse.bass_utils import run_bass_kernel_spmd

F32 = mybir.dt.float32
BF16 = mybir.dt.bfloat16
AF = mybir.ActivationFunctionType
ALU = mybir.AluOpType

E = 1024
NCH = 8
T = 2048
S = 8192
NPAN = 4
GW = 516
EXP_BIAS = -20.0
EPS = 1e-5

ENGS = ("sync", "gpsimd", "vector", "scalar", "tensor")


class Ctr:
    def __init__(self):
        self.h = None
        self.n = 0

    def bump(self, k=1):
        self.n += k
        return self.n


class Builder:
    def __init__(self):
        self.q = {e: [] for e in ENGS}
        self.waited = {e: {} for e in ENGS}

    def emit(self, eng, fn):
        self.q[eng].append(fn)

    def wait(self, eng, ctr, val):
        if not val or val <= 0:
            return
        if self.waited[eng].get(id(ctr), 0) >= val:
            return
        self.waited[eng][id(ctr)] = val
        self.emit(eng, lambda e, c=ctr, v=val: e.wait_ge(c.h, v))

    def waits(self, eng, deps):
        for (c, v) in deps:
            self.wait(eng, c, v)


def col_blocks(width, maxn=512):
    out, c = [], 0
    while c < width:
        w = min(maxn, width - c)
        out.append((c, w))
        c += w
    return out


def build_and_run(debug_stage, in_maps):
    nc = bass.Bass("TRN2", target_bir_lowering=False)

    xT = nc.dram_tensor("xT", [E, T + 2], F32, kind="ExternalInput").ap()
    wrT = nc.dram_tensor("wrT", [E, E], F32, kind="ExternalInput").ap()
    wkT = nc.dram_tensor("wkT", [E, E], F32, kind="ExternalInput").ap()
    wvT = nc.dram_tensor("wvT", [E, E], F32, kind="ExternalInput").ap()
    wqaT = nc.dram_tensor("wqaT", [E, 256], F32, kind="ExternalInput").ap()
    wkaT = nc.dram_tensor("wkaT", [E, 256], F32, kind="ExternalInput").ap()
    wvaT = nc.dram_tensor("wvaT", [E, 256], F32, kind="ExternalInput").ap()
    woT = nc.dram_tensor("woT", [E, E], F32, kind="ExternalInput").ap()
    bo_l = nc.dram_tensor("bo_l", [128, NCH], F32, kind="ExternalInput").ap()
    mur_l = nc.dram_tensor("mur_l", [128, NCH], F32, kind="ExternalInput").ap()
    muk_l = nc.dram_tensor("muk_l", [128, NCH], F32, kind="ExternalInput").ap()
    hm_in = nc.dram_tensor("hm_in", [128, 1], F32, kind="ExternalInput").ap()
    y = nc.dram_tensor("y", [E, T], F32, kind="ExternalOutput").ap()

    x1sp = nc.dram_tensor("x1sp", [E, T + 1], F32, kind="Internal").ap()
    cc1_in = nc.dram_tensor("cc1_in", [E, T], BF16, kind="Internal")
    cc1_out = nc.dram_tensor("cc1_out", [4 * E, T], BF16, kind="Internal")
    cc2_in = nc.dram_tensor("cc2_in", [256, S], BF16, kind="Internal")
    cc2_out = nc.dram_tensor("cc2_out", [E, S], BF16, kind="Internal")
    cc2s_in = nc.dram_tensor("cc2s_in", [4, S], F32, kind="Internal")
    cc2s_out = nc.dram_tensor("cc2s_out", [16, S], F32, kind="Internal")
    groups = [[0, 1, 2, 3], [4, 5, 6, 7]]


    ki = np.arange(128)[:, None]
    qi = np.arange(512)[None, :]
    masks_np = [((ki + 128 * d) <= qi) for d in range(4)]
    for d in range(-2, 4):
        kk = ki + 128 * d
        masks_np.append((kk <= qi) & (kk >= qi - 256))
    mask_t = [nc.inline_tensor(np.ascontiguousarray(m).astype(ml_dtypes.bfloat16),
                               name=f"mask{i}") for i, m in enumerate(masks_np)]
    ind_np = np.zeros((16, 8 * 128), dtype=np.float32)
    for ac in range(8):
        hd_top = 4 * (ac // 2) + 2 * (ac % 2)
        ind_np[hd_top, 128 * ac:128 * ac + 64] = 1.0
        ind_np[hd_top + 1, 128 * ac + 64:128 * ac + 128] = 1.0
    ind_t = nc.inline_tensor(ind_np, name="ind_norm")

    masks_s = nc.alloc_sbuf_tensor("masks_s", [128, 10 * 512], BF16).ap()
    e_t = [nc.alloc_sbuf_tensor(f"e{i}", [128, 1024], BF16).ap() for i in range(2)]
    wqa_s = nc.alloc_sbuf_tensor("wqa_s", [128, NCH * 256], BF16).ap()
    wka_s = nc.alloc_sbuf_tensor("wka_s", [128, NCH * 256], BF16).ap()
    wva_s = nc.alloc_sbuf_tensor("wva_s", [128, NCH * 256], BF16).ap()
    mur_s = nc.alloc_sbuf_tensor("mur_s", [128, NCH], F32).ap()
    muk_s = nc.alloc_sbuf_tensor("muk_s", [128, NCH], F32).ap()
    bo_s = nc.alloc_sbuf_tensor("bo_s", [128, NCH], F32).ap()
    hm_s = nc.alloc_sbuf_tensor("hm_s", [128, 1], F32).ap()
    ones_st = nc.alloc_sbuf_tensor("ones_st", [128, 1], BF16).ap()
    ones_bf = nc.alloc_sbuf_tensor("ones_bf", [1, 128], F32).ap()
    ebias_s = nc.alloc_sbuf_tensor("ebias_s", [128, 1], F32).ap()
    mean_r = nc.alloc_sbuf_tensor("mean_r", [1, GW], F32).ap()
    msq_r = nc.alloc_sbuf_tensor("msq_r", [1, GW], F32).ap()
    rstd_r = nc.alloc_sbuf_tensor("rstd_r", [1, GW], F32).ap()
    a_r = nc.alloc_sbuf_tensor("a_r", [1, GW], F32).ap()
    ind_s = nc.alloc_sbuf_tensor("ind_s", [16, 8 * 128], F32).ap()
    sums_st = nc.alloc_sbuf_tensor("sums_st", [16, 520], F32).ap()
    srow = nc.alloc_sbuf_tensor("srow", [1, 2 * 512], F32).ap()
    rstd_b = nc.alloc_sbuf_tensor("rstd_b", [128, GW], F32).ap()
    a_b = nc.alloc_sbuf_tensor("a_b", [128, GW], F32).ap()
    kf32 = nc.alloc_sbuf_tensor("kf32", [128, GW], F32).ap()
    kf32b = nc.alloc_sbuf_tensor("kf32b", [128, GW], F32).ap()
    tmpf = nc.alloc_sbuf_tensor("tmpf", [128, GW], F32).ap()
    xbf_c = nc.alloc_sbuf_tensor("xbf_c", [128, 2 * GW], BF16).ap()
    xsq_c = nc.alloc_sbuf_tensor("xsq_c", [128, 2 * GW], BF16).ap()
    xn2pan = nc.alloc_sbuf_tensor("xn2pan", [128, 2 * 512], BF16).ap()

    AR = 139840
    arena = nc.alloc_sbuf_tensor("arena", [128, AR // 2], BF16).ap()

    def abf(off, cols):
        assert off % 2 == 0 and off + 2 * cols <= AR, (off, cols)
        return arena[:, off // 2: off // 2 + cols]

    def af32(off, cols):
        assert off % 4 == 0 and off + 4 * cols <= AR, (off, cols)
        return arena[:, off // 2: off // 2 + 2 * cols].bitcast(F32)

    O_WR, O_WK, O_WV, O_WO = 0, 16384, 32768, 49152
    O_XB, O_XN1, O_D, O_XR, O_XK = 65536, 82176, 90432, 98688, 106944
    O_R = 115200
    O_AST, O_X1B, O_X2, O_XN3 = 65536, 73792, 90304, 106816
    O_D2, O_XR2, O_XK2 = 115072, 123328, 131584
    O_R2 = O_X1B
    O_QG, O_QL, O_KG, O_KL, O_V, O_XGA = 0, 16384, 32768, 49152, 65536, 99328

    s0 = nc.alloc_psum_tensor("S0", [128, 1024], F32).ap()
    s1 = nc.alloc_psum_tensor("S1", [128, 1024], F32).ap()
    ou = nc.alloc_psum_tensor("OU", [128, 1536], F32).ap()
    bcp = nc.alloc_psum_tensor("BCp", [128, 512], F32).ap()

    def banks(t, cs, cw):
        base = {id(s0): 0, id(s1): 2, id(ou): 4, id(bcp): 7}[id(t)]
        return set(range(base + cs // 512, base + (cs + cw - 1) // 512 + 1))

    bank_cons = {i: [] for i in range(8)}

    b = Builder()
    LD, GD, SD, CCS, PE, SC, VE = (Ctr() for _ in range(7))

    def psum_w_deps(t, cs, cw):
        out = []
        for bk in banks(t, cs, cw):
            out.extend(bank_cons[bk])
        return out

    def psum_consumed(t, cs, cw, ctr, val):
        for bk in banks(t, cs, cw):
            bank_cons[bk] = [(ctr, val)]

    def psum_consumed_add(t, cs, cw, ctr, val):
        for bk in banks(t, cs, cw):
            bank_cons[bk].append((ctr, val))

    lane_defs = [(s0, 0), (s1, 0), (ou, 0)]
    ln_scr_free = [0, 0]
    dbg_hook = [0]

    def dump_sbuf(src_ap, rows, cols, dt):
        dbg_t = nc.dram_tensor("dbg", [rows, cols], dt, kind="ExternalOutput").ap()
        b.wait("scalar", VE, VE.n)
        b.wait("scalar", PE, PE.n)
        b.wait("scalar", SC, SC.n)
        SD.bump(16)
        b.emit("scalar", lambda e, d=dbg_t, s=src_ap: e.dma_start(out=d, in_=s).then_inc(SD.h, 16))
        b.wait("gpsimd", SD, SD.n)
        b.wait("gpsimd", LD, LD.n)
        b.wait("gpsimd", GD, GD.n)

    def dump_dram(src_ap, rows, cols, dt):
        dbg_t = nc.dram_tensor("dbg", [rows, cols], dt, kind="ExternalOutput").ap()
        GD.bump(16)
        b.emit("gpsimd", lambda e, d=dbg_t, s=src_ap: e.dma_start(out=d, in_=s).then_inc(GD.h, 16))
        b.wait("gpsimd", SD, SD.n)
        b.wait("gpsimd", LD, LD.n)
        b.wait("gpsimd", GD, GD.n)

    for (dst, src) in ((mur_s, mur_l), (muk_s, muk_l), (bo_s, bo_l), (hm_s, hm_in)):
        LD.bump(16)
        b.emit("sync", lambda e, d=dst, s=src: e.dma_start(out=d, in_=s).then_inc(LD.h, 16))
    for i in range(10):
        LD.bump(16)
        b.emit("sync", lambda e, d=masks_s[:, 512 * i:512 * (i + 1)], s=mask_t[i].ap():
               e.dma_start(out=d, in_=s).then_inc(LD.h, 16))
    LD.bump(16)
    b.emit("sync", lambda e: e.dma_start(out=ind_s, in_=ind_t.ap()).then_inc(LD.h, 16))
    for (dst, src) in ((wqa_s, wqaT), (wka_s, wkaT), (wva_s, wvaT)):
        GD.bump(16)
        b.emit("gpsimd", lambda e, d=dst.rearrange("p (c j) -> p c j", c=NCH),
               s=src.rearrange("(c p) j -> p c j", p=128):
               e.dma_start(out=d, in_=s).then_inc(GD.h, 16))
    for (off, src) in ((O_WR, wrT), (O_WK, wkT), (O_WV, wvT)):
        GD.bump(16)
        b.emit("gpsimd", lambda e, d=abf(off, NCH * 1024).rearrange("p (c j) -> p c j", c=NCH),
               s=src.rearrange("(c p) j -> p c j", p=128):
               e.dma_start(out=d, in_=s).then_inc(GD.h, 16))
    for (ap_, val_) in ((ones_st, 1.0 / 1024.0), (ones_bf, 1.0),
                        (ebias_s, EXP_BIAS)):
        VE.bump()
        b.emit("vector", lambda e, a=ap_, vv=val_: e.memset(a, vv).then_inc(VE.h, 1))
    init_ld, init_gd, init_ve = LD.n, GD.n, VE.n

    def ln_pass(src_of_ec, width, dst_of_ec, src_dep_of_ec,
                dst_dep_of_ec=None, post_ec=None):
        blocks = col_blocks(width)
        for ec in range(NCH):
            par = ec % 2
            src = src_of_ec(ec)
            deps = src_dep_of_ec(ec) if src_dep_of_ec else []
            xb = xbf_c[:, GW * par: GW * par + width]
            xq = xsq_c[:, GW * par: GW * par + width]
            b.waits("scalar", deps)
            b.wait("scalar", PE, ln_scr_free[par])
            sc_v = SC.bump()
            b.emit("scalar", lambda e, o=xb, s=src: e.activation(o, s, AF.Copy).then_inc(SC.h, 1))
            b.waits("vector", deps)
            b.wait("vector", PE, ln_scr_free[par])
            ve_v = VE.bump()
            b.emit("vector", lambda e, o=xq, s=src: e.tensor_tensor(o, s, s, ALU.mult).then_inc(VE.h, 1))
            b.wait("tensor", SC, sc_v)
            b.wait("tensor", VE, ve_v)
            if ec == 0:
                b.waits("tensor", psum_w_deps(s0, 0, width))
                b.waits("tensor", psum_w_deps(s1, 0, width))
            pe_v = PE.bump()
            for bi, (cs, cw) in enumerate(blocks):
                b.emit("tensor", lambda e, o=s0[0:1, cs:cs + cw], r=xb[:, cs:cs + cw],
                       st=(ec == 0), sp=(ec == NCH - 1): e.matmul(o, ones_st, r, start=st, stop=sp))
                last = (bi == len(blocks) - 1)
                b.emit("tensor", lambda e, o=s1[0:1, cs:cs + cw], r=xq[:, cs:cs + cw],
                       st=(ec == 0), sp=(ec == NCH - 1), lt=last:
                       e.matmul(o, ones_st, r, start=st, stop=sp).then_inc(PE.h, 1) if lt
                       else e.matmul(o, ones_st, r, start=st, stop=sp))
            ln_scr_free[par] = pe_v
        b.wait("vector", PE, PE.n)
        w = width
        if debug_stage == "stats" and dbg_hook[0] == 0:
            dbg_hook[0] = 1
            b.emit("vector", lambda e: e.tensor_copy(mean_r[:, 0:w], s0[0:1, 0:w]))
            vv = VE.bump()
            b.emit("vector", lambda e: e.tensor_copy(msq_r[:, 0:w], s1[0:1, 0:w]).then_inc(VE.h, 1))
            b.wait("scalar", VE, vv)
            dump_sbuf(mean_r[:, 0:w], 1, w, F32)
            return "DUMPED"
        b.emit("vector", lambda e: e.tensor_copy(mean_r[:, 0:w], s0[0:1, 0:w]))
        b.emit("vector", lambda e: e.tensor_copy(msq_r[:, 0:w], s1[0:1, 0:w]))
        b.emit("vector", lambda e: e.tensor_tensor(a_r[:, 0:w], mean_r[:, 0:w], mean_r[:, 0:w], ALU.mult))
        b.emit("vector", lambda e: e.tensor_tensor(rstd_r[:, 0:w], msq_r[:, 0:w], a_r[:, 0:w], ALU.subtract))
        ve_rows = VE.bump()
        b.emit("vector", lambda e: e.tensor_scalar_add(rstd_r[:, 0:w], rstd_r[:, 0:w], EPS).then_inc(VE.h, 1))
        b.wait("scalar", VE, ve_rows)
        sc_sq = SC.bump()
        b.emit("scalar", lambda e: e.activation(rstd_r[:, 0:w], rstd_r[:, 0:w], AF.Sqrt).then_inc(SC.h, 1))
        b.wait("vector", SC, sc_sq)
        b.emit("vector", lambda e: e.reciprocal(rstd_r[:, 0:w], rstd_r[:, 0:w]))
        ve_rows2 = VE.bump()
        b.emit("vector", lambda e: e.scalar_tensor_tensor(
            a_r[:, 0:w], mean_r[:, 0:w], -1.0, rstd_r[:, 0:w], ALU.mult, ALU.mult).then_inc(VE.h, 1))
        b.wait("tensor", VE, ve_rows2)
        b.waits("tensor", psum_w_deps(ou, 0, width))
        b.waits("tensor", psum_w_deps(s0, 0, width))
        pe_bc = PE.bump()
        for bi, (cs, cw) in enumerate(blocks):
            b.emit("tensor", lambda e, o=ou[:, cs:cs + cw], r=rstd_r[0:1, cs:cs + cw]:
                   e.matmul(o, ones_bf, r, start=True, stop=True))
            last = (bi == len(blocks) - 1)
            b.emit("tensor", lambda e, o=s0[:, cs:cs + cw], r=a_r[0:1, cs:cs + cw], lt=last:
                   e.matmul(o, ones_bf, r, start=True, stop=True).then_inc(PE.h, 1) if lt
                   else e.matmul(o, ones_bf, r, start=True, stop=True))
        b.wait("scalar", PE, pe_bc)
        b.emit("scalar", lambda e: e.activation(rstd_b[:, 0:w], ou[:, 0:w], AF.Copy))
        sc_bc = SC.bump()
        b.emit("scalar", lambda e: e.activation(a_b[:, 0:w], s0[:, 0:w], AF.Copy).then_inc(SC.h, 1))
        psum_consumed(ou, 0, width, SC, sc_bc)
        psum_consumed(s0, 0, width, SC, sc_bc)
        psum_consumed(s1, 0, width, VE, ve_rows)
        b.wait("vector", SC, sc_bc)
        out_vals = []
        for ec in range(NCH):
            src = src_of_ec(ec)
            dst = dst_of_ec(ec)
            if dst_dep_of_ec:
                b.waits("vector", dst_dep_of_ec(ec))
            b.emit("vector", lambda e, s=src: e.tensor_tensor(
                tmpf[:, 0:w], s[:, 0:w], rstd_b[:, 0:w], ALU.mult))
            vv = VE.bump()
            b.emit("vector", lambda e, d=dst: e.tensor_tensor(
                d, tmpf[:, 0:w], a_b[:, 0:w], ALU.add).then_inc(VE.h, 1))
            out_vals.append(vv)
            if post_ec:
                post_ec(ec, vv)
        return out_vals

    def ffn_block(xr_of_ec, xk_of_ec, width, xres_of_ec, res_off, xout_of_ec,
                  wr_off, wk_off, wv_off, r_off, kact_off, mix_done):
        blocks = col_blocks(width)
        kact_free = [0, 0]
        kact_done = [0] * NCH
        b.wait("tensor", VE, mix_done)
        b.wait("tensor", GD, init_gd)
        for dc in range(NCH):
            for proj in range(2):
                woff = wr_off if proj == 0 else wk_off
                ln_t, ln_c = lane_defs[(2 * dc + proj) % 3]
                b.waits("tensor", psum_w_deps(ln_t, ln_c, width))
                pe_v = PE.bump()
                for ec in range(NCH):
                    lhs = abf(woff + 2 * (1024 * ec + 128 * dc), 128)
                    rhs = (xr_of_ec if proj == 0 else xk_of_ec)(ec)
                    for bi, (cs, cw) in enumerate(blocks):
                        last = (ec == NCH - 1 and bi == len(blocks) - 1)
                        b.emit("tensor", lambda e, o=ln_t[:, ln_c + cs:ln_c + cs + cw], l=lhs,
                               r=rhs[:, cs:cs + cw], st=(ec == 0), sp=(ec == NCH - 1), lt=last:
                               e.matmul(o, l, r, start=st, stop=sp).then_inc(PE.h, 1) if lt
                               else e.matmul(o, l, r, start=st, stop=sp))
                b.wait("scalar", PE, pe_v)
                if proj == 0:
                    sc_v = SC.bump()
                    b.emit("scalar", lambda e, o=af32(r_off + 4 * GW * dc, width),
                           i=ln_t[:, ln_c:ln_c + width]:
                           e.activation(o, i, AF.Sigmoid).then_inc(SC.h, 1))
                    psum_consumed(ln_t, ln_c, width, SC, sc_v)
                else:
                    kp = [kf32, kf32b][dc % 2]
                    b.wait("scalar", VE, kact_free[dc % 2])
                    sc_v = SC.bump()
                    b.emit("scalar", lambda e, o=kp[:, 0:width], i=ln_t[:, ln_c:ln_c + width]:
                           e.activation(o, i, AF.Copy).then_inc(SC.h, 1))
                    psum_consumed(ln_t, ln_c, width, SC, sc_v)
                    b.wait("vector", SC, sc_v)
                    ve_v = VE.bump()
                    b.emit("vector", lambda e, o=abf(kact_off + 2 * GW * dc, width), i=kp[:, 0:width]:
                           e.scalar_tensor_tensor(o, i, 0.0, i, ALU.max, ALU.mult).then_inc(VE.h, 1))
                    kact_free[dc % 2] = ve_v
                    kact_done[dc] = ve_v
        out_vals = []
        b.wait("tensor", VE, max(kact_done))
        for dc in range(NCH):
            ln_t, ln_c = lane_defs[dc % 3]
            b.waits("tensor", psum_w_deps(ln_t, ln_c, width))
            pe_v = PE.bump()
            for ac in range(NCH):
                lhs = abf(wv_off + 2 * (1024 * ac + 128 * dc), 128)
                rhs = abf(kact_off + 2 * GW * ac, width)
                for bi, (cs, cw) in enumerate(blocks):
                    last = (ac == NCH - 1 and bi == len(blocks) - 1)
                    b.emit("tensor", lambda e, o=ln_t[:, ln_c + cs:ln_c + cs + cw], l=lhs,
                           r=rhs[:, cs:cs + cw], st=(ac == 0), sp=(ac == NCH - 1), lt=last:
                           e.matmul(o, l, r, start=st, stop=sp).then_inc(PE.h, 1) if lt
                           else e.matmul(o, l, r, start=st, stop=sp))
            b.wait("vector", PE, pe_v)
            b.emit("vector", lambda e, i=ln_t[:, ln_c:ln_c + width],
                   r=af32(r_off + 4 * GW * dc, width):
                   e.scalar_tensor_tensor(tmpf[:, 0:width], i, 0.5, r, ALU.mult, ALU.mult))
            ve_v = VE.bump()
            b.emit("vector", lambda e, o=xout_of_ec(dc), xs=xres_of_ec(dc):
                   e.tensor_tensor(o, xs[:, res_off:res_off + width], tmpf[:, 0:width],
                                   ALU.add).then_inc(VE.h, 1))
            psum_consumed(ln_t, ln_c, width, VE, ve_v)
            out_vals.append(ve_v)
        return out_vals

    def emit_p1():
        xb_free = [0]
        xb_free_sd = [0]
        xb_free_sc = [0]
        xn2_sd = {}
        b.wait("tensor", VE, init_ve)
        for p in range(NPAN):
            W1, WF = 514, 513

            def xpan(ec):
                return af32(O_XB + 4 * GW * ec, GW)

            b.wait("sync", VE, xb_free[0])
            b.wait("sync", SD, xb_free_sd[0])
            b.wait("sync", SC, xb_free_sc[0])
            for ec in range(NCH):
                LD.bump(16)
                b.emit("sync", lambda e, d=xpan(ec)[:, 0:W1],
                       s=xT[128 * ec:128 * (ec + 1), 512 * p:512 * p + W1]:
                       e.dma_start(out=d, in_=s).then_inc(LD.h, 16))
            ldep = [(LD, LD.n), (VE, init_ve)]

            if debug_stage == "xload" and p == 0:
                b.wait("scalar", LD, LD.n)
                dump_sbuf(af32(O_XB, 8 * GW), 128, 8 * GW, F32)
                return "DUMPED"

            if ln_pass(lambda ec: xpan(ec)[:, 0:W1], W1,
                       lambda ec: abf(O_XN1 + 2 * GW * ec, W1),
                       lambda ec: ldep) == "DUMPED":
                return "DUMPED"

            if debug_stage == "ln1" and p == 0:
                dump_sbuf(abf(O_XN1, 8 * GW), 128, 8 * GW, BF16)
                return "DUMPED"

            mix_vals = []
            for ec in range(NCH):
                xn1 = abf(O_XN1 + 2 * GW * ec, GW)
                dt_ = abf(O_D + 2 * GW * ec, WF)
                b.emit("vector", lambda e, o=dt_, x=xn1:
                       e.tensor_tensor(o, x[:, 1:1 + WF], x[:, 0:WF], ALU.subtract))
                b.emit("vector", lambda e, o=abf(O_XR + 2 * GW * ec, WF), d=dt_,
                       m=mur_s[:, ec:ec + 1], x=xn1:
                       e.scalar_tensor_tensor(o, d, m, x[:, 0:WF], ALU.mult, ALU.add))
                vv = VE.bump()
                b.emit("vector", lambda e, o=abf(O_XK + 2 * GW * ec, WF), d=dt_,
                       m=muk_s[:, ec:ec + 1], x=xn1:
                       e.scalar_tensor_tensor(o, d, m, x[:, 0:WF], ALU.mult, ALU.add).then_inc(VE.h, 1))
                mix_vals.append(vv)

            if debug_stage == "mix" and p == 0:
                b.wait("scalar", VE, VE.n)
                dump_sbuf(abf(O_XR, 8 * GW), 128, 8 * GW, BF16)
                return "DUMPED"

            x1_vals = ffn_block(lambda ec: abf(O_XR + 2 * GW * ec, WF),
                                lambda ec: abf(O_XK + 2 * GW * ec, WF),
                                WF, xpan, 1,
                                lambda dc: xpan(dc)[:, 1:1 + WF],
                                O_WR, O_WK, O_WV, O_R, O_D, max(mix_vals))

            if debug_stage == "ffn1" and p == 0:
                b.wait("scalar", VE, VE.n)
                dump_sbuf(af32(O_XB, 8 * GW), 128, 8 * GW, F32)
                return "DUMPED"

            for ec in range(NCH):
                b.wait("scalar", VE, x1_vals[ec])
                SD.bump(16)
                b.emit("scalar", lambda e, d=x1sp[128 * ec:128 * (ec + 1), 512 * p:512 * p + WF],
                       s=xpan(ec)[:, 1:1 + WF]:
                       e.dma_start(out=d, in_=s).then_inc(SD.h, 16))

            def post_ec(ec, ve_val, p=p):
                par = ec % 2
                b.wait("scalar", VE, ve_val)
                SD.bump(16)
                b.emit("scalar", lambda e,
                       d=cc1_in.ap()[128 * ec:128 * (ec + 1), 512 * p:512 * p + 512],
                       s=xn2pan[:, 512 * par:512 * par + 512]:
                       e.dma_start(out=d, in_=s).then_inc(SD.h, 16))
                xn2_sd[(p, ec)] = SD.n

            xn2_vals = ln_pass(
                lambda ec: xpan(ec)[:, 2:514], 512,
                lambda ec: xn2pan[:, 512 * (ec % 2):512 * (ec % 2) + 512],
                lambda ec, x1_vals=x1_vals: [(VE, x1_vals[ec])],
                dst_dep_of_ec=lambda ec, p=p: [(SD, xn2_sd.get((p, ec - 2),
                                                               xn2_sd.get((p - 1, ec + 6), 0)))],
                post_ec=post_ec)
            xb_free[0] = max(xn2_vals)
            xb_free_sd[0] = SD.n
            xb_free_sc[0] = SC.n

    if emit_p1() == "DUMPED":
        return _finish(nc, b, (LD, GD, SD, CCS, PE, SC, VE), in_maps)

    if debug_stage == "x1":
        b.wait("gpsimd", SD, SD.n)
        dump_dram(x1sp, E, T + 1, F32)
        return _finish(nc, b, (LD, GD, SD, CCS, PE, SC, VE), in_maps)

    b.wait("gpsimd", SD, SD.n)
    cc1v = CCS.bump()
    b.emit("gpsimd", lambda e: e.collective_compute(
        "AllGather", ALU.bypass, replica_groups=groups,
        ins=[cc1_in.ap().opt()], outs=[cc1_out.ap().opt()]).then_inc(CCS.h, 1))

    if debug_stage == "xn2g":
        b.wait("gpsimd", CCS, cc1v)
        dump_dram(cc1_out.ap(), 4 * E, T, BF16)
        return _finish(nc, b, (LD, GD, SD, CCS, PE, SC, VE), in_maps)

    qk_done = 0
    v_done = 0
    xg_free = 0
    lane_i = 0
    for rr in range(4):
        b.wait("sync", CCS, cc1v)
        b.wait("sync", PE, xg_free)
        for ec in range(NCH):
            LD.bump(16)
            b.emit("sync", lambda e, d=abf(O_XGA + 2 * 2048 * ec, 2048),
                   s=cc1_out.ap()[1024 * rr + 128 * ec:1024 * rr + 128 * (ec + 1), :]:
                   e.dma_start(out=d, in_=s).then_inc(LD.h, 16))
        xg_ld = LD.n
        if rr == 0:
            b.wait("vector", LD, xg_ld)
            VE.bump()
            b.emit("vector", lambda e: e.memset(
                abf(O_V, 64 * 264).rearrange("p (k c) -> p k c", c=66)[:, :, 64:65],
                1.0).then_inc(VE.h, 1))
            if debug_stage == "vones":
                dump_sbuf(abf(O_V, 64 * 264), 128, 64 * 264, BF16)
                return _finish(nc, b, (LD, GD, SD, CCS, PE, SC, VE), in_maps)
        b.wait("tensor", LD, xg_ld)
        b.wait("tensor", GD, init_gd)
        for kt in range(16):
            ln_t, ln_c = lane_defs[lane_i % 3]
            lane_i += 1
            b.waits("tensor", psum_w_deps(ln_t, ln_c, 256))
            pe_v = PE.bump()
            for ec in range(NCH):
                last = (ec == NCH - 1)
                b.emit("tensor", lambda e, o=ln_t[:, ln_c:ln_c + 256],
                       l=abf(O_XGA + 2 * (2048 * ec + 128 * kt), 128),
                       r=wva_s[:, 256 * ec:256 * (ec + 1)], st=(ec == 0), sp=(ec == NCH - 1), lt=last:
                       e.matmul(o, l, r, start=st, stop=sp).then_inc(PE.h, 1) if lt
                       else e.matmul(o, l, r, start=st, stop=sp))
            b.wait("vector", PE, pe_v)
            ve_v = VE.bump()
            b.emit("vector", lambda e,
                   d=abf(O_V + 2 * 264 * (16 * rr + kt), 264).rearrange(
                       "p (h c) -> p h c", c=66)[:, :, 0:64],
                   s=ln_t[:, ln_c:ln_c + 256].rearrange("p (h c) -> p h c", c=64):
                   e.tensor_copy(d, s).then_inc(VE.h, 1))
            psum_consumed(ln_t, ln_c, 256, VE, ve_v)
            v_done = ve_v
        for (dst_off, w_s, cls, scale) in ((O_QG, wqa_s, 0, 0.125), (O_QL, wqa_s, 1, 0.125),
                                           (O_KG, wka_s, 0, 1.0), (O_KL, wka_s, 1, 1.0)):
            for cb in range(4):
                ln_t, ln_c = lane_defs[lane_i % 3]
                lane_i += 1
                b.waits("tensor", psum_w_deps(ln_t, ln_c, 512))
                pe_v = PE.bump()
                for ec in range(NCH):
                    last = (ec == NCH - 1)
                    b.emit("tensor", lambda e, o=ln_t[:, ln_c:ln_c + 512],
                           l=w_s[:, 256 * ec + 128 * cls:256 * ec + 128 * cls + 128],
                           r=abf(O_XGA + 2 * (2048 * ec + 512 * cb), 512),
                           st=(ec == 0), sp=(ec == NCH - 1), lt=last:
                           e.matmul(o, l, r, start=st, stop=sp).then_inc(PE.h, 1) if lt
                           else e.matmul(o, l, r, start=st, stop=sp))
                b.wait("scalar", PE, pe_v)
                sc_v = SC.bump()
                b.emit("scalar", lambda e, d=abf(dst_off + 2 * (2048 * rr + 512 * cb), 512),
                       s=ln_t[:, ln_c:ln_c + 512], sc=scale:
                       e.activation(d, s, AF.Copy, scale=sc).then_inc(SC.h, 1))
                psum_consumed(ln_t, ln_c, 512, SC, sc_v)
                qk_done = sc_v
        xg_free = PE.n

    if debug_stage == "vtile":
        b.wait("scalar", VE, VE.n)
        b.wait("scalar", SC, SC.n)
        b.wait("scalar", PE, PE.n)
        dump_sbuf(abf(O_V, 64 * 264), 128, 64 * 264, BF16)
        return _finish(nc, b, (LD, GD, SD, CCS, PE, SC, VE), in_maps)

    emask_done = [0, 0]
    epar_reader = [0, 0]
    spar_reader = [0, 0]
    kbi = 0
    attn_norm_done = 0
    attn_sc_done = 0
    srow_free = [0, 0]

    def emit_vmm(rec, is_first, is_last, cls, slots):
        kb, par, sc_exp, ve_m = rec
        et = e_t[par]
        b.wait("tensor", SC, sc_exp)
        if ve_m is not None:
            b.wait("tensor", VE, ve_m)
        if is_first:
            for pr in range(2):
                b.waits("tensor", psum_w_deps(ou, 512 * slots[pr], 512))
        pe_v = PE.bump()
        for pr in range(2):
            sl = ou[:, 512 * slots[pr]:512 * slots[pr] + 512]
            b.emit("tensor", lambda e, o=sl,
                   l=abf(O_V + 2 * (264 * kb + 66 * (2 * cls + pr)), 65),
                   r=et[:, 512 * pr:512 * pr + 512], st=is_first, sp=is_last, lt=(pr == 1):
                   e.matmul(o[0:65, :], l, r, start=st, stop=sp).then_inc(PE.h, 1) if lt
                   else e.matmul(o[0:65, :], l, r, start=st, stop=sp))
        epar_reader[par] = pe_v

    b.wait("tensor", SC, qk_done)
    b.wait("tensor", VE, v_done)
    b.wait("vector", LD, init_ld)
    for cls in range(2):
        q_off = O_QG if cls == 0 else O_QL
        k_off = O_KG if cls == 0 else O_KL
        at_off = O_XGA + 2 * 8192 * cls
        for qb in range(16):
            kbs = (list(range(0, 4 * qb + 4)) if cls == 0
                   else list(range(max(0, 4 * qb - 2), 4 * qb + 4)))
            t_i = 16 * cls + qb
            slots = [(2 * t_i) % 3, (2 * t_i + 1) % 3]
            info = []
            for i, kb in enumerate(kbs):
                par = kbi % 2
                kbi += 1
                st_t = s0 if par == 0 else s1
                et = e_t[par]
                if cls == 0:
                    d = kb - 4 * qb
                    mid = d if d >= 0 else None
                else:
                    mid = 4 + (kb - 4 * qb) + 2
                b.wait("tensor", SC, spar_reader[par])
                b.waits("tensor", psum_w_deps(st_t, 0, 1024))
                b.emit("tensor", lambda e, o=st_t[:, 0:512], l=abf(k_off + 2 * 128 * kb, 128),
                       r=abf(q_off + 2 * 512 * qb, 512):
                       e.matmul(o, l[0:64, :], r[0:64, :], start=True, stop=True))
                pe_sc = PE.bump()
                b.emit("tensor", lambda e, o=st_t[:, 512:1024], l=abf(k_off + 2 * 128 * kb, 128),
                       r=abf(q_off + 2 * 512 * qb, 512):
                       e.matmul(o, l[64:128, :], r[64:128, :], start=True, stop=True).then_inc(PE.h, 1))
                b.wait("scalar", PE, pe_sc)
                b.wait("scalar", PE, epar_reader[par])
                b.wait("scalar", VE, emask_done[par])
                sc_exp = SC.bump()
                b.emit("scalar", lambda e, o=et, s=st_t:
                       e.activation(o, s, AF.Exp, bias=ebias_s).then_inc(SC.h, 1))
                spar_reader[par] = sc_exp
                ve_m = None
                if mid is not None:
                    b.wait("vector", SC, sc_exp)
                    b.wait("vector", PE, epar_reader[par])
                    b.emit("vector", lambda e, o=et[:, 0:512],
                           m=masks_s[:, 512 * mid:512 * mid + 512]:
                           e.tensor_tensor(o, o, m, ALU.mult))
                    ve_m = VE.bump()
                    b.emit("vector", lambda e, o=et[:, 512:1024],
                           m=masks_s[:, 512 * mid:512 * mid + 512]:
                           e.tensor_tensor(o, o, m, ALU.mult).then_inc(VE.h, 1))
                    emask_done[par] = ve_m
                info.append((kb, par, sc_exp, ve_m))
                if i > 0:
                    emit_vmm(info[i - 1], i - 1 == 0, False, cls, slots)
            emit_vmm(info[-1], len(kbs) == 1, True, cls, slots)
            if len(kbs) > 1:
                pass
            for pr in range(2):
                sl = ou[:, 512 * slots[pr]:512 * slots[pr] + 512]
                b.wait("scalar", PE, PE.n)
                sc_n = SC.bump()
                b.emit("scalar", lambda e, o=abf(at_off + 2 * 512 * qb, 512), s=sl, pr_=pr:
                       e.activation(o[64 * pr_:64 * pr_ + 64, :], s[0:64, :],
                                    AF.Copy).then_inc(SC.h, 1))
                spar = (2 * (16 * cls + qb) + pr) % 2
                b.wait("vector", PE, PE.n)
                b.wait("vector", SD, srow_free[spar])
                ve_n = VE.bump()
                b.emit("vector", lambda e, s=sl, sp=spar:
                       e.tensor_copy(srow[0:1, 512 * sp:512 * sp + 512],
                                     s[64:65, :]).then_inc(VE.h, 1))
                b.wait("scalar", VE, ve_n)
                SD.bump(16)
                b.emit("scalar", lambda e, sp=spar, h=2 * cls + pr, q_=qb:
                       e.dma_start(out=cc2s_in.ap()[h:h + 1, 512 * q_:512 * q_ + 512],
                                   in_=srow[0:1, 512 * sp:512 * sp + 512]).then_inc(SD.h, 16))
                srow_free[spar] = SD.n
                psum_consumed(ou, 512 * slots[pr], 512, SC, sc_n)
                psum_consumed_add(ou, 512 * slots[pr], 512, VE, ve_n)
                attn_norm_done = max(attn_norm_done, ve_n)
                attn_sc_done = sc_n

    psum_consumed(s0, 0, 1024, SC, spar_reader[0])
    psum_consumed(s1, 0, 1024, SC, spar_reader[1])

    b.wait("scalar", SC, attn_sc_done)
    b.wait("scalar", VE, attn_norm_done)
    for cls in range(2):
        SD.bump(16)
        b.emit("scalar", lambda e, d=cc2_in.ap()[128 * cls:128 * (cls + 1), :],
               s=abf(O_XGA + 2 * 8192 * cls, 8192): e.dma_start(out=d, in_=s).then_inc(SD.h, 16))
    b.wait("gpsimd", SD, SD.n)
    CCS.bump()
    b.emit("gpsimd", lambda e: e.collective_compute(
        "AllGather", ALU.bypass, replica_groups=groups,
        ins=[cc2s_in.ap().opt()], outs=[cc2s_out.ap().opt()]).then_inc(CCS.h, 1))
    cc2v = CCS.bump()
    b.emit("gpsimd", lambda e: e.collective_compute(
        "AllGather", ALU.bypass, replica_groups=groups,
        ins=[cc2_in.ap().opt()], outs=[cc2_out.ap().opt()]).then_inc(CCS.h, 1))

    if debug_stage == "attn":
        b.wait("gpsimd", CCS, cc2v)
        dump_dram(cc2_out.ap(), E, S, BF16)
        return _finish(nc, b, (LD, GD, SD, CCS, PE, SC, VE), in_maps)

    attn_end_pe = PE.n
    b.wait("gpsimd", PE, attn_end_pe)
    b.wait("gpsimd", VE, attn_norm_done)
    for (off, src) in ((O_WR, wrT), (O_WK, wkT), (O_WV, wvT), (O_WO, woT)):
        GD.bump(16)
        b.emit("gpsimd", lambda e, d=abf(off, NCH * 1024).rearrange("p (c j) -> p c j", c=NCH),
               s=src.rearrange("(c p) j -> p c j", p=128):
               e.dma_start(out=d, in_=s).then_inc(GD.h, 16))
    w2_ld = GD.n

    sw_state = {}

    def stage_switch(e, p):
        if "idx" not in sw_state:
            pid = e.partition_id()
            rreg = e.alloc_register("rksw")
            e.reg_mod(rreg, pid, 4)
            sw_state["idx"] = e.snap(rreg, min_val=0, max_val=3)
        idx = sw_state["idx"]
        cc2o = cc2_out.ap()
        cc2s = cc2s_out.ap()
        for i in e.Switch(idx, 4):
            if p == 0:
                for ec in range(NCH):
                    e.dma_start(out=abf(O_AST + 2 * GW * ec, GW)[:, 1:513],
                                in_=cc2o[128 * ec:128 * (ec + 1), 2048 * i:2048 * i + 512]
                                ).then_inc(LD.h, 16)
                for ec in range(NCH):
                    hc = max(0, 2048 * i - 1)
                    e.dma_start(out=abf(O_AST + 2 * GW * ec, GW)[:, 0:1],
                                in_=cc2o[128 * ec:128 * (ec + 1), hc:hc + 1],
                                allow_slow_non_contiguous=True).then_inc(LD.h, 16)
                e.dma_start(out=sums_st[:, 1:513],
                            in_=cc2s[:, 2048 * i:2048 * i + 512]).then_inc(LD.h, 16)
                hc = max(0, 2048 * i - 1)
                e.dma_start(out=sums_st[:, 0:1], in_=cc2s[:, hc:hc + 1],
                            allow_slow_non_contiguous=True).then_inc(LD.h, 16)
            else:
                c0 = 512 * p - 1
                for ec in range(NCH):
                    e.dma_start(out=abf(O_AST + 2 * GW * ec, GW)[:, 0:513],
                                in_=cc2o[128 * ec:128 * (ec + 1), 2048 * i + c0:2048 * i + c0 + 513]
                                ).then_inc(LD.h, 16)
                e.dma_start(out=sums_st[:, 0:513],
                            in_=cc2s[:, 2048 * i + c0:2048 * i + c0 + 513]).then_inc(LD.h, 16)
                e.dma_start(out=sums_st[:, 516:517],
                            in_=cc2s[:, 0:1],
                            allow_slow_non_contiguous=True).then_inc(LD.h, 16)

    wo_done_panel = 0
    x2stt_done_panel = 0
    norm_bc_used = [0]
    for p in range(NPAN):
        W2 = 513
        b.wait("sync", CCS, cc2v)
        b.wait("sync", PE, max(attn_end_pe, wo_done_panel))
        for _ in range((16 if p == 0 else 8) + 2):
            LD.bump(16)
        b.emit("sync", lambda e, p=p: stage_switch(e, p))
        ast_ld = LD.n
        b.wait("sync", VE, x2stt_done_panel)
        for ec in range(NCH):
            LD.bump(16)
            b.emit("sync", lambda e, d=af32(O_X1B + 4 * GW * ec, GW)[:, 0:W2],
                   s=x1sp[128 * ec:128 * (ec + 1), 512 * p:512 * p + W2]:
                   e.dma_start(out=d, in_=s).then_inc(LD.h, 16))
        x1b_ld = LD.n

        b.wait("vector", LD, ast_ld)
        b.wait("vector", PE, norm_bc_used[0])
        ve_rcp = VE.bump()
        b.emit("vector", lambda e: e.reciprocal(sums_st[:, 0:513],
                                                sums_st[:, 0:513]).then_inc(VE.h, 1))
        norm_vals = []
        for ac in range(NCH):
            b.wait("tensor", VE, ve_rcp)
            b.waits("tensor", psum_w_deps(bcp, 0, 512))
            for (cs, cw) in ((0, 1), (1, 512)):
                if cs > 0:
                    b.wait("tensor", VE, norm_vals[-1])
                pe_bc = PE.bump()
                b.emit("tensor", lambda e, l=ind_s[:, 128 * ac:128 * (ac + 1)],
                       r=sums_st[:, cs:cs + cw], o=bcp[:, 0:cw]:
                       e.matmul(o, l, r, start=True, stop=True).then_inc(PE.h, 1))
                b.wait("vector", PE, pe_bc)
                ve_n = VE.bump()
                b.emit("vector", lambda e, o=abf(O_AST + 2 * GW * ac, GW)[:, cs:cs + cw],
                       m=bcp[:, 0:cw]:
                       e.tensor_tensor(o, o, m, ALU.mult).then_inc(VE.h, 1))
                norm_vals.append(ve_n)
                psum_consumed(bcp, 0, cw, VE, ve_n)
        norm_done = max(norm_vals)
        norm_bc_used[0] = PE.n

        x2_vals = []
        b.wait("tensor", GD, w2_ld)
        b.wait("tensor", LD, max(ast_ld, x1b_ld))
        b.wait("tensor", VE, norm_done)
        for dc in range(NCH):
            ln_t, ln_c = lane_defs[dc % 3]
            b.waits("tensor", psum_w_deps(ln_t, ln_c, W2))
            pe_v = PE.bump()
            wo_blocks = col_blocks(W2)
            for ac in range(NCH):
                lhs = abf(O_WO + 2 * (1024 * ac + 128 * dc), 128)
                rhs = abf(O_AST + 2 * GW * ac, GW)
                for bi, (cs, cw) in enumerate(wo_blocks):
                    last = (ac == NCH - 1 and bi == len(wo_blocks) - 1)
                    b.emit("tensor", lambda e, o=ln_t[:, ln_c + cs:ln_c + cs + cw], l=lhs,
                           r=rhs[:, cs:cs + cw], st=(ac == 0), sp=(ac == NCH - 1), lt=last:
                           e.matmul(o, l, r, start=st, stop=sp).then_inc(PE.h, 1) if lt
                           else e.matmul(o, l, r, start=st, stop=sp))
            b.wait("vector", PE, pe_v)
            ve_v = VE.bump()
            b.emit("vector", lambda e, o=af32(O_X2 + 4 * GW * dc, W2), i=ln_t[:, ln_c:ln_c + W2],
                   bb=bo_s[:, dc:dc + 1], x1=af32(O_X1B + 4 * GW * dc, W2):
                   e.scalar_tensor_tensor(o, i, bb, x1, ALU.add, ALU.add).then_inc(VE.h, 1))
            psum_consumed(ln_t, ln_c, W2, VE, ve_v)
            if p == 0:
                ve_v = VE.bump()
                b.emit("vector", lambda e, o=af32(O_X2 + 4 * GW * dc, 1):
                       e.tensor_scalar_mul(o, o, hm_s).then_inc(VE.h, 1))
            x2_vals.append(VE.n)
        wo_done_panel = PE.n
        x2stt_done_panel = max(x2_vals)

        ln_pass(lambda ec: af32(O_X2 + 4 * GW * ec, GW)[:, 0:W2], W2,
                lambda ec: abf(O_XN3 + 2 * GW * ec, W2),
                lambda ec, xv=list(x2_vals): [(VE, xv[ec])])

        mix_vals = []
        for ec in range(NCH):
            xn3 = abf(O_XN3 + 2 * GW * ec, GW)
            dt_ = abf(O_D2 + 2 * GW * ec, 512)
            b.emit("vector", lambda e, o=dt_, x=xn3:
                   e.tensor_tensor(o, x[:, 1:513], x[:, 0:512], ALU.subtract))
            b.emit("vector", lambda e, o=abf(O_XR2 + 2 * GW * ec, 512), d=dt_,
                   m=mur_s[:, ec:ec + 1], x=xn3:
                   e.scalar_tensor_tensor(o, d, m, x[:, 0:512], ALU.mult, ALU.add))
            vv = VE.bump()
            b.emit("vector", lambda e, o=abf(O_XK2 + 2 * GW * ec, 512), d=dt_,
                   m=muk_s[:, ec:ec + 1], x=xn3:
                   e.scalar_tensor_tensor(o, d, m, x[:, 0:512], ALU.mult, ALU.add).then_inc(VE.h, 1))
            mix_vals.append(vv)

        x3_vals = ffn_block(lambda ec: abf(O_XR2 + 2 * GW * ec, 512),
                            lambda ec: abf(O_XK2 + 2 * GW * ec, 512),
                            512,
                            lambda ec: af32(O_X2 + 4 * GW * ec, GW), 1,
                            lambda dc: af32(O_X2 + 4 * GW * dc, GW)[:, 1:513],
                            O_WR, O_WK, O_WV, O_R2, O_D2, max(mix_vals))
        x2stt_done_panel = max(x3_vals)

        for ec in range(NCH):
            b.wait("scalar", VE, x3_vals[ec])
            SD.bump(16)
            b.emit("scalar", lambda e, d=y[128 * ec:128 * (ec + 1), 512 * p:512 * p + 512],
                   s=af32(O_X2 + 4 * GW * ec, GW)[:, 1:513]:
                   e.dma_start(out=d, in_=s).then_inc(SD.h, 16))

    b.wait("gpsimd", SD, SD.n)
    b.wait("gpsimd", LD, LD.n)
    b.wait("gpsimd", GD, GD.n)
    return _finish(nc, b, (LD, GD, SD, CCS, PE, SC, VE), in_maps)


def _finish(nc, b, ctrs, in_maps):
    LD, GD, SD, CCS, PE, SC, VE = ctrs
    with ExitStack() as ctx:
        for c, name in zip(ctrs, ("LD", "GD", "SD", "CCS", "PE", "SC", "VE")):
            c.h = ctx.enter_context(nc.semaphore(name))
        with nc.Block() as block:
            if b.q["sync"]:
                @block.sync
                def _(e):
                    for f in b.q["sync"]:
                        f(e)
            if b.q["gpsimd"]:
                @block.gpsimd
                def _(e):
                    for f in b.q["gpsimd"]:
                        f(e)
            if b.q["vector"]:
                @block.vector
                def _(e):
                    for f in b.q["vector"]:
                        f(e)
            if b.q["scalar"]:
                @block.scalar
                def _(e):
                    for f in b.q["scalar"]:
                        f(e)
            if b.q["tensor"]:
                @block.tensor
                def _(e):
                    for f in b.q["tensor"]:
                        f(e)
        if in_maps is None:
            n = sum(len(v) for v in b.q.values())
            print(f"[dry-run] instructions: {n} (sync={len(b.q['sync'])} "
                  f"gp={len(b.q['gpsimd'])} ve={len(b.q['vector'])} "
                  f"sc={len(b.q['scalar'])} pe={len(b.q['tensor'])}) "
                  f"n_bass={nc.n_instructions()}")
            return None
        res = run_bass_kernel_spmd(nc, in_maps, core_ids=list(range(8)))
    return res.results


def host_inputs(inputs):
    x = np.ascontiguousarray(np.asarray(inputs["L0_attn_hidden_states"]), dtype=np.float32)
    Wq = np.asarray(inputs["L0_attn_Wq"])
    Wk = np.asarray(inputs["L0_attn_Wk"])
    Wv = np.asarray(inputs["L0_attn_Wv"])
    Wo = np.asarray(inputs["L0_attn_Wo"])
    bo = np.asarray(inputs["L0_attn_bo"])
    Wr = np.asarray(inputs["L0_ffn_W_r"])
    Wkf = np.asarray(inputs["L0_ffn_W_k"])
    Wvf = np.asarray(inputs["L0_ffn_W_v"])
    mur = np.asarray(inputs["L0_ffn_mu_r"])
    muk = np.asarray(inputs["L0_ffn_mu_k"])

    perm = np.concatenate(
        [np.arange(64 * h, 64 * h + 64) for j in range(4)
         for h in (2 * j, 2 * j + 1, 8 + 2 * j, 8 + 2 * j + 1)])
    lay = lambda v: np.ascontiguousarray(v.reshape(NCH, 128).T, dtype=np.float32)
    base = {
        "wrT": np.ascontiguousarray(Wr.T, dtype=np.float32),
        "wkT": np.ascontiguousarray(Wkf.T, dtype=np.float32),
        "wvT": np.ascontiguousarray(Wvf.T, dtype=np.float32),
        "woT": np.ascontiguousarray(Wo.T[perm, :], dtype=np.float32),
        "bo_l": lay(bo), "mur_l": lay(mur), "muk_l": lay(muk),
    }
    in_maps = []
    for c in range(8):
        bidx, r = c // 4, c % 4
        t0 = 2048 * r
        xs = np.zeros((T + 2, E), dtype=np.float32)
        lo = max(0, t0 - 2)
        xs[2 - (t0 - lo):] = x[bidx, lo: t0 + T]
        heads = [2 * r, 2 * r + 1, 8 + 2 * r, 8 + 2 * r + 1]
        hsl = np.concatenate([np.arange(64 * h, 64 * h + 64) for h in heads])
        m = dict(base)
        m["xT"] = np.ascontiguousarray(xs.T)
        m["wqaT"] = np.ascontiguousarray(Wq[hsl, :].T, dtype=np.float32)
        m["wkaT"] = np.ascontiguousarray(Wk[hsl, :].T, dtype=np.float32)
        m["wvaT"] = np.ascontiguousarray(Wv[hsl, :].T, dtype=np.float32)
        m["hm_in"] = np.full((128, 1), 0.0 if r == 0 else 1.0, dtype=np.float32)
        in_maps.append(m)
    return in_maps


def kernel(**inputs):
    in_maps = host_inputs(inputs)
    results = build_and_run("full", in_maps)
    out = np.empty((2, S, E), dtype=np.float32)
    for c in range(8):
        bidx, r = c // 4, c % 4
        out[bidx, 2048 * r: 2048 * r + T] = results[c]["y"].T
    return out


# version 07
# speedup vs baseline: 1.3115x; 1.1163x over previous
import numpy as np
import ml_dtypes
from contextlib import ExitStack

import concourse.bass as bass
import concourse.mybir as mybir
from concourse.bass_utils import run_bass_kernel_spmd

F32 = mybir.dt.float32
BF16 = mybir.dt.bfloat16
AF = mybir.ActivationFunctionType
ALU = mybir.AluOpType

E = 1024
NCH = 8
T = 2048
S = 8192
NPAN = 4
GW = 516
EXP_BIAS = -20.0
EPS = 1e-5

ENGS = ("sync", "gpsimd", "vector", "scalar", "tensor")


class Ctr:
    def __init__(self):
        self.h = None
        self.n = 0

    def bump(self, k=1):
        self.n += k
        return self.n


class Builder:
    def __init__(self):
        self.q = {e: [] for e in ENGS}
        self.waited = {e: {} for e in ENGS}

    def emit(self, eng, fn):
        self.q[eng].append(fn)

    def wait(self, eng, ctr, val):
        if not val or val <= 0:
            return
        if self.waited[eng].get(id(ctr), 0) >= val:
            return
        self.waited[eng][id(ctr)] = val
        self.emit(eng, lambda e, c=ctr, v=val: e.wait_ge(c.h, v))

    def waits(self, eng, deps):
        for (c, v) in deps:
            self.wait(eng, c, v)


def col_blocks(width, maxn=512):
    out, c = [], 0
    while c < width:
        w = min(maxn, width - c)
        out.append((c, w))
        c += w
    return out


def build_and_run(debug_stage, in_maps):
    nc = bass.Bass("TRN2", target_bir_lowering=False)

    xT = nc.dram_tensor("xT", [E, T + 2], F32, kind="ExternalInput").ap()
    wrT = nc.dram_tensor("wrT", [E, E], F32, kind="ExternalInput").ap()
    wkT = nc.dram_tensor("wkT", [E, E], F32, kind="ExternalInput").ap()
    wvT = nc.dram_tensor("wvT", [E, E], F32, kind="ExternalInput").ap()
    wqaT = nc.dram_tensor("wqaT", [E, 256], F32, kind="ExternalInput").ap()
    wkaT = nc.dram_tensor("wkaT", [E, 256], F32, kind="ExternalInput").ap()
    wvaT = nc.dram_tensor("wvaT", [E, 256], F32, kind="ExternalInput").ap()
    woT = nc.dram_tensor("woT", [E, E], F32, kind="ExternalInput").ap()
    bo_l = nc.dram_tensor("bo_l", [128, NCH], F32, kind="ExternalInput").ap()
    mur_l = nc.dram_tensor("mur_l", [128, NCH], F32, kind="ExternalInput").ap()
    muk_l = nc.dram_tensor("muk_l", [128, NCH], F32, kind="ExternalInput").ap()
    hm_in = nc.dram_tensor("hm_in", [128, 1], F32, kind="ExternalInput").ap()
    y = nc.dram_tensor("y", [E, T], F32, kind="ExternalOutput").ap()

    x1sp = nc.dram_tensor("x1sp", [E, T + 1], F32, kind="Internal").ap()
    cc1_in = [nc.dram_tensor(f"cc1_in{p}", [E, 512], BF16, kind="Internal")
              for p in range(NPAN)]
    cc1_out = [nc.dram_tensor(f"cc1_out{p}", [4 * E, 512], BF16, kind="Internal")
               for p in range(NPAN)]
    cc2_in = [nc.dram_tensor(f"cc2_in{g}", [256, T], BF16, kind="Internal")
              for g in range(4)]
    cc2_out = [nc.dram_tensor(f"cc2_out{g}", [E, T], BF16, kind="Internal")
               for g in range(4)]
    cc2s_in = [nc.dram_tensor(f"cc2s_in{g}", [4, T], F32, kind="Internal")
               for g in range(4)]
    cc2s_out = [nc.dram_tensor(f"cc2s_out{g}", [16, T], F32, kind="Internal")
                for g in range(4)]
    ccsen_in = nc.dram_tensor("ccsen_in", [16, 16], F32, kind="Internal")
    ccsen_out = nc.dram_tensor("ccsen_out", [64, 16], F32, kind="Internal")
    groups = [[0, 1, 2, 3], [4, 5, 6, 7]]


    ki = np.arange(128)[:, None]
    qi = np.arange(512)[None, :]
    masks_np = [((ki + 128 * d) <= qi) for d in range(4)]
    for d in range(-2, 4):
        kk = ki + 128 * d
        masks_np.append((kk <= qi) & (kk >= qi - 256))
    mask_t = [nc.inline_tensor(np.ascontiguousarray(m).astype(ml_dtypes.bfloat16),
                               name=f"mask{i}") for i, m in enumerate(masks_np)]
    ind_np = np.zeros((16, 8 * 128), dtype=np.float32)
    for ac in range(8):
        hd_top = 4 * (ac // 2) + 2 * (ac % 2)
        ind_np[hd_top, 128 * ac:128 * ac + 64] = 1.0
        ind_np[hd_top + 1, 128 * ac + 64:128 * ac + 128] = 1.0
    ind_t = nc.inline_tensor(ind_np, name="ind_norm")

    masks_s = nc.alloc_sbuf_tensor("masks_s", [128, 10 * 512], BF16).ap()
    e_t = [nc.alloc_sbuf_tensor(f"e{i}", [128, 1024], BF16).ap() for i in range(2)]
    wqa_s = nc.alloc_sbuf_tensor("wqa_s", [128, NCH * 256], BF16).ap()
    wka_s = nc.alloc_sbuf_tensor("wka_s", [128, NCH * 256], BF16).ap()
    wva_s = nc.alloc_sbuf_tensor("wva_s", [128, NCH * 256], BF16).ap()
    mur_s = nc.alloc_sbuf_tensor("mur_s", [128, NCH], F32).ap()
    muk_s = nc.alloc_sbuf_tensor("muk_s", [128, NCH], F32).ap()
    bo_s = nc.alloc_sbuf_tensor("bo_s", [128, NCH], F32).ap()
    hm_s = nc.alloc_sbuf_tensor("hm_s", [128, 1], F32).ap()
    ones_st = nc.alloc_sbuf_tensor("ones_st", [128, 1], BF16).ap()
    ones_bf = nc.alloc_sbuf_tensor("ones_bf", [1, 128], F32).ap()
    ebias_s = nc.alloc_sbuf_tensor("ebias_s", [128, 1], F32).ap()
    mean_r = nc.alloc_sbuf_tensor("mean_r", [1, GW], F32).ap()
    msq_r = nc.alloc_sbuf_tensor("msq_r", [1, GW], F32).ap()
    rstd_r = nc.alloc_sbuf_tensor("rstd_r", [1, GW], F32).ap()
    a_r = nc.alloc_sbuf_tensor("a_r", [1, GW], F32).ap()
    ind_s = nc.alloc_sbuf_tensor("ind_s", [16, 8 * 128], F32).ap()
    sums_st = nc.alloc_sbuf_tensor("sums_st", [16, 520], F32).ap()
    srow = nc.alloc_sbuf_tensor("srow", [1, 2 * 512], F32).ap()
    rstd_b = nc.alloc_sbuf_tensor("rstd_b", [128, GW], F32).ap()
    a_b = nc.alloc_sbuf_tensor("a_b", [128, GW], F32).ap()
    kf32 = nc.alloc_sbuf_tensor("kf32", [128, GW], F32).ap()
    kf32b = nc.alloc_sbuf_tensor("kf32b", [128, GW], F32).ap()
    tmpf = nc.alloc_sbuf_tensor("tmpf", [128, GW], F32).ap()
    xbf_c = nc.alloc_sbuf_tensor("xbf_c", [128, 2 * GW], BF16).ap()
    xsq_c = nc.alloc_sbuf_tensor("xsq_c", [128, 2 * GW], BF16).ap()
    xn2pan = nc.alloc_sbuf_tensor("xn2pan", [128, 2 * 512], BF16).ap()

    AR = 139840
    arena = nc.alloc_sbuf_tensor("arena", [128, AR // 2], BF16).ap()

    def abf(off, cols):
        assert off % 2 == 0 and off + 2 * cols <= AR, (off, cols)
        return arena[:, off // 2: off // 2 + cols]

    def af32(off, cols):
        assert off % 4 == 0 and off + 4 * cols <= AR, (off, cols)
        return arena[:, off // 2: off // 2 + 2 * cols].bitcast(F32)

    O_WR, O_WK, O_WV, O_WO = 0, 16384, 32768, 49152
    O_XB, O_XN1, O_D, O_XR, O_XK = 65536, 82176, 90432, 98688, 106944
    O_R = 115200
    O_AST, O_X1B, O_X2, O_XN3 = 65536, 73792, 90304, 106816
    O_D2, O_XR2, O_XK2 = 115072, 123328, 131584
    O_R2 = O_X1B
    O_QG, O_QL, O_KG, O_KL, O_V, O_XGA = 0, 16384, 32768, 49152, 65536, 99328

    s0 = nc.alloc_psum_tensor("S0", [128, 1024], F32).ap()
    s1 = nc.alloc_psum_tensor("S1", [128, 1024], F32).ap()
    ou = nc.alloc_psum_tensor("OU", [128, 1536], F32).ap()
    bcp = nc.alloc_psum_tensor("BCp", [128, 512], F32).ap()

    def banks(t, cs, cw):
        base = {id(s0): 0, id(s1): 2, id(ou): 4, id(bcp): 7}[id(t)]
        return set(range(base + cs // 512, base + (cs + cw - 1) // 512 + 1))

    bank_cons = {i: [] for i in range(8)}

    b = Builder()
    LD, GD, SD, CCS, PE, SC, VE = (Ctr() for _ in range(7))

    def psum_w_deps(t, cs, cw):
        out = []
        for bk in banks(t, cs, cw):
            out.extend(bank_cons[bk])
        return out

    def psum_consumed(t, cs, cw, ctr, val):
        for bk in banks(t, cs, cw):
            bank_cons[bk] = [(ctr, val)]

    def psum_consumed_add(t, cs, cw, ctr, val):
        for bk in banks(t, cs, cw):
            bank_cons[bk].append((ctr, val))

    lane_defs = [(s0, 0), (s1, 0), (ou, 0)]
    ln_scr_free = [0, 0]
    dbg_hook = [0]
    ag1_v = []

    def dump_sbuf(src_ap, rows, cols, dt):
        dbg_t = nc.dram_tensor("dbg", [rows, cols], dt, kind="ExternalOutput").ap()
        b.wait("scalar", VE, VE.n)
        b.wait("scalar", PE, PE.n)
        b.wait("scalar", SC, SC.n)
        SD.bump(16)
        b.emit("scalar", lambda e, d=dbg_t, s=src_ap: e.dma_start(out=d, in_=s).then_inc(SD.h, 16))
        b.wait("gpsimd", SD, SD.n)
        b.wait("gpsimd", LD, LD.n)
        b.wait("gpsimd", GD, GD.n)

    def dump_dram(src_ap, rows, cols, dt):
        dbg_t = nc.dram_tensor("dbg", [rows, cols], dt, kind="ExternalOutput").ap()
        GD.bump(16)
        b.emit("gpsimd", lambda e, d=dbg_t, s=src_ap: e.dma_start(out=d, in_=s).then_inc(GD.h, 16))
        b.wait("gpsimd", SD, SD.n)
        b.wait("gpsimd", LD, LD.n)
        b.wait("gpsimd", GD, GD.n)

    for (dst, src) in ((mur_s, mur_l), (muk_s, muk_l), (bo_s, bo_l), (hm_s, hm_in)):
        LD.bump(16)
        b.emit("sync", lambda e, d=dst, s=src: e.dma_start(out=d, in_=s).then_inc(LD.h, 16))
    for i in range(10):
        LD.bump(16)
        b.emit("sync", lambda e, d=masks_s[:, 512 * i:512 * (i + 1)], s=mask_t[i].ap():
               e.dma_start(out=d, in_=s).then_inc(LD.h, 16))
    LD.bump(16)
    b.emit("sync", lambda e: e.dma_start(out=ind_s, in_=ind_t.ap()).then_inc(LD.h, 16))
    for (dst, src) in ((wqa_s, wqaT), (wka_s, wkaT), (wva_s, wvaT)):
        GD.bump(16)
        b.emit("gpsimd", lambda e, d=dst.rearrange("p (c j) -> p c j", c=NCH),
               s=src.rearrange("(c p) j -> p c j", p=128):
               e.dma_start(out=d, in_=s).then_inc(GD.h, 16))
    for (off, src) in ((O_WR, wrT), (O_WK, wkT), (O_WV, wvT)):
        GD.bump(16)
        b.emit("gpsimd", lambda e, d=abf(off, NCH * 1024).rearrange("p (c j) -> p c j", c=NCH),
               s=src.rearrange("(c p) j -> p c j", p=128):
               e.dma_start(out=d, in_=s).then_inc(GD.h, 16))
    for (ap_, val_) in ((ones_st, 1.0 / 1024.0), (ones_bf, 1.0),
                        (ebias_s, EXP_BIAS)):
        VE.bump()
        b.emit("vector", lambda e, a=ap_, vv=val_: e.memset(a, vv).then_inc(VE.h, 1))
    init_ld, init_gd, init_ve = LD.n, GD.n, VE.n

    def ln_pass(src_of_ec, width, dst_of_ec, src_dep_of_ec,
                dst_dep_of_ec=None, post_ec=None):
        blocks = col_blocks(width)
        for ec in range(NCH):
            par = ec % 2
            src = src_of_ec(ec)
            deps = src_dep_of_ec(ec) if src_dep_of_ec else []
            xb = xbf_c[:, GW * par: GW * par + width]
            xq = xsq_c[:, GW * par: GW * par + width]
            b.waits("scalar", deps)
            b.wait("scalar", PE, ln_scr_free[par])
            sc_v = SC.bump()
            b.emit("scalar", lambda e, o=xb, s=src: e.activation(o, s, AF.Copy).then_inc(SC.h, 1))
            b.waits("vector", deps)
            b.wait("vector", PE, ln_scr_free[par])
            ve_v = VE.bump()
            b.emit("vector", lambda e, o=xq, s=src: e.tensor_tensor(o, s, s, ALU.mult).then_inc(VE.h, 1))
            b.wait("tensor", SC, sc_v)
            b.wait("tensor", VE, ve_v)
            if ec == 0:
                b.waits("tensor", psum_w_deps(s0, 0, width))
                b.waits("tensor", psum_w_deps(s1, 0, width))
            pe_v = PE.bump()
            for bi, (cs, cw) in enumerate(blocks):
                b.emit("tensor", lambda e, o=s0[0:1, cs:cs + cw], r=xb[:, cs:cs + cw],
                       st=(ec == 0), sp=(ec == NCH - 1): e.matmul(o, ones_st, r, start=st, stop=sp))
                last = (bi == len(blocks) - 1)
                b.emit("tensor", lambda e, o=s1[0:1, cs:cs + cw], r=xq[:, cs:cs + cw],
                       st=(ec == 0), sp=(ec == NCH - 1), lt=last:
                       e.matmul(o, ones_st, r, start=st, stop=sp).then_inc(PE.h, 1) if lt
                       else e.matmul(o, ones_st, r, start=st, stop=sp))
            ln_scr_free[par] = pe_v
        b.wait("vector", PE, PE.n)
        w = width
        if debug_stage == "stats" and dbg_hook[0] == 0:
            dbg_hook[0] = 1
            b.emit("vector", lambda e: e.tensor_copy(mean_r[:, 0:w], s0[0:1, 0:w]))
            vv = VE.bump()
            b.emit("vector", lambda e: e.tensor_copy(msq_r[:, 0:w], s1[0:1, 0:w]).then_inc(VE.h, 1))
            b.wait("scalar", VE, vv)
            dump_sbuf(mean_r[:, 0:w], 1, w, F32)
            return "DUMPED"
        b.emit("vector", lambda e: e.tensor_copy(mean_r[:, 0:w], s0[0:1, 0:w]))
        b.emit("vector", lambda e: e.tensor_copy(msq_r[:, 0:w], s1[0:1, 0:w]))
        b.emit("vector", lambda e: e.tensor_tensor(a_r[:, 0:w], mean_r[:, 0:w], mean_r[:, 0:w], ALU.mult))
        b.emit("vector", lambda e: e.tensor_tensor(rstd_r[:, 0:w], msq_r[:, 0:w], a_r[:, 0:w], ALU.subtract))
        ve_rows = VE.bump()
        b.emit("vector", lambda e: e.tensor_scalar_add(rstd_r[:, 0:w], rstd_r[:, 0:w], EPS).then_inc(VE.h, 1))
        b.wait("scalar", VE, ve_rows)
        sc_sq = SC.bump()
        b.emit("scalar", lambda e: e.activation(rstd_r[:, 0:w], rstd_r[:, 0:w], AF.Sqrt).then_inc(SC.h, 1))
        b.wait("vector", SC, sc_sq)
        b.emit("vector", lambda e: e.reciprocal(rstd_r[:, 0:w], rstd_r[:, 0:w]))
        ve_rows2 = VE.bump()
        b.emit("vector", lambda e: e.scalar_tensor_tensor(
            a_r[:, 0:w], mean_r[:, 0:w], -1.0, rstd_r[:, 0:w], ALU.mult, ALU.mult).then_inc(VE.h, 1))
        b.wait("tensor", VE, ve_rows2)
        b.waits("tensor", psum_w_deps(ou, 0, width))
        b.waits("tensor", psum_w_deps(s0, 0, width))
        pe_bc = PE.bump()
        for bi, (cs, cw) in enumerate(blocks):
            b.emit("tensor", lambda e, o=ou[:, cs:cs + cw], r=rstd_r[0:1, cs:cs + cw]:
                   e.matmul(o, ones_bf, r, start=True, stop=True))
            last = (bi == len(blocks) - 1)
            b.emit("tensor", lambda e, o=s0[:, cs:cs + cw], r=a_r[0:1, cs:cs + cw], lt=last:
                   e.matmul(o, ones_bf, r, start=True, stop=True).then_inc(PE.h, 1) if lt
                   else e.matmul(o, ones_bf, r, start=True, stop=True))
        b.wait("scalar", PE, pe_bc)
        b.emit("scalar", lambda e: e.activation(rstd_b[:, 0:w], ou[:, 0:w], AF.Copy))
        sc_bc = SC.bump()
        b.emit("scalar", lambda e: e.activation(a_b[:, 0:w], s0[:, 0:w], AF.Copy).then_inc(SC.h, 1))
        psum_consumed(ou, 0, width, SC, sc_bc)
        psum_consumed(s0, 0, width, SC, sc_bc)
        psum_consumed(s1, 0, width, VE, ve_rows)
        b.wait("vector", SC, sc_bc)
        out_vals = []
        for ec in range(NCH):
            src = src_of_ec(ec)
            dst = dst_of_ec(ec)
            if dst_dep_of_ec:
                b.waits("vector", dst_dep_of_ec(ec))
            b.emit("vector", lambda e, s=src: e.tensor_tensor(
                tmpf[:, 0:w], s[:, 0:w], rstd_b[:, 0:w], ALU.mult))
            vv = VE.bump()
            b.emit("vector", lambda e, d=dst: e.tensor_tensor(
                d, tmpf[:, 0:w], a_b[:, 0:w], ALU.add).then_inc(VE.h, 1))
            out_vals.append(vv)
            if post_ec:
                post_ec(ec, vv)
        return out_vals

    def ffn_block(xr_of_ec, xk_of_ec, width, xres_of_ec, res_off, xout_of_ec,
                  wr_off, wk_off, wv_off, r_off, kact_off, mix_done):
        blocks = col_blocks(width)
        kact_free = [0, 0]
        kact_done = [0] * NCH
        b.wait("tensor", VE, mix_done)
        b.wait("tensor", GD, init_gd)
        for dc in range(NCH):
            for proj in range(2):
                woff = wr_off if proj == 0 else wk_off
                ln_t, ln_c = lane_defs[(2 * dc + proj) % 3]
                b.waits("tensor", psum_w_deps(ln_t, ln_c, width))
                pe_v = PE.bump()
                for ec in range(NCH):
                    lhs = abf(woff + 2 * (1024 * ec + 128 * dc), 128)
                    rhs = (xr_of_ec if proj == 0 else xk_of_ec)(ec)
                    for bi, (cs, cw) in enumerate(blocks):
                        last = (ec == NCH - 1 and bi == len(blocks) - 1)
                        b.emit("tensor", lambda e, o=ln_t[:, ln_c + cs:ln_c + cs + cw], l=lhs,
                               r=rhs[:, cs:cs + cw], st=(ec == 0), sp=(ec == NCH - 1), lt=last:
                               e.matmul(o, l, r, start=st, stop=sp).then_inc(PE.h, 1) if lt
                               else e.matmul(o, l, r, start=st, stop=sp))
                b.wait("scalar", PE, pe_v)
                if proj == 0:
                    sc_v = SC.bump()
                    b.emit("scalar", lambda e, o=af32(r_off + 4 * GW * dc, width),
                           i=ln_t[:, ln_c:ln_c + width]:
                           e.activation(o, i, AF.Sigmoid).then_inc(SC.h, 1))
                    psum_consumed(ln_t, ln_c, width, SC, sc_v)
                else:
                    kp = [kf32, kf32b][dc % 2]
                    b.wait("scalar", VE, kact_free[dc % 2])
                    sc_v = SC.bump()
                    b.emit("scalar", lambda e, o=kp[:, 0:width], i=ln_t[:, ln_c:ln_c + width]:
                           e.activation(o, i, AF.Copy).then_inc(SC.h, 1))
                    psum_consumed(ln_t, ln_c, width, SC, sc_v)
                    b.wait("vector", SC, sc_v)
                    ve_v = VE.bump()
                    b.emit("vector", lambda e, o=abf(kact_off + 2 * GW * dc, width), i=kp[:, 0:width]:
                           e.scalar_tensor_tensor(o, i, 0.0, i, ALU.max, ALU.mult).then_inc(VE.h, 1))
                    kact_free[dc % 2] = ve_v
                    kact_done[dc] = ve_v
        out_vals = []
        b.wait("tensor", VE, max(kact_done))
        for dc in range(NCH):
            ln_t, ln_c = lane_defs[dc % 3]
            b.waits("tensor", psum_w_deps(ln_t, ln_c, width))
            pe_v = PE.bump()
            for ac in range(NCH):
                lhs = abf(wv_off + 2 * (1024 * ac + 128 * dc), 128)
                rhs = abf(kact_off + 2 * GW * ac, width)
                for bi, (cs, cw) in enumerate(blocks):
                    last = (ac == NCH - 1 and bi == len(blocks) - 1)
                    b.emit("tensor", lambda e, o=ln_t[:, ln_c + cs:ln_c + cs + cw], l=lhs,
                           r=rhs[:, cs:cs + cw], st=(ac == 0), sp=(ac == NCH - 1), lt=last:
                           e.matmul(o, l, r, start=st, stop=sp).then_inc(PE.h, 1) if lt
                           else e.matmul(o, l, r, start=st, stop=sp))
            b.wait("vector", PE, pe_v)
            b.emit("vector", lambda e, i=ln_t[:, ln_c:ln_c + width],
                   r=af32(r_off + 4 * GW * dc, width):
                   e.scalar_tensor_tensor(tmpf[:, 0:width], i, 0.5, r, ALU.mult, ALU.mult))
            ve_v = VE.bump()
            b.emit("vector", lambda e, o=xout_of_ec(dc), xs=xres_of_ec(dc):
                   e.tensor_tensor(o, xs[:, res_off:res_off + width], tmpf[:, 0:width],
                                   ALU.add).then_inc(VE.h, 1))
            psum_consumed(ln_t, ln_c, width, VE, ve_v)
            out_vals.append(ve_v)
        return out_vals

    def emit_p1():
        xb_free = [0]
        xb_free_sd = [0]
        xb_free_sc = [0]
        xn2_sd = {}
        b.wait("tensor", VE, init_ve)
        for p in range(NPAN):
            W1, WF = 514, 513

            def xpan(ec):
                return af32(O_XB + 4 * GW * ec, GW)

            b.wait("sync", VE, xb_free[0])
            b.wait("sync", SD, xb_free_sd[0])
            b.wait("sync", SC, xb_free_sc[0])
            for ec in range(NCH):
                LD.bump(16)
                b.emit("sync", lambda e, d=xpan(ec)[:, 0:W1],
                       s=xT[128 * ec:128 * (ec + 1), 512 * p:512 * p + W1]:
                       e.dma_start(out=d, in_=s).then_inc(LD.h, 16))
            ldep = [(LD, LD.n), (VE, init_ve)]

            if debug_stage == "xload" and p == 0:
                b.wait("scalar", LD, LD.n)
                dump_sbuf(af32(O_XB, 8 * GW), 128, 8 * GW, F32)
                return "DUMPED"

            if ln_pass(lambda ec: xpan(ec)[:, 0:W1], W1,
                       lambda ec: abf(O_XN1 + 2 * GW * ec, W1),
                       lambda ec: ldep) == "DUMPED":
                return "DUMPED"

            if debug_stage == "ln1" and p == 0:
                dump_sbuf(abf(O_XN1, 8 * GW), 128, 8 * GW, BF16)
                return "DUMPED"

            mix_vals = []
            for ec in range(NCH):
                xn1 = abf(O_XN1 + 2 * GW * ec, GW)
                dt_ = abf(O_D + 2 * GW * ec, WF)
                b.emit("vector", lambda e, o=dt_, x=xn1:
                       e.tensor_tensor(o, x[:, 1:1 + WF], x[:, 0:WF], ALU.subtract))
                b.emit("vector", lambda e, o=abf(O_XR + 2 * GW * ec, WF), d=dt_,
                       m=mur_s[:, ec:ec + 1], x=xn1:
                       e.scalar_tensor_tensor(o, d, m, x[:, 0:WF], ALU.mult, ALU.add))
                vv = VE.bump()
                b.emit("vector", lambda e, o=abf(O_XK + 2 * GW * ec, WF), d=dt_,
                       m=muk_s[:, ec:ec + 1], x=xn1:
                       e.scalar_tensor_tensor(o, d, m, x[:, 0:WF], ALU.mult, ALU.add).then_inc(VE.h, 1))
                mix_vals.append(vv)

            if debug_stage == "mix" and p == 0:
                b.wait("scalar", VE, VE.n)
                dump_sbuf(abf(O_XR, 8 * GW), 128, 8 * GW, BF16)
                return "DUMPED"

            x1_vals = ffn_block(lambda ec: abf(O_XR + 2 * GW * ec, WF),
                                lambda ec: abf(O_XK + 2 * GW * ec, WF),
                                WF, xpan, 1,
                                lambda dc: xpan(dc)[:, 1:1 + WF],
                                O_WR, O_WK, O_WV, O_R, O_D, max(mix_vals))

            if debug_stage == "ffn1" and p == 0:
                b.wait("scalar", VE, VE.n)
                dump_sbuf(af32(O_XB, 8 * GW), 128, 8 * GW, F32)
                return "DUMPED"

            for ec in range(NCH):
                b.wait("scalar", VE, x1_vals[ec])
                SD.bump(16)
                b.emit("scalar", lambda e, d=x1sp[128 * ec:128 * (ec + 1), 512 * p:512 * p + WF],
                       s=xpan(ec)[:, 1:1 + WF]:
                       e.dma_start(out=d, in_=s).then_inc(SD.h, 16))

            def post_ec(ec, ve_val, p=p):
                par = ec % 2
                b.wait("scalar", VE, ve_val)
                SD.bump(16)
                b.emit("scalar", lambda e,
                       d=cc1_in[p].ap()[128 * ec:128 * (ec + 1), :],
                       s=xn2pan[:, 512 * par:512 * par + 512]:
                       e.dma_start(out=d, in_=s).then_inc(SD.h, 16))
                xn2_sd[(p, ec)] = SD.n

            xn2_vals = ln_pass(
                lambda ec: xpan(ec)[:, 2:514], 512,
                lambda ec: xn2pan[:, 512 * (ec % 2):512 * (ec % 2) + 512],
                lambda ec, x1_vals=x1_vals: [(VE, x1_vals[ec])],
                dst_dep_of_ec=lambda ec, p=p: [(SD, xn2_sd.get((p, ec - 2),
                                                               xn2_sd.get((p - 1, ec + 6), 0)))],
                post_ec=post_ec)
            xb_free[0] = max(xn2_vals)
            xb_free_sd[0] = SD.n
            xb_free_sc[0] = SC.n
            b.wait("gpsimd", SD, SD.n)
            ag1_v.append(CCS.bump())
            b.emit("gpsimd", lambda e, p=p: e.collective_compute(
                "AllGather", ALU.bypass, replica_groups=groups,
                ins=[cc1_in[p].ap().opt()], outs=[cc1_out[p].ap().opt()]).then_inc(CCS.h, 1))

    if emit_p1() == "DUMPED":
        return _finish(nc, b, (LD, GD, SD, CCS, PE, SC, VE), in_maps)

    if debug_stage == "x1":
        b.wait("gpsimd", SD, SD.n)
        dump_dram(x1sp, E, T + 1, F32)
        return _finish(nc, b, (LD, GD, SD, CCS, PE, SC, VE), in_maps)

    if debug_stage == "xn2g":
        b.wait("gpsimd", CCS, ag1_v[-1])
        dump_dram(cc1_out[1].ap(), 4 * E, 512, BF16)
        return _finish(nc, b, (LD, GD, SD, CCS, PE, SC, VE), in_maps)

    qk_done = 0
    v_done = 0
    xg_free = 0
    lane_i = 0
    ag1_sen = [CCS.bump()]
    b.emit("gpsimd", lambda e: e.collective_compute(
        "AllGather", ALU.bypass, replica_groups=groups,
        ins=[ccsen_in.ap().opt()], outs=[ccsen_out.ap().opt()]).then_inc(CCS.h, 1))
    p1_all_ve = VE.n
    p1_all_sd = SD.n
    for rr in range(4):
        b.wait("sync", VE, p1_all_ve)
        b.wait("sync", SD, p1_all_sd)
        b.wait("sync", PE, xg_free)
        for p_ in range(NPAN):
            b.wait("sync", CCS, ag1_sen[0] if p_ == NPAN - 1 else ag1_v[p_ + 1])
            for ec in range(NCH):
                LD.bump(16)
                b.emit("sync", lambda e, d=abf(O_XGA + 2 * (2048 * ec + 512 * p_), 512),
                       s=cc1_out[p_].ap()[1024 * rr + 128 * ec:1024 * rr + 128 * (ec + 1), :]:
                       e.dma_start(out=d, in_=s).then_inc(LD.h, 16))
        xg_ld = LD.n
        if rr == 0:
            b.wait("vector", LD, xg_ld)
            VE.bump()
            b.emit("vector", lambda e: e.memset(
                abf(O_V, 64 * 264).rearrange("p (k c) -> p k c", c=66)[:, :, 64:65],
                1.0).then_inc(VE.h, 1))
            if debug_stage == "vones":
                dump_sbuf(abf(O_V, 64 * 264), 128, 64 * 264, BF16)
                return _finish(nc, b, (LD, GD, SD, CCS, PE, SC, VE), in_maps)
        b.wait("tensor", LD, xg_ld)
        b.wait("tensor", GD, init_gd)
        for kt in range(16):
            ln_t, ln_c = lane_defs[lane_i % 3]
            lane_i += 1
            b.waits("tensor", psum_w_deps(ln_t, ln_c, 256))
            pe_v = PE.bump()
            for ec in range(NCH):
                last = (ec == NCH - 1)
                b.emit("tensor", lambda e, o=ln_t[:, ln_c:ln_c + 256],
                       l=abf(O_XGA + 2 * (2048 * ec + 128 * kt), 128),
                       r=wva_s[:, 256 * ec:256 * (ec + 1)], st=(ec == 0), sp=(ec == NCH - 1), lt=last:
                       e.matmul(o, l, r, start=st, stop=sp).then_inc(PE.h, 1) if lt
                       else e.matmul(o, l, r, start=st, stop=sp))
            b.wait("vector", PE, pe_v)
            ve_v = VE.bump()
            b.emit("vector", lambda e,
                   d=abf(O_V + 2 * 264 * (16 * rr + kt), 264).rearrange(
                       "p (h c) -> p h c", c=66)[:, :, 0:64],
                   s=ln_t[:, ln_c:ln_c + 256].rearrange("p (h c) -> p h c", c=64):
                   e.tensor_copy(d, s).then_inc(VE.h, 1))
            psum_consumed(ln_t, ln_c, 256, VE, ve_v)
            v_done = ve_v
        for (dst_off, w_s, cls, scale) in ((O_QG, wqa_s, 0, 0.125), (O_QL, wqa_s, 1, 0.125),
                                           (O_KG, wka_s, 0, 1.0), (O_KL, wka_s, 1, 1.0)):
            for cb in range(4):
                ln_t, ln_c = lane_defs[lane_i % 3]
                lane_i += 1
                b.waits("tensor", psum_w_deps(ln_t, ln_c, 512))
                pe_v = PE.bump()
                for ec in range(NCH):
                    last = (ec == NCH - 1)
                    b.emit("tensor", lambda e, o=ln_t[:, ln_c:ln_c + 512],
                           l=w_s[:, 256 * ec + 128 * cls:256 * ec + 128 * cls + 128],
                           r=abf(O_XGA + 2 * (2048 * ec + 512 * cb), 512),
                           st=(ec == 0), sp=(ec == NCH - 1), lt=last:
                           e.matmul(o, l, r, start=st, stop=sp).then_inc(PE.h, 1) if lt
                           else e.matmul(o, l, r, start=st, stop=sp))
                b.wait("scalar", PE, pe_v)
                sc_v = SC.bump()
                b.emit("scalar", lambda e, d=abf(dst_off + 2 * (2048 * rr + 512 * cb), 512),
                       s=ln_t[:, ln_c:ln_c + 512], sc=scale:
                       e.activation(d, s, AF.Copy, scale=sc).then_inc(SC.h, 1))
                psum_consumed(ln_t, ln_c, 512, SC, sc_v)
                qk_done = sc_v
        xg_free = PE.n

    if debug_stage == "vtile":
        b.wait("scalar", VE, VE.n)
        b.wait("scalar", SC, SC.n)
        b.wait("scalar", PE, PE.n)
        dump_sbuf(abf(O_V, 64 * 264), 128, 64 * 264, BF16)
        return _finish(nc, b, (LD, GD, SD, CCS, PE, SC, VE), in_maps)

    emask_done = [0, 0]
    epar_reader = [0, 0]
    spar_reader = [0, 0]
    kbi = 0
    attn_norm_done = 0
    attn_sc_done = 0
    srow_free = [0, 0]

    def emit_vmm(rec, is_first, is_last, cls, slots):
        kb, par, sc_exp, ve_m = rec
        et = e_t[par]
        b.wait("tensor", SC, sc_exp)
        if ve_m is not None:
            b.wait("tensor", VE, ve_m)
        if is_first:
            for pr in range(2):
                b.waits("tensor", psum_w_deps(ou, 512 * slots[pr], 512))
        pe_v = PE.bump()
        for pr in range(2):
            sl = ou[:, 512 * slots[pr]:512 * slots[pr] + 512]
            b.emit("tensor", lambda e, o=sl,
                   l=abf(O_V + 2 * (264 * kb + 66 * (2 * cls + pr)), 65),
                   r=et[:, 512 * pr:512 * pr + 512], st=is_first, sp=is_last, lt=(pr == 1):
                   e.matmul(o[0:65, :], l, r, start=st, stop=sp).then_inc(PE.h, 1) if lt
                   else e.matmul(o[0:65, :], l, r, start=st, stop=sp))
        epar_reader[par] = pe_v

    b.wait("tensor", SC, qk_done)
    b.wait("tensor", VE, v_done)
    b.wait("vector", LD, init_ld)
    ag2_v = []
    ag2_sen = [0]
    for qb in range(16):
        for cls in range(2):
            q_off = O_QG if cls == 0 else O_QL
            k_off = O_KG if cls == 0 else O_KL
            at_off = O_XGA + 2 * 8192 * cls
            kbs = (list(range(0, 4 * qb + 4)) if cls == 0
                   else list(range(max(0, 4 * qb - 2), 4 * qb + 4)))
            t_i = 2 * qb + cls
            slots = [(2 * t_i) % 3, (2 * t_i + 1) % 3]
            info = []
            for i, kb in enumerate(kbs):
                par = kbi % 2
                kbi += 1
                st_t = s0 if par == 0 else s1
                et = e_t[par]
                if cls == 0:
                    d = kb - 4 * qb
                    mid = d if d >= 0 else None
                else:
                    mid = 4 + (kb - 4 * qb) + 2
                b.wait("tensor", SC, spar_reader[par])
                b.waits("tensor", psum_w_deps(st_t, 0, 1024))
                b.emit("tensor", lambda e, o=st_t[:, 0:512], l=abf(k_off + 2 * 128 * kb, 128),
                       r=abf(q_off + 2 * 512 * qb, 512):
                       e.matmul(o, l[0:64, :], r[0:64, :], start=True, stop=True))
                pe_sc = PE.bump()
                b.emit("tensor", lambda e, o=st_t[:, 512:1024], l=abf(k_off + 2 * 128 * kb, 128),
                       r=abf(q_off + 2 * 512 * qb, 512):
                       e.matmul(o, l[64:128, :], r[64:128, :], start=True, stop=True).then_inc(PE.h, 1))
                b.wait("scalar", PE, pe_sc)
                b.wait("scalar", PE, epar_reader[par])
                b.wait("scalar", VE, emask_done[par])
                sc_exp = SC.bump()
                b.emit("scalar", lambda e, o=et, s=st_t:
                       e.activation(o, s, AF.Exp, bias=ebias_s).then_inc(SC.h, 1))
                spar_reader[par] = sc_exp
                ve_m = None
                if mid is not None:
                    b.wait("vector", SC, sc_exp)
                    b.wait("vector", PE, epar_reader[par])
                    b.emit("vector", lambda e, o=et[:, 0:512],
                           m=masks_s[:, 512 * mid:512 * mid + 512]:
                           e.tensor_tensor(o, o, m, ALU.mult))
                    ve_m = VE.bump()
                    b.emit("vector", lambda e, o=et[:, 512:1024],
                           m=masks_s[:, 512 * mid:512 * mid + 512]:
                           e.tensor_tensor(o, o, m, ALU.mult).then_inc(VE.h, 1))
                    emask_done[par] = ve_m
                info.append((kb, par, sc_exp, ve_m))
                if i > 0:
                    emit_vmm(info[i - 1], i - 1 == 0, False, cls, slots)
            emit_vmm(info[-1], len(kbs) == 1, True, cls, slots)
            if len(kbs) > 1:
                pass
            for pr in range(2):
                sl = ou[:, 512 * slots[pr]:512 * slots[pr] + 512]
                b.wait("scalar", PE, PE.n)
                sc_n = SC.bump()
                b.emit("scalar", lambda e, o=abf(at_off + 2 * 512 * qb, 512), s=sl, pr_=pr:
                       e.activation(o[64 * pr_:64 * pr_ + 64, :], s[0:64, :],
                                    AF.Copy).then_inc(SC.h, 1))
                spar = (2 * (16 * cls + qb) + pr) % 2
                b.wait("vector", PE, PE.n)
                b.wait("vector", SD, srow_free[spar])
                ve_n = VE.bump()
                b.emit("vector", lambda e, s=sl, sp=spar:
                       e.tensor_copy(srow[0:1, 512 * sp:512 * sp + 512],
                                     s[64:65, :]).then_inc(VE.h, 1))
                b.wait("scalar", VE, ve_n)
                SD.bump(16)
                b.emit("scalar", lambda e, sp=spar, h=2 * cls + pr, q_=qb:
                       e.dma_start(out=cc2s_in[q_ // 4].ap()[h:h + 1,
                                                             512 * (q_ % 4):512 * (q_ % 4) + 512],
                                   in_=srow[0:1, 512 * sp:512 * sp + 512]).then_inc(SD.h, 16))
                srow_free[spar] = SD.n
                psum_consumed(ou, 512 * slots[pr], 512, SC, sc_n)
                psum_consumed_add(ou, 512 * slots[pr], 512, VE, ve_n)
                attn_norm_done = max(attn_norm_done, ve_n)
                attn_sc_done = sc_n
        if qb % 4 == 3:
            g = qb // 4
            b.wait("scalar", SC, attn_sc_done)
            b.wait("scalar", VE, attn_norm_done)
            for cls2 in range(2):
                SD.bump(16)
                b.emit("scalar", lambda e, g=g, d=cc2_in[qb // 4].ap(), c2=cls2,
                       s=abf(O_XGA + 2 * (8192 * cls2 + 2048 * (qb // 4)), 2048):
                       e.dma_start(out=d[128 * c2:128 * (c2 + 1), :],
                                   in_=s).then_inc(SD.h, 16))
            b.wait("gpsimd", SD, SD.n)
            CCS.bump()
            b.emit("gpsimd", lambda e, g=g: e.collective_compute(
                "AllGather", ALU.bypass, replica_groups=groups,
                ins=[cc2s_in[g].ap().opt()], outs=[cc2s_out[g].ap().opt()]).then_inc(CCS.h, 1))
            ag2_v.append(CCS.bump())
            b.emit("gpsimd", lambda e, g=g: e.collective_compute(
                "AllGather", ALU.bypass, replica_groups=groups,
                ins=[cc2_in[g].ap().opt()], outs=[cc2_out[g].ap().opt()]).then_inc(CCS.h, 1))
            if g == 3:
                ag2_sen[0] = CCS.bump()
                b.emit("gpsimd", lambda e: e.collective_compute(
                    "AllGather", ALU.bypass, replica_groups=groups,
                    ins=[ccsen_in.ap().opt()], outs=[ccsen_out.ap().opt()]).then_inc(CCS.h, 1))

    psum_consumed(s0, 0, 1024, SC, spar_reader[0])
    psum_consumed(s1, 0, 1024, SC, spar_reader[1])

    if debug_stage == "attn":
        b.wait("gpsimd", CCS, ag2_v[-1])
        dump_dram(cc2_out[1].ap(), E, T, BF16)
        return _finish(nc, b, (LD, GD, SD, CCS, PE, SC, VE), in_maps)

    attn_end_pe = PE.n
    b.wait("gpsimd", PE, attn_end_pe)
    b.wait("gpsimd", VE, attn_norm_done)
    for (off, src) in ((O_WR, wrT), (O_WK, wkT), (O_WV, wvT), (O_WO, woT)):
        GD.bump(16)
        b.emit("gpsimd", lambda e, d=abf(off, NCH * 1024).rearrange("p (c j) -> p c j", c=NCH),
               s=src.rearrange("(c p) j -> p c j", p=128):
               e.dma_start(out=d, in_=s).then_inc(GD.h, 16))
    w2_ld = GD.n

    sw_state = {}

    def stage_switch(e, p):
        if "idx" not in sw_state:
            pid = e.partition_id()
            rreg = e.alloc_register("rksw")
            e.reg_mod(rreg, pid, 4)
            sw_state["idx"] = e.snap(rreg, min_val=0, max_val=3)
        idx = sw_state["idx"]
        for i in e.Switch(idx, 4):
            cc2o = cc2_out[i].ap()
            cc2s = cc2s_out[i].ap()
            cc2o_prev = cc2_out[max(0, i - 1)].ap()
            cc2s_prev = cc2s_out[max(0, i - 1)].ap()
            e.wait_ge(CCS.h, ag2_sen[0] if i == 3 else ag2_v[i + 1])
            if p == 0:
                pass
                for ec in range(NCH):
                    e.dma_start(out=abf(O_AST + 2 * GW * ec, GW)[:, 1:513],
                                in_=cc2o[128 * ec:128 * (ec + 1), 0:512]
                                ).then_inc(LD.h, 16)
                for ec in range(NCH):
                    hc = 2047 if i > 0 else 0
                    e.dma_start(out=abf(O_AST + 2 * GW * ec, GW)[:, 0:1],
                                in_=cc2o_prev[128 * ec:128 * (ec + 1), hc:hc + 1],
                                allow_slow_non_contiguous=True).then_inc(LD.h, 16)
                e.dma_start(out=sums_st[:, 1:513],
                            in_=cc2s[:, 0:512]).then_inc(LD.h, 16)
                hc = 2047 if i > 0 else 0
                e.dma_start(out=sums_st[:, 0:1], in_=cc2s_prev[:, hc:hc + 1],
                            allow_slow_non_contiguous=True).then_inc(LD.h, 16)
            else:
                c0 = 512 * p - 1
                for ec in range(NCH):
                    e.dma_start(out=abf(O_AST + 2 * GW * ec, GW)[:, 0:513],
                                in_=cc2o[128 * ec:128 * (ec + 1), c0:c0 + 513]
                                ).then_inc(LD.h, 16)
                e.dma_start(out=sums_st[:, 0:513],
                            in_=cc2s[:, c0:c0 + 513]).then_inc(LD.h, 16)
                e.dma_start(out=sums_st[:, 516:517],
                            in_=cc2s[:, 0:1],
                            allow_slow_non_contiguous=True).then_inc(LD.h, 16)

    wo_done_panel = 0
    x2stt_done_panel = 0
    norm_bc_used = [0]
    for p in range(NPAN):
        W2 = 513
        b.wait("sync", PE, max(attn_end_pe, wo_done_panel))
        for _ in range((16 if p == 0 else 8) + 2):
            LD.bump(16)
        b.emit("sync", lambda e, p=p: stage_switch(e, p))
        ast_ld = LD.n
        b.wait("sync", VE, x2stt_done_panel)
        for ec in range(NCH):
            LD.bump(16)
            b.emit("sync", lambda e, d=af32(O_X1B + 4 * GW * ec, GW)[:, 0:W2],
                   s=x1sp[128 * ec:128 * (ec + 1), 512 * p:512 * p + W2]:
                   e.dma_start(out=d, in_=s).then_inc(LD.h, 16))
        x1b_ld = LD.n

        b.wait("vector", LD, ast_ld)
        b.wait("vector", PE, norm_bc_used[0])
        ve_rcp = VE.bump()
        b.emit("vector", lambda e: e.reciprocal(sums_st[:, 0:513],
                                                sums_st[:, 0:513]).then_inc(VE.h, 1))
        norm_vals = []
        for ac in range(NCH):
            b.wait("tensor", VE, ve_rcp)
            b.waits("tensor", psum_w_deps(bcp, 0, 512))
            for (cs, cw) in ((0, 1), (1, 512)):
                if cs > 0:
                    b.wait("tensor", VE, norm_vals[-1])
                pe_bc = PE.bump()
                b.emit("tensor", lambda e, l=ind_s[:, 128 * ac:128 * (ac + 1)],
                       r=sums_st[:, cs:cs + cw], o=bcp[:, 0:cw]:
                       e.matmul(o, l, r, start=True, stop=True).then_inc(PE.h, 1))
                b.wait("vector", PE, pe_bc)
                ve_n = VE.bump()
                b.emit("vector", lambda e, o=abf(O_AST + 2 * GW * ac, GW)[:, cs:cs + cw],
                       m=bcp[:, 0:cw]:
                       e.tensor_tensor(o, o, m, ALU.mult).then_inc(VE.h, 1))
                norm_vals.append(ve_n)
                psum_consumed(bcp, 0, cw, VE, ve_n)
        norm_done = max(norm_vals)
        norm_bc_used[0] = PE.n

        x2_vals = []
        b.wait("tensor", GD, w2_ld)
        b.wait("tensor", LD, max(ast_ld, x1b_ld))
        b.wait("tensor", VE, norm_done)
        for dc in range(NCH):
            ln_t, ln_c = lane_defs[dc % 3]
            b.waits("tensor", psum_w_deps(ln_t, ln_c, W2))
            pe_v = PE.bump()
            wo_blocks = col_blocks(W2)
            for ac in range(NCH):
                lhs = abf(O_WO + 2 * (1024 * ac + 128 * dc), 128)
                rhs = abf(O_AST + 2 * GW * ac, GW)
                for bi, (cs, cw) in enumerate(wo_blocks):
                    last = (ac == NCH - 1 and bi == len(wo_blocks) - 1)
                    b.emit("tensor", lambda e, o=ln_t[:, ln_c + cs:ln_c + cs + cw], l=lhs,
                           r=rhs[:, cs:cs + cw], st=(ac == 0), sp=(ac == NCH - 1), lt=last:
                           e.matmul(o, l, r, start=st, stop=sp).then_inc(PE.h, 1) if lt
                           else e.matmul(o, l, r, start=st, stop=sp))
            b.wait("vector", PE, pe_v)
            ve_v = VE.bump()
            b.emit("vector", lambda e, o=af32(O_X2 + 4 * GW * dc, W2), i=ln_t[:, ln_c:ln_c + W2],
                   bb=bo_s[:, dc:dc + 1], x1=af32(O_X1B + 4 * GW * dc, W2):
                   e.scalar_tensor_tensor(o, i, bb, x1, ALU.add, ALU.add).then_inc(VE.h, 1))
            psum_consumed(ln_t, ln_c, W2, VE, ve_v)
            if p == 0:
                ve_v = VE.bump()
                b.emit("vector", lambda e, o=af32(O_X2 + 4 * GW * dc, 1):
                       e.tensor_scalar_mul(o, o, hm_s).then_inc(VE.h, 1))
            x2_vals.append(VE.n)
        wo_done_panel = PE.n
        x2stt_done_panel = max(x2_vals)

        ln_pass(lambda ec: af32(O_X2 + 4 * GW * ec, GW)[:, 0:W2], W2,
                lambda ec: abf(O_XN3 + 2 * GW * ec, W2),
                lambda ec, xv=list(x2_vals): [(VE, xv[ec])])

        mix_vals = []
        for ec in range(NCH):
            xn3 = abf(O_XN3 + 2 * GW * ec, GW)
            dt_ = abf(O_D2 + 2 * GW * ec, 512)
            b.emit("vector", lambda e, o=dt_, x=xn3:
                   e.tensor_tensor(o, x[:, 1:513], x[:, 0:512], ALU.subtract))
            b.emit("vector", lambda e, o=abf(O_XR2 + 2 * GW * ec, 512), d=dt_,
                   m=mur_s[:, ec:ec + 1], x=xn3:
                   e.scalar_tensor_tensor(o, d, m, x[:, 0:512], ALU.mult, ALU.add))
            vv = VE.bump()
            b.emit("vector", lambda e, o=abf(O_XK2 + 2 * GW * ec, 512), d=dt_,
                   m=muk_s[:, ec:ec + 1], x=xn3:
                   e.scalar_tensor_tensor(o, d, m, x[:, 0:512], ALU.mult, ALU.add).then_inc(VE.h, 1))
            mix_vals.append(vv)

        x3_vals = ffn_block(lambda ec: abf(O_XR2 + 2 * GW * ec, 512),
                            lambda ec: abf(O_XK2 + 2 * GW * ec, 512),
                            512,
                            lambda ec: af32(O_X2 + 4 * GW * ec, GW), 1,
                            lambda dc: af32(O_X2 + 4 * GW * dc, GW)[:, 1:513],
                            O_WR, O_WK, O_WV, O_R2, O_D2, max(mix_vals))
        x2stt_done_panel = max(x3_vals)

        for ec in range(NCH):
            b.wait("scalar", VE, x3_vals[ec])
            SD.bump(16)
            b.emit("scalar", lambda e, d=y[128 * ec:128 * (ec + 1), 512 * p:512 * p + 512],
                   s=af32(O_X2 + 4 * GW * ec, GW)[:, 1:513]:
                   e.dma_start(out=d, in_=s).then_inc(SD.h, 16))

    b.wait("gpsimd", SD, SD.n)
    b.wait("gpsimd", LD, LD.n)
    b.wait("gpsimd", GD, GD.n)
    return _finish(nc, b, (LD, GD, SD, CCS, PE, SC, VE), in_maps)


def _finish(nc, b, ctrs, in_maps):
    LD, GD, SD, CCS, PE, SC, VE = ctrs
    with ExitStack() as ctx:
        for c, name in zip(ctrs, ("LD", "GD", "SD", "CCS", "PE", "SC", "VE")):
            c.h = ctx.enter_context(nc.semaphore(name))
        with nc.Block() as block:
            if b.q["sync"]:
                @block.sync
                def _(e):
                    for f in b.q["sync"]:
                        f(e)
            if b.q["gpsimd"]:
                @block.gpsimd
                def _(e):
                    for f in b.q["gpsimd"]:
                        f(e)
            if b.q["vector"]:
                @block.vector
                def _(e):
                    for f in b.q["vector"]:
                        f(e)
            if b.q["scalar"]:
                @block.scalar
                def _(e):
                    for f in b.q["scalar"]:
                        f(e)
            if b.q["tensor"]:
                @block.tensor
                def _(e):
                    for f in b.q["tensor"]:
                        f(e)
        if in_maps is None:
            n = sum(len(v) for v in b.q.values())
            print(f"[dry-run] instructions: {n} (sync={len(b.q['sync'])} "
                  f"gp={len(b.q['gpsimd'])} ve={len(b.q['vector'])} "
                  f"sc={len(b.q['scalar'])} pe={len(b.q['tensor'])}) "
                  f"n_bass={nc.n_instructions()}")
            return None
        res = run_bass_kernel_spmd(nc, in_maps, core_ids=list(range(8)))
    return res.results


def host_inputs(inputs):
    x = np.ascontiguousarray(np.asarray(inputs["L0_attn_hidden_states"]), dtype=np.float32)
    Wq = np.asarray(inputs["L0_attn_Wq"])
    Wk = np.asarray(inputs["L0_attn_Wk"])
    Wv = np.asarray(inputs["L0_attn_Wv"])
    Wo = np.asarray(inputs["L0_attn_Wo"])
    bo = np.asarray(inputs["L0_attn_bo"])
    Wr = np.asarray(inputs["L0_ffn_W_r"])
    Wkf = np.asarray(inputs["L0_ffn_W_k"])
    Wvf = np.asarray(inputs["L0_ffn_W_v"])
    mur = np.asarray(inputs["L0_ffn_mu_r"])
    muk = np.asarray(inputs["L0_ffn_mu_k"])

    perm = np.concatenate(
        [np.arange(64 * h, 64 * h + 64) for j in range(4)
         for h in (2 * j, 2 * j + 1, 8 + 2 * j, 8 + 2 * j + 1)])
    lay = lambda v: np.ascontiguousarray(v.reshape(NCH, 128).T, dtype=np.float32)
    base = {
        "wrT": np.ascontiguousarray(Wr.T, dtype=np.float32),
        "wkT": np.ascontiguousarray(Wkf.T, dtype=np.float32),
        "wvT": np.ascontiguousarray(Wvf.T, dtype=np.float32),
        "woT": np.ascontiguousarray(Wo.T[perm, :], dtype=np.float32),
        "bo_l": lay(bo), "mur_l": lay(mur), "muk_l": lay(muk),
    }
    in_maps = []
    for c in range(8):
        bidx, r = c // 4, c % 4
        t0 = 2048 * r
        xs = np.zeros((T + 2, E), dtype=np.float32)
        lo = max(0, t0 - 2)
        xs[2 - (t0 - lo):] = x[bidx, lo: t0 + T]
        heads = [2 * r, 2 * r + 1, 8 + 2 * r, 8 + 2 * r + 1]
        hsl = np.concatenate([np.arange(64 * h, 64 * h + 64) for h in heads])
        m = dict(base)
        m["xT"] = np.ascontiguousarray(xs.T)
        m["wqaT"] = np.ascontiguousarray(Wq[hsl, :].T, dtype=np.float32)
        m["wkaT"] = np.ascontiguousarray(Wk[hsl, :].T, dtype=np.float32)
        m["wvaT"] = np.ascontiguousarray(Wv[hsl, :].T, dtype=np.float32)
        m["hm_in"] = np.full((128, 1), 0.0 if r == 0 else 1.0, dtype=np.float32)
        in_maps.append(m)
    return in_maps


def kernel(**inputs):
    in_maps = host_inputs(inputs)
    results = build_and_run("full", in_maps)
    out = np.empty((2, S, E), dtype=np.float32)
    for c in range(8):
        bidx, r = c // 4, c % 4
        out[bidx, 2048 * r: 2048 * r + T] = results[c]["y"].T
    return out
